# version 01
# baseline (speedup 1.0000x reference)
import math
import numpy as np
from contextlib import ExitStack

import concourse.bass as bass
import concourse.mybir as mybir
from concourse.bass_utils import run_bass_kernel_spmd

F32 = mybir.dt.float32
BF16 = mybir.dt.bfloat16
ALU = mybir.AluOpType
AF = mybir.ActivationFunctionType
AX = mybir.AxisListType

D = 4096
NCH = 32
S = 2048
TT = 256
NT = 8
H = 32
DH = 128
EPS = 1e-5
SCALE = 1.0 / math.sqrt(128.0)
NEG_BIG = -1e30
W_ = NCH * 128

CFG_NT = NT


class Bld:
    ENGS = ('sync', 'tensor', 'vector', 'scalar', 'gpsimd')

    def __init__(self, nc, es):
        self.nc = nc
        self.q = {e: [] for e in self.ENGS}
        self.h = {}
        self.cnt = {}
        for name in ('sW0', 'sW1', 'sP0', 'sP1', 'sB0', 'sB1', 'sRT',
                     'sC', 'sXT', 'sCS', 'sREP', 'sTS', 'sOUT',
                     'sMM', 'sV', 'sSC'):
            self.h[name] = es.enter_context(nc.semaphore(name))
            self.cnt[name] = 0
        self.slot_mm = {}

    def emit(self, eng, fn):
        self.q[eng].append(fn)

    def wait(self, eng, sname, val):
        assert val <= self.cnt[sname], (sname, val, self.cnt[sname])
        if val <= 0:
            return
        h = self.h[sname]
        self.q[eng].append(lambda e, h=h, v=val: e.wait_ge(h, v))

    def dma(self, eng, out, in_, sname, guard=None):
        if guard is not None:
            for g in ([guard] if isinstance(guard, str) else guard):
                gs, gv = self.slot_mm.get(g, ('sMM', 0))
                self.wait(eng, gs, gv)
        h = self.h[sname]
        self.cnt[sname] += 16
        self.q[eng].append(lambda e, o=out, i=in_, h=h: e.dma_start(out=o, in_=i).then_inc(h, 16))
        return self.cnt[sname]

    def mark(self, eng, sname):
        h = self.h[sname]
        self.cnt[sname] += 1
        self.q[eng].append(lambda e, h=h: e.drain().then_inc(h, 1))
        return self.cnt[sname]

    def inc(self, eng, sname):
        h = self.h[sname]
        self.cnt[sname] += 1
        self.q[eng].append(lambda e, h=h: e.drain().then_inc(h, 1))
        return self.cnt[sname]

    def emit_inc(self, eng, fn, sname):
        h = self.h[sname]
        self.cnt[sname] += 1

        def g(e, fn=fn, h=h):
            inst = fn(e)
            inst.then_inc(h, 1)
        self.q[eng].append(g)
        return self.cnt[sname]


def build_nc():
    nc = bass.Bass('TRN2', target_bir_lowering=False, debug=False)
    n_t = CFG_NT

    def din(name, shape):
        return nc.dram_tensor(name, list(shape), F32, kind='ExternalInput').ap()

    x0p = din('x0p', (NCH, 128, S))
    cosp = din('cosp', (128, S))
    sinp = din('sinp', (128, S))
    rampi = din('rampi', (1, TT))
    mlasti = din('mlasti', (128, 1))
    diagi = din('diagi', (128, 128))
    identi = din('identi', (128, 128))
    rotmi = din('rotmi', (128, 128))
    rep16i = din('rep16i', (8, 128))
    wq0 = din('wq0', (NCH, 128, W_))
    wo0 = din('wo0', (NCH, 128, W_))
    wb0 = din('wb0', (NCH, 128, W_))
    wq1 = din('wq1', (NCH, 128, W_))
    wo1 = din('wo1', (NCH, 128, W_))
    wf1 = din('wf1', (NCH, 128, W_))
    wk0 = din('wk0', (128, W_))
    wv0 = din('wv0', (128, W_))
    wk1 = din('wk1', (128, W_))
    wv1 = din('wv1', (128, W_))
    wrt = din('wrt', (128, NCH * 8))
    atw = din('atw', (128, W_))
    btw = din('btw', (NCH, 128, 128))
    bb0 = din('bb0', (128, NCH))
    qb1 = din('qb1', (128, NCH))
    ob1 = din('ob1', (128, NCH))
    fb1 = din('fb1', (128, NCH))
    fs1 = din('fs1', (128, NCH))
    fg1 = din('fg1', (128, NCH))
    fbt1 = din('fbt1', (128, NCH))
    kb1 = din('kb1', (128, 1))
    vb1 = din('vb1', (128, 1))
    qs1 = din('qs1', (1, 1))
    y_out = nc.dram_tensor('y_out', [NT, 128, NCH, 128], F32, kind='ExternalOutput').ap()
    tern_sc = nc.dram_tensor('tern_sc', [NCH, 128, W_], BF16, kind='Internal').ap()

    es = ExitStack()
    with es:
        SB = lambda n, shp, dt=F32: es.enter_context(nc.sbuf_tensor(n, list(shp), dt))

        xt = SB('xt', (128, NCH * TT))
        a1 = SB('a1', (128, NCH * TT))
        a2 = SB('a2', (128, NCH * TT))
        wba = SB('wba', (128, W_))
        wbb = SB('wbb', (128, W_))
        k0r = SB('k0r', (128, S))
        v0t = SB('v0t', (128, S))
        k1b = SB('k1b', (128, S))
        v1t = SB('v1t', (128, S))
        costl = SB('costl', (128, TT))
        sintl = SB('sintl', (128, TT))
        attns = SB('attns', (128, 512))
        crep = SB('crep', (128, 512))
        tmpb = SB('tmpb', (128, 512))
        btb = SB('btb', (128, 256))
        gbuf = SB('gbuf', (128, TT))
        wrep = SB('wrep', (128, TT))
        rows = SB('rows', (1, 2048))
        ternb = SB('ternb', (128, W_), BF16)
        lnpair = SB('lnpair', (128, 2 * W_), BF16)
        biases = SB('biases', (128, 8 * NCH))
        smallc = SB('smallc', (128, 8))
        diag = SB('diag', (128, 128))
        ident = SB('ident', (128, 128))
        rotm = SB('rotm', (128, 128))
        rep16 = SB('rep16', (8, 128))
        onec = SB('onec', (128, 1))
        oner = SB('oner', (1, 128))
        ltt = SB('ltt', (128, 16))
        rsc = SB('rsc', (128, 64))

        wrtb = btb
        wts = tmpb[0:8, 256:512]

        pb = [es.enter_context(nc.psum_tensor('pb%d' % i, [128, 512], F32)) for i in range(8)]

        b = Bld(nc, es)

        def row(off, w):
            return rows[0:1, off:off + w]

        R_AL, R_AL1, R_M, R_V, R_TMP = 0, 256, 512, 1024, 1536
        R_R1, R_R2, R_C1, R_C2 = 512, 768, 1024, 1280

        BIAS = {'bb0': 0, 'qb1': NCH, 'ob1': 2 * NCH, 'fb1': 3 * NCH,
                'fs1': 4 * NCH, 'fg1': 5 * NCH, 'fbt1': 6 * NCH}

        def bcol(name, o):
            return biases[:, BIAS[name] + o: BIAS[name] + o + 1]

        KCOL = smallc[:, 0:1]
        VCOL = smallc[:, 1:2]
        QSCOL = smallc[:, 2:3]
        MLCOL = smallc[:, 3:4]

        for dst, src in ((diag[:, :], diagi), (ident[:, :], identi), (rotm[:, :], rotmi),
                         (rep16[:, :], rep16i),
                         (biases[:, 0:NCH], bb0), (biases[:, NCH:2 * NCH], qb1),
                         (biases[:, 2 * NCH:3 * NCH], ob1), (biases[:, 3 * NCH:4 * NCH], fb1),
                         (biases[:, 4 * NCH:5 * NCH], fs1), (biases[:, 5 * NCH:6 * NCH], fg1),
                         (biases[:, 6 * NCH:7 * NCH], fbt1),
                         (smallc[:, 0:1], kb1), (smallc[:, 1:2], vb1), (smallc[:, 3:4], mlasti),
                         (rows[0:1, 1792:1792 + TT], rampi), (rows[0:1, R_M:R_M + 1], qs1)):
            b.dma('gpsimd', dst, src, 'sC')
        c_load = b.cnt['sC']

        b.wait('vector', 'sC', c_load)

        def _init_v(e):
            e.memset(onec[:, :], 1.0)
            e.memset(oner[:, :], 1.0)
            e.drain()
            e.tensor_scalar(row(R_AL, TT), row(1792, TT), 1.0 / 255.0, None, ALU.mult)
            e.tensor_scalar(row(R_AL1, TT), row(1792, TT), -1.0 / 255.0, 1.0, ALU.mult, ALU.add)
            e.drain()
            e.reciprocal(row(R_M + 1, 1), row(R_M, 1))
            e.drain()
        b.emit('vector', _init_v)
        cv = b.mark('vector', 'sV')
        b.wait('tensor', 'sV', cv)
        cm = b.emit_inc('tensor', lambda e: e.matmul(pb[0][0:128, 0:1], oner[:, :],
                                                     row(R_M + 1, 1), start=True, stop=True), 'sMM')
        b.wait('scalar', 'sMM', cm)
        b.emit('scalar', lambda e: e.copy(smallc[:, 2:3], pb[0][0:128, 0:1]))
        cs = b.mark('scalar', 'sSC')
        b.wait('tensor', 'sSC', cs)

        def rows_from_stats(ntok):
            b.wait('vector', 'sMM', b.cnt['sMM'])

            def _rows(e):
                e.tensor_scalar(row(R_M, ntok), pb[6][0:1, 0:ntok], 1.0 / 4096.0, None, ALU.mult)
                e.tensor_scalar(row(R_V, ntok), pb[7][0:1, 0:ntok], 1.0 / 4096.0, None, ALU.mult)
                e.drain()
                e.tensor_tensor(row(R_TMP, ntok), row(R_M, ntok), row(R_M, ntok), ALU.mult)
                e.drain()
                e.tensor_tensor(row(R_V, ntok), row(R_V, ntok), row(R_TMP, ntok), ALU.subtract)
                e.drain()
                e.tensor_scalar(row(R_V, ntok), row(R_V, ntok), EPS, None, ALU.add)
                e.drain()
            b.emit('vector', _rows)
            cv = b.mark('vector', 'sV')
            b.wait('scalar', 'sV', cv)
            b.emit('scalar', lambda e: e.activation(row(R_TMP, ntok), row(R_V, ntok),
                                                    AF.Sqrt, bias=0.0, scale=1.0))
            cs = b.mark('scalar', 'sSC')
            b.wait('vector', 'sSC', cs)

            def _rr(e):
                e.reciprocal(row(R_V, ntok), row(R_TMP, ntok))
                e.drain()
                e.tensor_tensor(row(R_TMP, ntok), row(R_M, ntok), row(R_V, ntok), ALU.mult)
                e.drain()
            b.emit('vector', _rr)
            return b.mark('vector', 'sV')

        def bcast_rows(ntok, dep_v):
            b.wait('tensor', 'sV', dep_v)

            def _bc(e):
                e.matmul(pb[0][0:128, 0:ntok], oner[:, :], row(R_V, ntok), start=True, stop=True)
                return e.matmul(pb[1][0:128, 0:ntok], oner[:, :], row(R_TMP, ntok), start=True, stop=True)
            return b.emit_inc('tensor', _bc, 'sMM')

        def in_stats(src_sb, ntok, chunk_w, sq_buf):
            b.wait('scalar', 'sV', b.cnt['sV'])
            if ntok == chunk_w:
                b.emit('scalar', lambda e: e.activation(sq_buf[:, 0:NCH * ntok],
                                                        src_sb[:, 0:NCH * ntok], AF.Square,
                                                        bias=0.0, scale=1.0))
            else:
                def _sq(e):
                    for c in range(NCH):
                        e.activation(sq_buf[:, c * ntok:(c + 1) * ntok],
                                     src_sb[:, c * chunk_w:c * chunk_w + ntok],
                                     AF.Square, bias=0.0, scale=1.0)
                b.emit('scalar', _sq)
            csq = b.mark('scalar', 'sSC')
            b.wait('tensor', 'sSC', csq)

            def _sums(e):
                for c in range(NCH):
                    e.matmul(pb[6][0:1, 0:ntok], onec[:, :],
                             src_sb[:, c * chunk_w:c * chunk_w + ntok],
                             start=(c == 0), stop=(c == NCH - 1))
                    last = e.matmul(pb[7][0:1, 0:ntok], onec[:, :],
                                    sq_buf[:, c * ntok:(c + 1) * ntok],
                                    start=(c == 0), stop=(c == NCH - 1))
                return last
            b.emit_inc('tensor', _sums, 'sMM')

        def ln_into(dst, src_sb, ntok, chunk_w, cmm_bcast):
            b.wait('vector', 'sMM', cmm_bcast)

            def _ln(e):
                for c in range(NCH):
                    s = src_sb[:, c * chunk_w:c * chunk_w + ntok]
                    d = dst[:, c * ntok:(c + 1) * ntok]
                    e.tensor_tensor(d, s, pb[0][0:128, 0:ntok], ALU.mult)
                    e.drain()
                    e.tensor_tensor(d, d, pb[1][0:128, 0:ntok], ALU.subtract)
                    e.drain()
            b.emit('vector', _ln)
            return b.mark('vector', 'sV')

        def residual_update(y_sb, ntok, chunk_w, cmm_bcast):
            b.wait('vector', 'sMM', cmm_bcast)

            def _res(e):
                for c in range(NCH):
                    yc = y_sb[:, c * ntok:(c + 1) * ntok]
                    t = tmpb[:, 0:ntok]
                    e.tensor_tensor(t, yc, pb[0][0:128, 0:ntok], ALU.mult)
                    e.drain()
                    e.tensor_tensor(t, t, pb[1][0:128, 0:ntok], ALU.subtract)
                    e.drain()
                    dc = xt[:, c * chunk_w:c * chunk_w + ntok]
                    e.tensor_tensor(dc, dc, t, ALU.add)
                    e.drain()
            b.emit('vector', _res)
            return b.mark('vector', 'sV')

        def proj(w_dram, rhs_sb, ntok, chunk_w, out_cb, dep_v, bias_name=None,
                 lora_dep=None, stats=False):
            for o in range(NCH):
                par = o % 2
                buf = wba if par == 0 else wbb
                bank = pb[par]
                cw = b.dma('sync', buf[:, :], w_dram[o], 'sW%d' % par, guard='wb%d' % par)
                b.wait('tensor', 'sW%d' % par, cw)
                if lora_dep is not None:
                    cbt = b.dma('sync', btb[:, par * 128:par * 128 + 128], btw[o], 'sB%d' % par,
                                guard='bt%d' % par)
                    b.wait('tensor', 'sB%d' % par, cbt)
                if o == 0 and dep_v:
                    b.wait('tensor', 'sV', dep_v)
                if o == 0 and lora_dep:
                    b.wait('tensor', 'sV', lora_dep)

                def _grp(e, buf=buf, bank=bank, par=par):
                    for c in range(NCH):
                        last = e.matmul(bank[0:128, 0:ntok], buf[:, c * 128:(c + 1) * 128],
                                        rhs_sb[:, c * chunk_w:c * chunk_w + ntok],
                                        start=(c == 0), stop=(c == NCH - 1 and lora_dep is None))
                    if lora_dep is not None:
                        last = e.matmul(bank[0:128, 0:ntok], btb[:, par * 128:par * 128 + 128],
                                        gbuf[:, 0:ntok], start=False, stop=True)
                    return last
                cmm = b.emit_inc('tensor', _grp, 'sMM')
                b.slot_mm['wb%d' % par] = ('sMM', cmm)
                b.slot_mm['bt%d' % par] = ('sMM', cmm)
                b.wait('scalar', 'sMM', cmm)
                dst = out_cb(o)
                if bias_name is None:
                    b.emit('scalar', lambda e, dst=dst, bank=bank: e.copy(dst, bank[0:128, 0:ntok]))
                else:
                    bc = bcol(bias_name, o)
                    b.emit('scalar', lambda e, dst=dst, bank=bank, bc=bc:
                           e.activation(dst, bank[0:128, 0:ntok], AF.Identity, bias=bc, scale=1.0))
                if stats:
                    b.emit('scalar', lambda e, bank=bank:
                           e.activation(tmpb[:, 256:256 + ntok], bank[0:128, 0:ntok],
                                        AF.Square, bias=0.0, scale=1.0))
                csc = b.mark('scalar', 'sSC')
                b.wait('tensor', 'sSC', csc)
                if stats:
                    def _stat(e, dst=dst, o=o):
                        e.matmul(pb[6][0:1, 0:ntok], onec[:, :], dst,
                                 start=(o == 0), stop=(o == NCH - 1))
                        return e.matmul(pb[7][0:1, 0:ntok], onec[:, :], tmpb[:, 256:256 + ntok],
                                        start=(o == 0), stop=(o == NCH - 1))
                    b.emit_inc('tensor', _stat, 'sMM')

        cv_rope_prev = 0
        cwk0 = b.dma('sync', wba[:, :], wk0, 'sW0', guard='wb0')
        cwv0 = b.dma('sync', wbb[:, :], wv0, 'sW1', guard='wb1')
        b.wait('tensor', 'sW0', cwk0)
        b.wait('tensor', 'sW1', cwv0)
        for g in range(4):
            g0 = g * 512
            for c in range(NCH):
                slot = c % 2
                xc = a1[:, slot * 512: slot * 512 + 512]
                sq = a1[:, 1024 + slot * 512: 1024 + slot * 512 + 512]
                cx = b.dma('sync', xc, x0p[c, :, g0:g0 + 512], 'sP%d' % slot, guard='p0x%d' % slot)
                b.wait('scalar', 'sP%d' % slot, cx)
                b.emit('scalar', lambda e, xc=xc, sq=sq: e.activation(sq, xc, AF.Square,
                                                                     bias=0.0, scale=1.0))
                csq = b.mark('scalar', 'sSC')
                b.wait('tensor', 'sSC', csq)

                def _st(e, xc=xc, sq=sq, c=c):
                    e.matmul(pb[6][0:1, 0:512], onec[:, :], xc, start=(c == 0), stop=(c == NCH - 1))
                    return e.matmul(pb[7][0:1, 0:512], onec[:, :], sq,
                                    start=(c == 0), stop=(c == NCH - 1))
                b.slot_mm['p0x%d' % slot] = ('sMM', b.emit_inc('tensor', _st, 'sMM'))
            cvr = rows_from_stats(512)
            cbc = bcast_rows(512, cvr)
            for c in range(NCH):
                slot = c % 2
                xc = a1[:, slot * 512: slot * 512 + 512]
                lnc = a1[:, 1024 + slot * 512: 1024 + slot * 512 + 512]
                cx = b.dma('sync', xc, x0p[c, :, g0:g0 + 512], 'sP%d' % slot, guard='p0x%d' % slot)
                b.wait('vector', 'sP%d' % slot, cx)
                if c == 0:
                    b.wait('vector', 'sMM', cbc)

                def _ln(e, xc=xc, lnc=lnc):
                    e.tensor_tensor(lnc, xc, pb[0][0:128, 0:512], ALU.mult)
                    e.drain()
                    e.tensor_tensor(lnc, lnc, pb[1][0:128, 0:512], ALU.subtract)
                    e.drain()
                b.emit('vector', _ln)
                cvl = b.mark('vector', 'sV')
                b.wait('tensor', 'sV', cvl)

                def _kv(e, lnc=lnc, c=c):
                    e.matmul(pb[2][0:128, 0:512], wba[:, c * 128:(c + 1) * 128], lnc,
                             start=(c == 0), stop=(c == NCH - 1))
                    return e.matmul(pb[3][0:128, 0:512], wbb[:, c * 128:(c + 1) * 128], lnc,
                                    start=(c == 0), stop=(c == NCH - 1))
                b.slot_mm['p0x%d' % slot] = ('sMM', b.emit_inc('tensor', _kv, 'sMM'))
            cm_kv = b.cnt['sMM']
            b.wait('gpsimd', 'sV', cv_rope_prev)
            ccos = b.dma('gpsimd', a1[:, 2048:2560], cosp[:, g0:g0 + 512], 'sCS')
            csin = b.dma('gpsimd', a1[:, 2560:3072], sinp[:, g0:g0 + 512], 'sCS')
            b.wait('scalar', 'sMM', cm_kv)
            b.emit('scalar', lambda e: e.copy(a2[:, 0:512], pb[2][0:128, 0:512]))
            b.emit('scalar', lambda e: e.copy(a2[:, 512:1024], pb[3][0:128, 0:512]))
            cs_kv = b.mark('scalar', 'sSC')
            b.wait('tensor', 'sSC', cs_kv)
            cm_rot = b.emit_inc('tensor', lambda e: e.matmul(pb[2][0:128, 0:512], rotm[:, :],
                                                             a2[:, 0:512], start=True, stop=True), 'sMM')
            b.wait('vector', 'sMM', cm_rot)
            b.wait('vector', 'sCS', max(ccos, csin))

            def _rope(e, g0=g0):
                e.tensor_tensor(a1[:, 3072:3584], a2[:, 0:512], a1[:, 2048:2560], ALU.mult)
                e.drain()
                e.tensor_tensor(a1[:, 3584:4096], pb[2][0:128, 0:512], a1[:, 2560:3072], ALU.mult)
                e.drain()
                e.tensor_tensor(k0r[:, g0:g0 + 512], a1[:, 3072:3584], a1[:, 3584:4096], ALU.add)
                e.drain()
            b.emit('vector', _rope)
            cv_rope = b.mark('vector', 'sV')
            cv_rope_prev = cv_rope
            b.wait('tensor', 'sV', cv_rope)
            for bb_ in range(4):
                cmt = b.emit_inc('tensor', lambda e, bb_=bb_: e.transpose(
                    pb[4][0:128, 0:128], a2[:, 512 + bb_ * 128:512 + bb_ * 128 + 128],
                    ident[:, :]), 'sMM')
                b.wait('scalar', 'sMM', cmt)
                dst = v0t[:, (g * 4 + bb_) * 128:(g * 4 + bb_) * 128 + 128]
                b.emit('scalar', lambda e, dst=dst: e.copy(dst, pb[4][0:128, 0:128]))
                cst = b.mark('scalar', 'sSC')
                b.wait('tensor', 'sSC', cst)
        b.slot_mm['wb0'] = ('sMM', b.cnt['sMM'])
        b.slot_mm['wb1'] = ('sMM', b.cnt['sMM'])

        cv_stage_prev = cv_rope_prev
        cx_out_prev = 0
        for t in range(n_t):
            t0 = t * TT

            b.wait('gpsimd', 'sV', cv_stage_prev)
            cxt = b.dma('gpsimd', xt[:].rearrange('p (c w) -> p c w', c=NCH),
                        x0p[:, :, t0:t0 + TT].rearrange('c p w -> p c w'), 'sXT')
            ccos = b.dma('gpsimd', costl[:, :], cosp[:, t0:t0 + TT], 'sCS')
            csin = b.dma('gpsimd', sintl[:, :], sinp[:, t0:t0 + TT], 'sCS')

            b.wait('scalar', 'sXT', cxt)
            in_stats(xt, TT, TT, a1)
            cvr = rows_from_stats(TT)
            cbc = bcast_rows(TT, cvr)
            cln = ln_into(a1, xt, TT, TT, cbc)

            for o in range(H):
                par = o % 2
                buf = wba if par == 0 else wbb
                bank = pb[par]
                cw = b.dma('sync', buf[:, :], wq0[o], 'sW%d' % par, guard='wb%d' % par)
                b.wait('tensor', 'sW%d' % par, cw)
                if o == 0:
                    b.wait('tensor', 'sV', cln)

                def _grp(e, buf=buf, bank=bank):
                    for c in range(NCH):
                        last = e.matmul(bank[0:128, 0:TT], buf[:, c * 128:(c + 1) * 128],
                                        a1[:, c * TT:(c + 1) * TT],
                                        start=(c == 0), stop=(c == NCH - 1))
                    return last
                cmm = b.emit_inc('tensor', _grp, 'sMM')
                b.slot_mm['wb%d' % par] = ('sMM', cmm)
                b.wait('scalar', 'sMM', cmm)
                b.emit('scalar', lambda e, bank=bank: e.copy(tmpb[:, 0:TT], bank[0:128, 0:TT]))
                cs = b.mark('scalar', 'sSC')
                b.wait('tensor', 'sSC', cs)
                cmr = b.emit_inc('tensor', lambda e: e.matmul(pb[2][0:128, 0:TT], rotm[:, :],
                                                              tmpb[:, 0:TT], start=True, stop=True), 'sMM')
                b.wait('vector', 'sMM', cmr)
                if o == 0:
                    b.wait('vector', 'sCS', max(ccos, csin))

                def _qrope(e, o=o):
                    d = a2[:, o * TT:(o + 1) * TT]
                    e.tensor_tensor(d, tmpb[:, 0:TT], costl[:, :], ALU.mult)
                    e.drain()
                    e.tensor_tensor(tmpb[:, 256:512], pb[2][0:128, 0:TT], sintl[:, :], ALU.mult)
                    e.drain()
                    e.tensor_tensor(d, d, tmpb[:, 256:512], ALU.add)
                    e.drain()
                b.emit('vector', _qrope)
                cvq = b.mark('vector', 'sV')
                b.wait('tensor', 'sV', cvq)

            n_w = 1 if t == 0 else 2
            for h in range(H):
                for wi in range(n_w):
                    wbase = (t - (n_w - 1 - wi)) * 256
                    nk = min(512, S - wbase) // 128
                    avb = pb[4] if (n_w == 2 and wi == 0) else pb[5]
                    dnb = pb[6] if (n_w == 2 and wi == 0) else pb[7]
                    for kc in range(nk):
                        kb = wbase + kc * 128
                        sb = pb[2 + (kc % 2)]
                        cms = b.emit_inc('tensor', lambda e, sb=sb, kb=kb, h=h:
                                         e.matmul(sb[0:128, 0:TT], k0r[:, kb:kb + 128],
                                                  a2[:, h * TT:(h + 1) * TT],
                                                  start=True, stop=True), 'sMM')
                        b.wait('scalar', 'sMM', cms)
                        asl = attns[:, (kc % 2) * TT:(kc % 2) * TT + TT]
                        b.emit('scalar', lambda e, sb=sb, asl=asl:
                               e.activation(asl, sb[0:128, 0:TT], AF.Sigmoid, bias=0.0, scale=SCALE))
                        css = b.mark('scalar', 'sSC')
                        b.wait('tensor', 'sSC', css)

                        def _av(e, asl=asl, kb=kb, kc=kc, nk=nk, avb=avb, dnb=dnb):
                            e.matmul(avb[0:128, 0:TT], v0t[:, kb:kb + 128], asl,
                                     start=(kc == 0), stop=(kc == nk - 1))
                            return e.matmul(dnb[0:1, 0:TT], onec[:, :], asl,
                                            start=(kc == 0), stop=(kc == nk - 1))
                        b.emit_inc('tensor', _av, 'sMM')
                b.wait('vector', 'sMM', b.cnt['sMM'])

                def _cc(e, n_w=n_w):
                    if n_w == 2:
                        e.reciprocal(row(R_R1, TT), pb[6][0:1, 0:TT])
                        e.reciprocal(row(R_R2, TT), pb[7][0:1, 0:TT])
                        e.drain()
                        e.tensor_tensor(row(R_C1, TT), row(R_R1, TT), row(R_AL1, TT), ALU.mult)
                        e.tensor_tensor(row(R_C2, TT), row(R_R2, TT), row(R_AL, TT), ALU.mult)
                        e.drain()
                    else:
                        e.reciprocal(row(R_C2, TT), pb[7][0:1, 0:TT])
                        e.drain()
                b.emit('vector', _cc)
                cv_cc = b.mark('vector', 'sV')
                b.wait('tensor', 'sV', cv_cc)

                def _bc(e, n_w=n_w):
                    if n_w == 2:
                        e.matmul(pb[2][0:128, 0:TT], oner[:, :], row(R_C1, TT), start=True, stop=True)
                    return e.matmul(pb[3][0:128, 0:TT], oner[:, :], row(R_C2, TT),
                                    start=True, stop=True)
                cm_bc = b.emit_inc('tensor', _bc, 'sMM')
                b.wait('scalar', 'sMM', cm_bc)

                def _crep(e, n_w=n_w):
                    if n_w == 2:
                        e.copy(crep[:, 0:TT], pb[2][0:128, 0:TT])
                    e.copy(crep[:, TT:2 * TT], pb[3][0:128, 0:TT])
                b.emit('scalar', _crep)
                cs_cr = b.mark('scalar', 'sSC')
                b.wait('vector', 'sSC', cs_cr)

                def _blend(e, h=h, n_w=n_w):
                    d = a1[:, h * TT:(h + 1) * TT]
                    if n_w == 2:
                        e.tensor_tensor(tmpb[:, 0:TT], pb[4][0:128, 0:TT], crep[:, 0:TT], ALU.mult)
                        e.drain()
                        e.tensor_tensor(d, pb[5][0:128, 0:TT], crep[:, TT:2 * TT], ALU.mult)
                        e.drain()
                        e.tensor_tensor(d, d, tmpb[:, 0:TT], ALU.add)
                        e.drain()
                    else:
                        e.tensor_tensor(d, pb[5][0:128, 0:TT], crep[:, TT:2 * TT], ALU.mult)
                        e.drain()
                b.emit('vector', _blend)
                cv_bl = b.mark('vector', 'sV')
                b.wait('tensor', 'sV', cv_bl)

            proj(wo0, a1, TT, TT, lambda o: a2[:, o * TT:(o + 1) * TT],
                 dep_v=b.cnt['sV'], stats=True)
            cvr = rows_from_stats(TT)
            cbc2 = bcast_rows(TT, cvr)
            residual_update(a2, TT, TT, cbc2)

            in_stats(xt, TT, TT, a1)
            cvr = rows_from_stats(TT)
            cbc = bcast_rows(TT, cvr)
            cln = ln_into(a1, xt, TT, TT, cbc)

            cat = b.dma('sync', wba[:, :], atw, 'sW0', guard='wb0')
            crt = b.dma('sync', wrtb[:, :], wrt, 'sRT', guard=['bt0', 'bt1'])
            b.wait('tensor', 'sW0', cat)
            b.wait('tensor', 'sRT', crt)
            b.wait('tensor', 'sV', cln)

            def _hrt(e):
                for c in range(NCH):
                    e.matmul(pb[2][0:128, 0:TT], wba[:, c * 128:(c + 1) * 128],
                             a1[:, c * TT:(c + 1) * TT], start=(c == 0), stop=(c == NCH - 1))
                    last = e.matmul(pb[3][0:8, 0:TT], wrtb[:, c * 8:(c + 1) * 8],
                                    a1[:, c * TT:(c + 1) * TT], start=(c == 0), stop=(c == NCH - 1))
                return last
            cm_h = b.emit_inc('tensor', _hrt, 'sMM')
            b.slot_mm['wb0'] = ('sMM', cm_h)
            b.slot_mm['bt0'] = ('sMM', cm_h)
            b.slot_mm['bt1'] = ('sMM', cm_h)
            b.wait('scalar', 'sMM', cm_h)
            b.emit('scalar', lambda e: e.copy(tmpb[0:8, 0:TT], pb[3][0:8, 0:TT]))
            cs_l = b.mark('scalar', 'sSC')
            b.wait('tensor', 'sSC', cs_l)
            for half in range(2):
                cm_t = b.emit_inc('tensor', lambda e, half=half: e.transpose(
                    pb[3][0:128, 0:8], tmpb[0:8, half * 128:half * 128 + 128],
                    ident[0:8, 0:8]), 'sMM')
                b.wait('scalar', 'sMM', cm_t)
                b.emit('scalar', lambda e, half=half: e.copy(ltt[:, half * 8:half * 8 + 8],
                                                             pb[3][0:128, 0:8]))
                cs_t = b.mark('scalar', 'sSC')
                b.wait('tensor', 'sSC', cs_t)
            b.wait('vector', 'sSC', b.cnt['sSC'])

            def _gate_v1(e):
                for half in range(2):
                    lt = ltt[:, half * 8:half * 8 + 8]
                    m1 = rsc[:, half * 2 + 0:half * 2 + 1]
                    e.reduce_max(m1, lt, AX.X)
                    e.drain()
                    msk1 = rsc[:, 16 + half * 8:16 + half * 8 + 8]
                    e.tensor_scalar(msk1, lt, m1, None, ALU.is_ge)
                    e.drain()
                    l2 = rsc[:, 32 + half * 8:32 + half * 8 + 8]
                    e.scalar_tensor_tensor(l2, msk1, NEG_BIG, lt, ALU.mult, ALU.add)
                    e.drain()
                    m2 = rsc[:, half * 2 + 1:half * 2 + 2]
                    e.reduce_max(m2, l2, AX.X)
                    e.drain()
                    msk2 = rsc[:, 48 + half * 8:48 + half * 8 + 8]
                    e.tensor_scalar(msk2, l2, m2, None, ALU.is_ge)
                    e.drain()
                    dcol = rsc[:, 4 + half * 2:4 + half * 2 + 1]
                    e.tensor_tensor(dcol, m1, m2, ALU.subtract)
                    e.drain()
            b.emit('vector', _gate_v1)
            cv_g1 = b.mark('vector', 'sV')
            b.wait('scalar', 'sV', cv_g1)

            def _gate_s(e):
                for half in range(2):
                    dcol = rsc[:, 4 + half * 2:4 + half * 2 + 1]
                    e.activation(rsc[:, 8 + half * 2:8 + half * 2 + 1], dcol, AF.Sigmoid,
                                 bias=0.0, scale=1.0)
                    e.activation(rsc[:, 9 + half * 2:9 + half * 2 + 1], dcol, AF.Sigmoid,
                                 bias=0.0, scale=-1.0)
            b.emit('scalar', _gate_s)
            cs_g = b.mark('scalar', 'sSC')
            b.wait('vector', 'sSC', cs_g)

            def _gate_v2(e):
                for half in range(2):
                    w1 = rsc[:, 8 + half * 2:8 + half * 2 + 1]
                    w2 = rsc[:, 9 + half * 2:9 + half * 2 + 1]
                    msk1 = rsc[:, 16 + half * 8:16 + half * 8 + 8]
                    msk2 = rsc[:, 48 + half * 8:48 + half * 8 + 8]
                    t1 = rsc[:, 32 + half * 8:32 + half * 8 + 8]
                    e.tensor_scalar(t1, msk1, w1, None, ALU.mult)
                    e.tensor_scalar(msk2, msk2, w2, None, ALU.mult)
                    e.drain()
                    e.tensor_tensor(msk1, t1, msk2, ALU.add)
                    e.drain()
            b.emit('vector', _gate_v2)
            cv_g2 = b.mark('vector', 'sV')
            b.wait('tensor', 'sV', cv_g2)
            for half in range(2):
                cm_t = b.emit_inc('tensor', lambda e, half=half: e.transpose(
                    pb[3][0:8, 0:128], rsc[:, 16 + half * 8:16 + half * 8 + 8],
                    ident[:, :]), 'sMM')
                b.wait('scalar', 'sMM', cm_t)
                b.emit('scalar', lambda e, half=half: e.copy(wts[:, half * 128:half * 128 + 128],
                                                             pb[3][0:8, 0:128]))
                cs_t = b.mark('scalar', 'sSC')
                b.wait('tensor', 'sSC', cs_t)
            b.wait('tensor', 'sSC', b.cnt['sSC'])
            cm_rep = b.emit_inc('tensor', lambda e: e.matmul(pb[3][0:128, 0:TT], rep16[:, :],
                                                             wts[:, :], start=True, stop=True), 'sMM')
            b.wait('scalar', 'sMM', cm_rep)
            b.emit('scalar', lambda e: e.copy(wrep[:, :], pb[3][0:128, 0:TT]))
            cs_rep = b.mark('scalar', 'sSC')
            b.wait('vector', 'sSC', cs_rep)

            def _g(e):
                e.tensor_tensor(gbuf[:, :], pb[2][0:128, 0:TT], wrep[:, :], ALU.mult)
                e.drain()
            b.emit('vector', _g)
            cv_gg = b.mark('vector', 'sV')

            proj(wb0, a1, TT, TT, lambda o: a2[:, o * TT:(o + 1) * TT],
                 dep_v=cv_gg, bias_name='bb0', lora_dep=cv_gg, stats=True)
            cvr = rows_from_stats(TT)
            cbc2 = bcast_rows(TT, cvr)
            residual_update(a2, TT, TT, cbc2)

            in_stats(xt, TT, TT, a1)
            cvr = rows_from_stats(TT)
            cbc = bcast_rows(TT, cvr)
            cln = ln_into(a1, xt, TT, TT, cbc)

            cwk1 = b.dma('sync', wba[:, :], wk1, 'sW0', guard='wb0')
            cwv1 = b.dma('sync', wbb[:, :], wv1, 'sW1', guard='wb1')
            b.wait('tensor', 'sW0', cwk1)
            b.wait('tensor', 'sW1', cwv1)
            b.wait('tensor', 'sV', cln)

            def _kv1(e):
                for c in range(NCH):
                    e.matmul(pb[2][0:128, 0:TT], wba[:, c * 128:(c + 1) * 128],
                             a1[:, c * TT:(c + 1) * TT], start=(c == 0), stop=(c == NCH - 1))
                    last = e.matmul(pb[3][0:128, 0:TT], wbb[:, c * 128:(c + 1) * 128],
                                    a1[:, c * TT:(c + 1) * TT], start=(c == 0), stop=(c == NCH - 1))
                return last
            cm_kv1 = b.emit_inc('tensor', _kv1, 'sMM')
            b.slot_mm['wb0'] = ('sMM', cm_kv1)
            b.slot_mm['wb1'] = ('sMM', cm_kv1)
            b.wait('scalar', 'sMM', cm_kv1)
            b.emit('scalar', lambda e, t0=t0: e.activation(k1b[:, t0:t0 + TT], pb[2][0:128, 0:TT],
                                                           AF.Identity, bias=KCOL, scale=1.0))
            b.emit('scalar', lambda e: e.activation(tmpb[:, 0:TT], pb[3][0:128, 0:TT],
                                                    AF.Identity, bias=VCOL, scale=1.0))
            cs_kv1 = b.mark('scalar', 'sSC')
            b.wait('tensor', 'sSC', cs_kv1)
            for bb_ in range(2):
                cm_tr = b.emit_inc('tensor', lambda e, bb_=bb_: e.transpose(
                    pb[2][0:128, 0:128], tmpb[:, bb_ * 128:bb_ * 128 + 128],
                    ident[:, :]), 'sMM')
                b.wait('scalar', 'sMM', cm_tr)
                dst = v1t[:, (2 * t + bb_) * 128:(2 * t + bb_) * 128 + 128]
                b.emit('scalar', lambda e, dst=dst: e.copy(dst, pb[2][0:128, 0:128]))
                cs_tr = b.mark('scalar', 'sSC')
                b.wait('tensor', 'sSC', cs_tr)

            for o in range(H):
                par = o % 2
                buf = wba if par == 0 else wbb
                bank = pb[par]
                cw = b.dma('sync', buf[:, :], wq1[o], 'sW%d' % par, guard='wb%d' % par)
                b.wait('tensor', 'sW%d' % par, cw)

                def _grp(e, buf=buf, bank=bank):
                    for c in range(NCH):
                        last = e.matmul(bank[0:128, 0:128], buf[:, c * 128:(c + 1) * 128],
                                        a1[:, c * TT:c * TT + 128],
                                        start=(c == 0), stop=(c == NCH - 1))
                    return last
                cmm = b.emit_inc('tensor', _grp, 'sMM')
                b.slot_mm['wb%d' % par] = ('sMM', cmm)
                b.wait('scalar', 'sMM', cmm)
                dst = a2[:, o * 128:(o + 1) * 128]
                bc = bcol('qb1', o)
                b.emit('scalar', lambda e, dst=dst, bank=bank, bc=bc:
                       e.activation(dst, bank[0:128, 0:128], AF.Identity, bias=bc, scale=1.0))
                cs = b.mark('scalar', 'sSC')
                b.wait('tensor', 'sSC', cs)

            for h in range(H):
                nkc = 2 * t + 2
                for kc in range(nkc):
                    sb = pb[2 + (kc % 2)]
                    cms = b.emit_inc('tensor', lambda e, sb=sb, kc=kc, h=h:
                                     e.matmul(sb[0:128, 0:128], k1b[:, kc * 128:kc * 128 + 128],
                                              a2[:, h * 128:(h + 1) * 128],
                                              start=True, stop=True), 'sMM')
                    b.wait('scalar', 'sMM', cms)
                    esl = attns[:, (kc % 2) * TT:(kc % 2) * TT + 128]
                    b.emit('scalar', lambda e, sb=sb, esl=esl:
                           e.activation(esl, sb[0:128, 0:128], AF.Exp, bias=0.0, scale=SCALE))
                    css = b.mark('scalar', 'sSC')
                    if kc >= nkc - 2:
                        b.wait('vector', 'sSC', css)
                        if kc == nkc - 2:
                            def _msk(e, esl=esl):
                                e.tensor_tensor(esl, esl, diag[:, :], ALU.mult)
                                e.drain()
                        else:
                            def _msk(e, esl=esl):
                                e.tensor_scalar(esl, esl, MLCOL, None, ALU.mult)
                                e.drain()
                        b.emit('vector', _msk)
                        cvm = b.mark('vector', 'sV')
                        b.wait('tensor', 'sV', cvm)
                    else:
                        b.wait('tensor', 'sSC', css)

                    def _av(e, esl=esl, kc=kc, nkc=nkc):
                        e.matmul(pb[4][0:128, 0:128], v1t[:, kc * 128:kc * 128 + 128], esl,
                                 start=(kc == 0), stop=(kc == nkc - 1))
                        return e.matmul(pb[6][0:1, 0:128], onec[:, :], esl,
                                        start=(kc == 0), stop=(kc == nkc - 1))
                    b.emit_inc('tensor', _av, 'sMM')
                b.wait('vector', 'sMM', b.cnt['sMM'])

                def _r(e):
                    e.reciprocal(row(R_R1, 128), pb[6][0:1, 0:128])
                    e.drain()
                b.emit('vector', _r)
                cv_r = b.mark('vector', 'sV')
                b.wait('tensor', 'sV', cv_r)
                cm_rb = b.emit_inc('tensor', lambda e: e.matmul(pb[2][0:128, 0:128], oner[:, :],
                                                                row(R_R1, 128),
                                                                start=True, stop=True), 'sMM')
                b.wait('scalar', 'sMM', cm_rb)
                b.emit('scalar', lambda e: e.copy(tmpb[:, 0:128], pb[4][0:128, 0:128]))
                cs_av = b.mark('scalar', 'sSC')
                b.wait('vector', 'sSC', cs_av)

                def _nrm(e, h=h):
                    e.tensor_tensor(a1[:, h * 128:(h + 1) * 128], tmpb[:, 0:128],
                                    pb[2][0:128, 0:128], ALU.mult)
                    e.drain()
                b.emit('vector', _nrm)
                cv_n = b.mark('vector', 'sV')
                b.wait('tensor', 'sV', cv_n)

            proj(wo1, a1, 128, 128, lambda o: a2[:, o * 128:(o + 1) * 128],
                 dep_v=b.cnt['sV'], bias_name='ob1', stats=True)
            cvr = rows_from_stats(128)
            cbc2 = bcast_rows(128, cvr)
            residual_update(a2, 128, TT, cbc2)

            in_stats(xt, 128, TT, a1)
            cvr = rows_from_stats(128)
            cbc = bcast_rows(128, cvr)
            cln = ln_into(a1, xt, 128, TT, cbc)
            b.wait('vector', 'sV', cln)

            def _split(e):
                e.tensor_copy(lnpair[:, 0:W_], a1[:, 0:W_])
                e.drain()
                e.tensor_tensor(lnpair[:, W_:2 * W_], a1[:, 0:W_], lnpair[:, 0:W_], ALU.subtract)
                e.drain()
            b.emit('vector', _split)
            cv_sp = b.mark('vector', 'sV')
            b.wait('tensor', 'sV', cv_sp)

            tern_sc_done = {0: 0, 1: 0}
            b.wait('scalar', 'sV', cv_sp)
            for o in range(NCH):
                bank = pb[o % 2]
                if t == 0:
                    par = o % 2
                    buf = wba if par == 0 else wbb
                    b.wait('sync', 'sSC', tern_sc_done[par])
                    cw = b.dma('sync', buf[:, :], wf1[o], 'sW%d' % par, guard='wb%d' % par)
                    b.wait('scalar', 'sW%d' % par, cw)

                    def _tern_s(e, buf=buf):
                        e.activation(a1[:, 0:W_], buf[:, :], AF.Abs, bias=0.0, scale=1.0)
                        e.activation(a1[:, W_:2 * W_], buf[:, :], AF.Sign, bias=0.0, scale=1.0)
                        e.drain()
                    b.emit('scalar', _tern_s)
                    cs_t = b.mark('scalar', 'sSC')
                    tern_sc_done[par] = cs_t
                    b.slot_mm['wb%d' % par] = ('sSC', cs_t)
                    b.wait('vector', 'sSC', cs_t)
                    b.wait('vector', 'sTS', b.cnt['sTS'])

                    def _tern_v(e):
                        e.tensor_single_scalar(a1[:, 0:W_], a1[:, 0:W_], 0.1, ALU.is_ge)
                        e.drain()
                        e.tensor_tensor(ternb[:, :], a1[:, W_:2 * W_], a1[:, 0:W_], ALU.mult)
                        e.drain()
                    b.emit('vector', _tern_v)
                    cv_t = b.mark('vector', 'sV')
                    b.wait('gpsimd', 'sV', cv_t)
                    b.dma('gpsimd', tern_sc[o], ternb[:, :], 'sTS')
                    b.wait('tensor', 'sV', cv_t)
                else:
                    if o == 0:
                        b.wait('sync', 'sTS', 32 * 16)
                    cw = b.dma('sync', ternb[:, :], tern_sc[o], 'sW0',
                               guard=(['tern', 'wb0'] if o == 0 else 'tern'))
                    b.wait('tensor', 'sW0', cw)

                def _grp(e, bank=bank):
                    for c in range(NCH):
                        e.matmul(bank[0:128, 0:128], ternb[:, c * 128:(c + 1) * 128],
                                 lnpair[:, c * 128:c * 128 + 128], start=(c == 0), stop=False)
                    for c in range(NCH):
                        last = e.matmul(bank[0:128, 0:128], ternb[:, c * 128:(c + 1) * 128],
                                        lnpair[:, W_ + c * 128:W_ + c * 128 + 128],
                                        start=False, stop=(c == NCH - 1))
                    return last
                cmm = b.emit_inc('tensor', _grp, 'sMM')
                b.slot_mm['tern'] = ('sMM', cmm)
                b.wait('vector', 'sMM', cmm)
                dst = a2[:, o * 128:(o + 1) * 128]

                def _ev(e, dst=dst, bank=bank, o=o):
                    e.tensor_scalar(dst, bank[0:128, 0:128], bcol('fb1', o), bcol('fs1', o),
                                    ALU.add, ALU.mult)
                    e.drain()
                b.emit('vector', _ev)
                cvv = b.mark('vector', 'sV')
                b.wait('scalar', 'sV', cvv)
                b.emit('scalar', lambda e, dst=dst: e.activation(tmpb[:, 256:384], dst,
                                                                 AF.Square, bias=0.0, scale=1.0))
                cs2 = b.mark('scalar', 'sSC')
                b.wait('tensor', 'sSC', cs2)

                def _stat(e, dst=dst, o=o):
                    e.matmul(pb[6][0:1, 0:128], onec[:, :], dst, start=(o == 0), stop=(o == NCH - 1))
                    return e.matmul(pb[7][0:1, 0:128], onec[:, :], tmpb[:, 256:384],
                                    start=(o == 0), stop=(o == NCH - 1))
                b.emit_inc('tensor', _stat, 'sMM')

            b.slot_mm['wb0'] = b.slot_mm['tern']
            cvr = rows_from_stats(128)
            cbc3 = bcast_rows(128, cvr)
            b.wait('vector', 'sMM', cbc3)

            def _lnaff(e):
                for c in range(NCH):
                    yc = a2[:, c * 128:(c + 1) * 128]
                    e.tensor_tensor(yc, yc, pb[0][0:128, 0:128], ALU.mult)
                    e.drain()
                    e.tensor_tensor(yc, yc, pb[1][0:128, 0:128], ALU.subtract)
                    e.drain()
                    e.tensor_scalar(yc, yc, bcol('fg1', c), bcol('fbt1', c), ALU.mult, ALU.add)
                    e.drain()
            b.emit('vector', _lnaff)
            cv_la = b.mark('vector', 'sV')
            b.wait('scalar', 'sV', cv_la)
            b.emit('scalar', lambda e: e.activation(a1[:, 0:W_], a2[:, 0:W_], AF.Tanh,
                                                    bias=0.0, scale=QSCOL))
            cs_th = b.mark('scalar', 'sSC')
            b.wait('vector', 'sSC', cs_th)

            def _round2(e):
                e.tensor_scalar(a2[:, 0:W_], a1[:, 0:W_], 127.0, 128.5, ALU.mult, ALU.add)
                e.drain()
                e.tensor_copy(a1[:, 0:W_], a2[:, 0:W_])
                e.drain()
                for k in (128.0, 64.0, 32.0, 16.0, 8.0, 4.0, 2.0, 1.0):
                    e.tensor_scalar(a1[:, W_:2 * W_], a1[:, 0:W_], k, k, ALU.is_ge, ALU.mult)
                    e.drain()
                    e.tensor_tensor(a1[:, 0:W_], a1[:, 0:W_], a1[:, W_:2 * W_], ALU.subtract)
                    e.drain()
                e.tensor_tensor(a2[:, 0:W_], a2[:, 0:W_], a1[:, 0:W_], ALU.subtract)
                e.drain()
                e.tensor_scalar(a2[:, 0:W_], a2[:, 0:W_], -128.0, 1.0 / 127.0, ALU.add, ALU.mult)
                e.drain()
            b.emit('vector', _round2)
            cv_rd = b.mark('vector', 'sV')

            b.wait('scalar', 'sV', cv_rd)
            b.emit('scalar', lambda e: e.activation(a1[:, 0:W_], a2[:, 0:W_], AF.Square,
                                                    bias=0.0, scale=1.0))
            cs3 = b.mark('scalar', 'sSC')
            b.wait('tensor', 'sSC', cs3)

            def _st3(e):
                for c in range(NCH):
                    e.matmul(pb[6][0:1, 0:128], onec[:, :], a2[:, c * 128:(c + 1) * 128],
                             start=(c == 0), stop=(c == NCH - 1))
                    last = e.matmul(pb[7][0:1, 0:128], onec[:, :], a1[:, c * 128:(c + 1) * 128],
                                    start=(c == 0), stop=(c == NCH - 1))
                return last
            b.emit_inc('tensor', _st3, 'sMM')
            cvr = rows_from_stats(128)
            cbc4 = bcast_rows(128, cvr)
            residual_update(a2, 128, TT, cbc4)

            b.wait('vector', 'sV', b.cnt['sV'])

            def _stage(e):
                e.tensor_copy(a2[:, 0:W_].rearrange('p (c w) -> p c w', c=NCH),
                              xt[:].rearrange('p (c w) -> p c w', c=NCH)[:, :, 0:128])
                e.drain()
            b.emit('vector', _stage)
            cv_stage_prev = b.mark('vector', 'sV')
            b.wait('gpsimd', 'sV', cv_stage_prev)
            cx_out_prev = b.dma('gpsimd', y_out[t],
                                a2[:, 0:W_].rearrange('p (c w) -> p c w', c=NCH), 'sOUT')
            b.wait('vector', 'sOUT', cx_out_prev)

        b.wait('gpsimd', 'sOUT', b.cnt['sOUT'])
        for s in ('sTS', 'sCS', 'sXT', 'sREP'):
            if b.cnt[s]:
                b.wait('gpsimd', s, b.cnt[s])

        with nc.Block() as block:
            @block.sync
            def _(e):
                for f in b.q['sync']:
                    f(e)

            @block.tensor
            def _(e):
                for f in b.q['tensor']:
                    f(e)

            @block.vector
            def _(e):
                for f in b.q['vector']:
                    f(e)

            @block.scalar
            def _(e):
                for f in b.q['scalar']:
                    f(e)

            @block.gpsimd
            def _(e):
                for f in b.q['gpsimd']:
                    f(e)

    return nc



def _rope_cache():
    base = 10000.0
    inv_freq = 1.0 / base ** (np.arange(0, DH, 2, dtype=np.float32) / DH)
    t = np.arange(S, dtype=np.float32)
    f = np.outer(t, inv_freq)
    cos = np.zeros((S, DH), dtype=np.float32)
    sin = np.zeros((S, DH), dtype=np.float32)
    cos[:, 0::2] = np.cos(f)
    cos[:, 1::2] = np.cos(f)
    sin[:, 0::2] = np.sin(f)
    sin[:, 1::2] = np.sin(f)
    return cos, sin


def _pack_w(wt):
    return np.ascontiguousarray(
        wt.reshape(NCH, 128, NCH, 128).transpose(2, 1, 0, 3).reshape(NCH, 128, W_))


def _pack_w_small(wt, outw):
    return np.ascontiguousarray(
        wt.reshape(NCH, 128, outw).transpose(1, 0, 2).reshape(128, NCH * outw))


def kernel(**inputs):
    ins = {k: np.asarray(v) for k, v in inputs.items()}
    x0 = ins['L0_attn_x']

    cos, sin = _rope_cache()

    sh = {}
    sh['wq0'] = _pack_w(np.ascontiguousarray(ins['L0_attn_Wq'].T))
    sh['wo0'] = _pack_w(np.ascontiguousarray(ins['L0_attn_Wo'].T))
    sh['wb0'] = _pack_w(np.ascontiguousarray(ins['L0_ffn_W_base'].T))
    sh['wq1'] = _pack_w(np.ascontiguousarray(ins['L1_attn_wq_w'].T))
    sh['wo1'] = _pack_w(np.ascontiguousarray(ins['L1_attn_wo_w'].T))
    sh['wf1'] = _pack_w(np.ascontiguousarray(ins['L1_ffn_weight'].T))
    sh['wk0'] = _pack_w_small(np.ascontiguousarray(ins['L0_attn_Wk'].T), 128)
    sh['wv0'] = _pack_w_small(np.ascontiguousarray(ins['L0_attn_Wv'].T), 128)
    sh['wk1'] = _pack_w_small(np.ascontiguousarray(ins['L1_attn_wk_w'].T), 128)
    sh['wv1'] = _pack_w_small(np.ascontiguousarray(ins['L1_attn_wv_w'].T), 128)
    sh['wrt'] = _pack_w_small(np.ascontiguousarray(ins['L0_ffn_W_router'].T), 8)
    sh['atw'] = _pack_w_small(np.ascontiguousarray(ins['L0_ffn_A'].reshape(128, 4096).T), 128)
    bt = np.ascontiguousarray(ins['L0_ffn_B'].transpose(0, 2, 1).reshape(128, 4096))
    sh['btw'] = np.ascontiguousarray(bt.reshape(128, NCH, 128).transpose(1, 0, 2))
    for nm, key in (('bb0', 'L0_ffn_b_base'), ('qb1', 'L1_attn_wq_b'), ('ob1', 'L1_attn_wo_b'),
                    ('fb1', 'L1_ffn_bias'), ('fs1', 'L1_ffn_scale'),
                    ('fg1', 'L1_ffn_ln_gamma'), ('fbt1', 'L1_ffn_ln_beta')):
        sh[nm] = np.ascontiguousarray(ins[key].reshape(NCH, 128).T)
    sh['kb1'] = np.ascontiguousarray(ins['L1_attn_wk_b'].reshape(128, 1))
    sh['vb1'] = np.ascontiguousarray(ins['L1_attn_wv_b'].reshape(128, 1))
    sh['qs1'] = np.ascontiguousarray(ins['L1_ffn_quant_scale'].reshape(1, 1))
    kq = np.arange(128)
    sh['diagi'] = (kq[:, None] <= kq[None, :]).astype(np.float32)
    sh['identi'] = np.eye(128, dtype=np.float32)
    rotm = np.zeros((128, 128), np.float32)
    for d in range(64):
        rotm[d, 64 + d] = -1.0
        rotm[64 + d, d] = 1.0
    sh['rotmi'] = np.ascontiguousarray(rotm.T)
    rep16 = np.zeros((8, 128), np.float32)
    for e_ in range(8):
        rep16[e_, e_ * 16:(e_ + 1) * 16] = 2.0
    sh['rep16i'] = rep16

    in_maps = []
    for core in range(8):
        seq, role = core // 2, core % 2
        perm = np.arange(S)
        if role == 1:
            perm = np.ascontiguousarray(perm.reshape(-1, 2, 128)[:, ::-1, :]).reshape(-1)
        xl = x0[seq][perm]
        m = dict(sh)
        m['x0p'] = np.ascontiguousarray(xl.T.reshape(NCH, 128, S))
        m['cosp'] = np.ascontiguousarray(cos[perm].T)
        m['sinp'] = np.ascontiguousarray(sin[perm].T)
        if role == 0:
            m['rampi'] = np.arange(TT, dtype=np.float32).reshape(1, TT)
        else:
            m['rampi'] = np.concatenate([np.arange(128, 256), np.arange(0, 128)]
                                        ).astype(np.float32).reshape(1, TT)
        m['mlasti'] = np.full((128, 1), float(role), np.float32)
        in_maps.append(m)

    nc = build_nc()
    res = run_bass_kernel_spmd(nc, in_maps, core_ids=list(range(8)))

    out = np.empty((4, S, D), np.float32)
    for core in range(8):
        seq, role = core // 2, core % 2
        yo = res.results[core]['y_out']
        for t in range(NT):
            g = 2 * t + role
            out[seq, g * 128:(g + 1) * 128, :] = yo[t].transpose(2, 1, 0).reshape(128, D)
    return out


# version 03
# speedup vs baseline: 1.1425x; 1.1425x over previous
import math
import numpy as np
from contextlib import ExitStack

import concourse.bass as bass
import concourse.mybir as mybir
from concourse.bass_utils import run_bass_kernel_spmd

F32 = mybir.dt.float32
BF16 = mybir.dt.bfloat16
ALU = mybir.AluOpType
AF = mybir.ActivationFunctionType
AX = mybir.AxisListType

D = 4096
NCH = 32
S = 2048
TT = 256
NT = 8
H = 32
DH = 128
EPS = 1e-5
SCALE = 1.0 / math.sqrt(128.0)
NEG_BIG = -1e30
W_ = NCH * 128

CFG_NT = NT


class Bld:
    ENGS = ('sync', 'tensor', 'vector', 'scalar', 'gpsimd')

    def __init__(self, nc, es):
        self.nc = nc
        self.q = {e: [] for e in self.ENGS}
        self.h = {}
        self.cnt = {}
        for name in ('sW0', 'sW1', 'sP0', 'sP1', 'sB0', 'sB1', 'sRT',
                     'sC', 'sXT', 'sCS', 'sREP', 'sTS', 'sOUT',
                     'sMM', 'sV', 'sSC'):
            self.h[name] = es.enter_context(nc.semaphore(name))
            self.cnt[name] = 0
        self.slot_mm = {}

    def emit(self, eng, fn):
        self.q[eng].append(fn)

    def wait(self, eng, sname, val):
        assert val <= self.cnt[sname], (sname, val, self.cnt[sname])
        if val <= 0:
            return
        h = self.h[sname]
        self.q[eng].append(lambda e, h=h, v=val: e.wait_ge(h, v))

    def dma(self, eng, out, in_, sname, guard=None):
        if guard is not None:
            for g in ([guard] if isinstance(guard, str) else guard):
                gs, gv = self.slot_mm.get(g, ('sMM', 0))
                self.wait(eng, gs, gv)
        h = self.h[sname]
        self.cnt[sname] += 16
        self.q[eng].append(lambda e, o=out, i=in_, h=h: e.dma_start(out=o, in_=i).then_inc(h, 16))
        return self.cnt[sname]

    def mark(self, eng, sname):
        h = self.h[sname]
        self.cnt[sname] += 1
        self.q[eng].append(lambda e, h=h: e.drain().then_inc(h, 1))
        return self.cnt[sname]

    def inc(self, eng, sname):
        h = self.h[sname]
        self.cnt[sname] += 1
        self.q[eng].append(lambda e, h=h: e.drain().then_inc(h, 1))
        return self.cnt[sname]

    def emit_inc(self, eng, fn, sname):
        h = self.h[sname]
        self.cnt[sname] += 1

        def g(e, fn=fn, h=h):
            inst = fn(e)
            inst.then_inc(h, 1)
        self.q[eng].append(g)
        return self.cnt[sname]


def build_nc():
    nc = bass.Bass('TRN2', target_bir_lowering=False, debug=False)
    n_t = CFG_NT

    def din(name, shape):
        return nc.dram_tensor(name, list(shape), F32, kind='ExternalInput').ap()

    x0p = din('x0p', (NCH, 128, S))
    cosp = din('cosp', (128, S))
    sinp = din('sinp', (128, S))
    rampi = din('rampi', (1, TT))
    mlasti = din('mlasti', (128, 1))
    diagi = din('diagi', (128, 128))
    identi = din('identi', (128, 128))
    rotmi = din('rotmi', (128, 128))
    rep16i = din('rep16i', (8, 128))
    wq0 = din('wq0', (NCH, 128, W_))
    wo0 = din('wo0', (NCH, 128, W_))
    wb0 = din('wb0', (NCH, 128, W_))
    wq1 = din('wq1', (NCH, 128, W_))
    wo1 = din('wo1', (NCH, 128, W_))
    wf1 = din('wf1', (NCH, 128, W_))
    wk0 = din('wk0', (128, W_))
    wv0 = din('wv0', (128, W_))
    wk1 = din('wk1', (128, W_))
    wv1 = din('wv1', (128, W_))
    wrt = din('wrt', (128, NCH * 8))
    atw = din('atw', (128, W_))
    btw = din('btw', (NCH, 128, 128))
    bb0 = din('bb0', (128, NCH))
    qb1 = din('qb1', (128, NCH))
    ob1 = din('ob1', (128, NCH))
    fb1 = din('fb1', (128, NCH))
    fs1 = din('fs1', (128, NCH))
    fg1 = din('fg1', (128, NCH))
    fbt1 = din('fbt1', (128, NCH))
    kb1 = din('kb1', (128, 1))
    vb1 = din('vb1', (128, 1))
    qs1 = din('qs1', (1, 1))
    y_out = nc.dram_tensor('y_out', [NT, 128, NCH, 128], F32, kind='ExternalOutput').ap()
    tern_sc = nc.dram_tensor('tern_sc', [NCH, 128, W_], BF16, kind='Internal').ap()

    es = ExitStack()
    with es:
        SB = lambda n, shp, dt=F32: es.enter_context(nc.sbuf_tensor(n, list(shp), dt))

        xt = SB('xt', (128, NCH * TT))
        a1 = SB('a1', (128, NCH * TT))
        a2 = SB('a2', (128, NCH * TT))
        wba = SB('wba', (128, W_))
        wbb = SB('wbb', (128, W_))
        k0r = SB('k0r', (128, S))
        v0t = SB('v0t', (128, S))
        k1b = SB('k1b', (128, S))
        v1t = SB('v1t', (128, S))
        costl = SB('costl', (128, TT))
        sintl = SB('sintl', (128, TT))
        attns = SB('attns', (128, 512))
        crep = SB('crep', (128, 512))
        tmpb = SB('tmpb', (128, 512))
        btb = SB('btb', (128, 256))
        gbuf = SB('gbuf', (128, TT))
        wrep = SB('wrep', (128, TT))
        rows = SB('rows', (1, 2048))
        ternb = SB('ternb', (128, W_), BF16)
        lnpair = SB('lnpair', (128, 2 * W_), BF16)
        biases = SB('biases', (128, 8 * NCH))
        smallc = SB('smallc', (128, 8))
        diag = SB('diag', (128, 128))
        ident = SB('ident', (128, 128))
        rotm = SB('rotm', (128, 128))
        rep16 = SB('rep16', (8, 128))
        onec = SB('onec', (128, 1))
        oner = SB('oner', (1, 128))
        ltt = SB('ltt', (128, 16))
        rsc = SB('rsc', (128, 64))

        wrtb = btb
        wts = tmpb[0:8, 256:512]

        pb = [es.enter_context(nc.psum_tensor('pb%d' % i, [128, 512], F32)) for i in range(8)]

        b = Bld(nc, es)

        def row(off, w):
            return rows[0:1, off:off + w]

        R_AL, R_AL1, R_M, R_V, R_TMP = 0, 256, 512, 1024, 1536
        R_R1, R_R2, R_C1, R_C2 = 512, 768, 1024, 1280

        BIAS = {'bb0': 0, 'qb1': NCH, 'ob1': 2 * NCH, 'fb1': 3 * NCH,
                'fs1': 4 * NCH, 'fg1': 5 * NCH, 'fbt1': 6 * NCH}

        def bcol(name, o):
            return biases[:, BIAS[name] + o: BIAS[name] + o + 1]

        KCOL = smallc[:, 0:1]
        VCOL = smallc[:, 1:2]
        QSCOL = smallc[:, 2:3]
        MLCOL = smallc[:, 3:4]

        for dst, src in ((diag[:, :], diagi), (ident[:, :], identi), (rotm[:, :], rotmi),
                         (rep16[:, :], rep16i),
                         (biases[:, 0:NCH], bb0), (biases[:, NCH:2 * NCH], qb1),
                         (biases[:, 2 * NCH:3 * NCH], ob1), (biases[:, 3 * NCH:4 * NCH], fb1),
                         (biases[:, 4 * NCH:5 * NCH], fs1), (biases[:, 5 * NCH:6 * NCH], fg1),
                         (biases[:, 6 * NCH:7 * NCH], fbt1),
                         (smallc[:, 0:1], kb1), (smallc[:, 1:2], vb1), (smallc[:, 3:4], mlasti),
                         (rows[0:1, 1792:1792 + TT], rampi), (rows[0:1, R_M:R_M + 1], qs1)):
            b.dma('gpsimd', dst, src, 'sC')
        c_load = b.cnt['sC']

        b.wait('vector', 'sC', c_load)

        def _init_v(e):
            e.memset(onec[:, :], 1.0)
            e.memset(oner[:, :], 1.0)
            e.drain()
            e.tensor_scalar(row(R_AL, TT), row(1792, TT), 1.0 / 255.0, None, ALU.mult)
            e.tensor_scalar(row(R_AL1, TT), row(1792, TT), -1.0 / 255.0, 1.0, ALU.mult, ALU.add)
            e.drain()
            e.reciprocal(row(R_M + 1, 1), row(R_M, 1))
            e.drain()
        b.emit('vector', _init_v)
        cv = b.mark('vector', 'sV')
        b.wait('tensor', 'sV', cv)
        cm = b.emit_inc('tensor', lambda e: e.matmul(pb[0][0:128, 0:1], oner[:, :],
                                                     row(R_M + 1, 1), start=True, stop=True), 'sMM')
        b.wait('scalar', 'sMM', cm)
        b.emit('scalar', lambda e: e.copy(smallc[:, 2:3], pb[0][0:128, 0:1]))
        cs = b.mark('scalar', 'sSC')
        b.wait('tensor', 'sSC', cs)

        def rows_from_stats(ntok, fold=1):
            b.wait('vector', 'sMM', b.cnt['sMM'])

            def _rows(e):
                if fold == 1:
                    e.tensor_scalar(row(R_M, ntok), pb[6][0:1, 0:ntok], 1.0 / 4096.0, None, ALU.mult)
                    e.tensor_scalar(row(R_V, ntok), pb[7][0:1, 0:ntok], 1.0 / 4096.0, None, ALU.mult)
                    e.drain()
                else:
                    for bank_i, dst in ((6, R_M), (7, R_V)):
                        e.tensor_copy(row(dst, ntok), pb[bank_i][0:1, 0:ntok])
                        e.drain()
                        for f in range(1, fold):
                            e.tensor_tensor(row(dst, ntok), row(dst, ntok),
                                            pb[bank_i][0:1, f * ntok:(f + 1) * ntok], ALU.add)
                            e.drain()
                        e.tensor_scalar(row(dst, ntok), row(dst, ntok), 1.0 / 4096.0, None, ALU.mult)
                        e.drain()
                e.tensor_tensor(row(R_TMP, ntok), row(R_M, ntok), row(R_M, ntok), ALU.mult)
                e.drain()
                e.tensor_tensor(row(R_V, ntok), row(R_V, ntok), row(R_TMP, ntok), ALU.subtract)
                e.drain()
                e.tensor_scalar(row(R_V, ntok), row(R_V, ntok), EPS, None, ALU.add)
                e.drain()
            b.emit('vector', _rows)
            cv = b.mark('vector', 'sV')
            b.wait('scalar', 'sV', cv)
            b.emit('scalar', lambda e: e.activation(row(R_TMP, ntok), row(R_V, ntok),
                                                    AF.Sqrt, bias=0.0, scale=1.0))
            cs = b.mark('scalar', 'sSC')
            b.wait('vector', 'sSC', cs)

            def _rr(e):
                e.reciprocal(row(R_V, ntok), row(R_TMP, ntok))
                e.drain()
                e.tensor_tensor(row(R_TMP, ntok), row(R_M, ntok), row(R_V, ntok), ALU.mult)
                e.drain()
            b.emit('vector', _rr)
            return b.mark('vector', 'sV')

        def bcast_rows(ntok, dep_v):
            b.wait('tensor', 'sV', dep_v)

            def _bc(e):
                e.matmul(pb[0][0:128, 0:ntok], oner[:, :], row(R_V, ntok), start=True, stop=True)
                return e.matmul(pb[1][0:128, 0:ntok], oner[:, :], row(R_TMP, ntok), start=True, stop=True)
            return b.emit_inc('tensor', _bc, 'sMM')

        def in_stats(src_sb, ntok, chunk_w, sq_buf):
            b.wait('scalar', 'sV', b.cnt['sV'])
            if ntok == chunk_w:
                b.emit('scalar', lambda e: e.activation(sq_buf[:, 0:NCH * ntok],
                                                        src_sb[:, 0:NCH * ntok], AF.Square,
                                                        bias=0.0, scale=1.0))
            else:
                def _sq(e):
                    for c in range(NCH):
                        e.activation(sq_buf[:, c * ntok:(c + 1) * ntok],
                                     src_sb[:, c * chunk_w:c * chunk_w + ntok],
                                     AF.Square, bias=0.0, scale=1.0)
                b.emit('scalar', _sq)
            csq = b.mark('scalar', 'sSC')
            b.wait('tensor', 'sSC', csq)
            if ntok == chunk_w:
                npair = NCH // 2

                def _sums(e):
                    for c2 in range(npair):
                        e.matmul(pb[6][0:1, 0:2 * ntok], onec[:, :],
                                 src_sb[:, c2 * 2 * ntok:(c2 + 1) * 2 * ntok],
                                 start=(c2 == 0), stop=(c2 == npair - 1))
                        last = e.matmul(pb[7][0:1, 0:2 * ntok], onec[:, :],
                                        sq_buf[:, c2 * 2 * ntok:(c2 + 1) * 2 * ntok],
                                        start=(c2 == 0), stop=(c2 == npair - 1))
                    return last
                b.emit_inc('tensor', _sums, 'sMM')
                return 2

            def _sums(e):
                for c in range(NCH):
                    e.matmul(pb[6][0:1, 0:ntok], onec[:, :],
                             src_sb[:, c * chunk_w:c * chunk_w + ntok],
                             start=(c == 0), stop=(c == NCH - 1))
                    last = e.matmul(pb[7][0:1, 0:ntok], onec[:, :],
                                    sq_buf[:, c * ntok:(c + 1) * ntok],
                                    start=(c == 0), stop=(c == NCH - 1))
                return last
            b.emit_inc('tensor', _sums, 'sMM')
            return 1

        def ln_into(dst, src_sb, ntok, chunk_w, cmm_bcast):
            b.wait('vector', 'sMM', cmm_bcast)

            def _ln(e):
                for c in range(NCH):
                    s = src_sb[:, c * chunk_w:c * chunk_w + ntok]
                    d = dst[:, c * ntok:(c + 1) * ntok]
                    e.tensor_tensor(d, s, pb[0][0:128, 0:ntok], ALU.mult)
                    e.drain()
                    e.tensor_tensor(d, d, pb[1][0:128, 0:ntok], ALU.subtract)
                    e.drain()
            b.emit('vector', _ln)
            return b.mark('vector', 'sV')

        def residual_update(y_sb, ntok, chunk_w, cmm_bcast):
            b.wait('vector', 'sMM', cmm_bcast)

            def _res(e):
                for c in range(NCH):
                    yc = y_sb[:, c * ntok:(c + 1) * ntok]
                    t = tmpb[:, 0:ntok]
                    e.tensor_tensor(t, yc, pb[0][0:128, 0:ntok], ALU.mult)
                    e.drain()
                    e.tensor_tensor(t, t, pb[1][0:128, 0:ntok], ALU.subtract)
                    e.drain()
                    dc = xt[:, c * chunk_w:c * chunk_w + ntok]
                    e.tensor_tensor(dc, dc, t, ALU.add)
                    e.drain()
            b.emit('vector', _res)
            return b.mark('vector', 'sV')

        def proj(w_dram, rhs_sb, ntok, chunk_w, out_cb, dep_v, bias_name=None,
                 lora_dep=None, stats=False):
            pend = []
            csc_hist = {}

            def emit_stat(e_o, e_dst):
                def _stat(e, dst=e_dst, o=e_o):
                    e.matmul(pb[6][0:1, 0:ntok], onec[:, :], dst,
                             start=(o == 0), stop=(o == NCH - 1))
                    return e.matmul(pb[7][0:1, 0:ntok], onec[:, :],
                                    tmpb[:, (e_o % 2) * ntok:(e_o % 2) * ntok + ntok],
                                    start=(o == 0), stop=(o == NCH - 1))
                b.emit_inc('tensor', _stat, 'sMM')

            for o in range(NCH):
                par = o % 2
                buf = wba if par == 0 else wbb
                bank = pb[par]
                cw = b.dma('sync', buf[:, :], w_dram[o], 'sW%d' % par, guard='wb%d' % par)
                b.wait('tensor', 'sW%d' % par, cw)
                if lora_dep is not None:
                    cbt = b.dma('sync', btb[:, par * 128:par * 128 + 128], btw[o], 'sB%d' % par,
                                guard='bt%d' % par)
                    b.wait('tensor', 'sB%d' % par, cbt)
                if o == 0 and dep_v:
                    b.wait('tensor', 'sV', dep_v)
                if o == 0 and lora_dep:
                    b.wait('tensor', 'sV', lora_dep)
                if o >= 2:
                    b.wait('tensor', 'sSC', csc_hist[o - 2])

                def _grp(e, buf=buf, bank=bank, par=par):
                    for c in range(NCH):
                        last = e.matmul(bank[0:128, 0:ntok], buf[:, c * 128:(c + 1) * 128],
                                        rhs_sb[:, c * chunk_w:c * chunk_w + ntok],
                                        start=(c == 0), stop=(c == NCH - 1 and lora_dep is None))
                    if lora_dep is not None:
                        last = e.matmul(bank[0:128, 0:ntok], btb[:, par * 128:par * 128 + 128],
                                        gbuf[:, 0:ntok], start=False, stop=True)
                    return last
                cmm = b.emit_inc('tensor', _grp, 'sMM')
                b.slot_mm['wb%d' % par] = ('sMM', cmm)
                b.slot_mm['bt%d' % par] = ('sMM', cmm)
                if stats and pend:
                    po, pdst, pcsc = pend.pop(0)
                    b.wait('tensor', 'sSC', pcsc)
                    emit_stat(po, pdst)
                b.wait('scalar', 'sMM', cmm)
                dst = out_cb(o)
                if bias_name is None:
                    b.emit('scalar', lambda e, dst=dst, bank=bank: e.copy(dst, bank[0:128, 0:ntok]))
                else:
                    bc = bcol(bias_name, o)
                    b.emit('scalar', lambda e, dst=dst, bank=bank, bc=bc:
                           e.activation(dst, bank[0:128, 0:ntok], AF.Identity, bias=bc, scale=1.0))
                if stats:
                    b.emit('scalar', lambda e, bank=bank, o=o:
                           e.activation(tmpb[:, (o % 2) * ntok:(o % 2) * ntok + ntok],
                                        bank[0:128, 0:ntok], AF.Square, bias=0.0, scale=1.0))
                csc = b.mark('scalar', 'sSC')
                csc_hist[o] = csc
                if stats:
                    pend.append((o, dst, csc))
            if stats:
                po, pdst, pcsc = pend.pop(0)
                b.wait('tensor', 'sSC', pcsc)
                emit_stat(po, pdst)

        cv_rope_prev = 0
        cwk0 = b.dma('sync', wba[:, :], wk0, 'sW0', guard='wb0')
        cwv0 = b.dma('sync', wbb[:, :], wv0, 'sW1', guard='wb1')
        b.wait('tensor', 'sW0', cwk0)
        b.wait('tensor', 'sW1', cwv0)
        for g in range(4):
            g0 = g * 512
            for c in range(NCH):
                slot = c % 2
                xc = a1[:, slot * 512: slot * 512 + 512]
                sq = a1[:, 1024 + slot * 512: 1024 + slot * 512 + 512]
                cx = b.dma('sync', xc, x0p[c, :, g0:g0 + 512], 'sP%d' % slot, guard='p0x%d' % slot)
                b.wait('scalar', 'sP%d' % slot, cx)
                b.emit('scalar', lambda e, xc=xc, sq=sq: e.activation(sq, xc, AF.Square,
                                                                     bias=0.0, scale=1.0))
                csq = b.mark('scalar', 'sSC')
                b.wait('tensor', 'sSC', csq)

                def _st(e, xc=xc, sq=sq, c=c):
                    e.matmul(pb[6][0:1, 0:512], onec[:, :], xc, start=(c == 0), stop=(c == NCH - 1))
                    return e.matmul(pb[7][0:1, 0:512], onec[:, :], sq,
                                    start=(c == 0), stop=(c == NCH - 1))
                b.slot_mm['p0x%d' % slot] = ('sMM', b.emit_inc('tensor', _st, 'sMM'))
            cvr = rows_from_stats(512)
            cbc = bcast_rows(512, cvr)
            for c in range(NCH):
                slot = c % 2
                xc = a1[:, slot * 512: slot * 512 + 512]
                lnc = a1[:, 1024 + slot * 512: 1024 + slot * 512 + 512]
                cx = b.dma('sync', xc, x0p[c, :, g0:g0 + 512], 'sP%d' % slot, guard='p0x%d' % slot)
                b.wait('vector', 'sP%d' % slot, cx)
                if c == 0:
                    b.wait('vector', 'sMM', cbc)

                def _ln(e, xc=xc, lnc=lnc):
                    e.tensor_tensor(lnc, xc, pb[0][0:128, 0:512], ALU.mult)
                    e.drain()
                    e.tensor_tensor(lnc, lnc, pb[1][0:128, 0:512], ALU.subtract)
                    e.drain()
                b.emit('vector', _ln)
                cvl = b.mark('vector', 'sV')
                b.wait('tensor', 'sV', cvl)

                def _kv(e, lnc=lnc, c=c):
                    e.matmul(pb[2][0:128, 0:512], wba[:, c * 128:(c + 1) * 128], lnc,
                             start=(c == 0), stop=(c == NCH - 1))
                    return e.matmul(pb[3][0:128, 0:512], wbb[:, c * 128:(c + 1) * 128], lnc,
                                    start=(c == 0), stop=(c == NCH - 1))
                b.slot_mm['p0x%d' % slot] = ('sMM', b.emit_inc('tensor', _kv, 'sMM'))
            cm_kv = b.cnt['sMM']
            b.wait('gpsimd', 'sV', cv_rope_prev)
            ccos = b.dma('gpsimd', a1[:, 2048:2560], cosp[:, g0:g0 + 512], 'sCS')
            csin = b.dma('gpsimd', a1[:, 2560:3072], sinp[:, g0:g0 + 512], 'sCS')
            b.wait('scalar', 'sMM', cm_kv)
            b.emit('scalar', lambda e: e.copy(a2[:, 0:512], pb[2][0:128, 0:512]))
            b.emit('scalar', lambda e: e.copy(a2[:, 512:1024], pb[3][0:128, 0:512]))
            cs_kv = b.mark('scalar', 'sSC')
            b.wait('tensor', 'sSC', cs_kv)
            cm_rot = b.emit_inc('tensor', lambda e: e.matmul(pb[2][0:128, 0:512], rotm[:, :],
                                                             a2[:, 0:512], start=True, stop=True), 'sMM')
            b.wait('vector', 'sMM', cm_rot)
            b.wait('vector', 'sCS', max(ccos, csin))

            def _rope(e, g0=g0):
                e.tensor_tensor(a1[:, 3072:3584], a2[:, 0:512], a1[:, 2048:2560], ALU.mult)
                e.drain()
                e.tensor_tensor(a1[:, 3584:4096], pb[2][0:128, 0:512], a1[:, 2560:3072], ALU.mult)
                e.drain()
                e.tensor_tensor(k0r[:, g0:g0 + 512], a1[:, 3072:3584], a1[:, 3584:4096], ALU.add)
                e.drain()
            b.emit('vector', _rope)
            cv_rope = b.mark('vector', 'sV')
            cv_rope_prev = cv_rope
            b.wait('tensor', 'sV', cv_rope)
            for bb_ in range(4):
                cmt = b.emit_inc('tensor', lambda e, bb_=bb_: e.transpose(
                    pb[4][0:128, 0:128], a2[:, 512 + bb_ * 128:512 + bb_ * 128 + 128],
                    ident[:, :]), 'sMM')
                b.wait('scalar', 'sMM', cmt)
                dst = v0t[:, (g * 4 + bb_) * 128:(g * 4 + bb_) * 128 + 128]
                b.emit('scalar', lambda e, dst=dst: e.copy(dst, pb[4][0:128, 0:128]))
                cst = b.mark('scalar', 'sSC')
                b.wait('tensor', 'sSC', cst)
        b.slot_mm['wb0'] = ('sMM', b.cnt['sMM'])
        b.slot_mm['wb1'] = ('sMM', b.cnt['sMM'])

        cv_stage_prev = cv_rope_prev
        cx_out_prev = 0
        for t in range(n_t):
            t0 = t * TT

            b.wait('gpsimd', 'sV', cv_stage_prev)
            cxt = b.dma('gpsimd', xt[:].rearrange('p (c w) -> p c w', c=NCH),
                        x0p[:, :, t0:t0 + TT].rearrange('c p w -> p c w'), 'sXT')
            ccos = b.dma('gpsimd', costl[:, :], cosp[:, t0:t0 + TT], 'sCS')
            csin = b.dma('gpsimd', sintl[:, :], sinp[:, t0:t0 + TT], 'sCS')

            b.wait('scalar', 'sXT', cxt)
            fold = in_stats(xt, TT, TT, a1)
            cvr = rows_from_stats(TT, fold)
            cbc = bcast_rows(TT, cvr)
            cln = ln_into(a1, xt, TT, TT, cbc)

            for o in range(H):
                par = o % 2
                buf = wba if par == 0 else wbb
                bank = pb[par]
                cw = b.dma('sync', buf[:, :], wq0[o], 'sW%d' % par, guard='wb%d' % par)
                b.wait('tensor', 'sW%d' % par, cw)
                if o == 0:
                    b.wait('tensor', 'sV', cln)

                def _grp(e, buf=buf, bank=bank):
                    for c in range(NCH):
                        last = e.matmul(bank[0:128, 0:TT], buf[:, c * 128:(c + 1) * 128],
                                        a1[:, c * TT:(c + 1) * TT],
                                        start=(c == 0), stop=(c == NCH - 1))
                    return last
                cmm = b.emit_inc('tensor', _grp, 'sMM')
                b.slot_mm['wb%d' % par] = ('sMM', cmm)
                b.wait('scalar', 'sMM', cmm)
                b.emit('scalar', lambda e, bank=bank: e.copy(tmpb[:, 0:TT], bank[0:128, 0:TT]))
                cs = b.mark('scalar', 'sSC')
                b.wait('tensor', 'sSC', cs)
                cmr = b.emit_inc('tensor', lambda e: e.matmul(pb[2][0:128, 0:TT], rotm[:, :],
                                                              tmpb[:, 0:TT], start=True, stop=True), 'sMM')
                b.wait('vector', 'sMM', cmr)
                if o == 0:
                    b.wait('vector', 'sCS', max(ccos, csin))

                def _qrope(e, o=o):
                    d = a2[:, o * TT:(o + 1) * TT]
                    e.tensor_tensor(d, tmpb[:, 0:TT], costl[:, :], ALU.mult)
                    e.drain()
                    e.tensor_tensor(tmpb[:, 256:512], pb[2][0:128, 0:TT], sintl[:, :], ALU.mult)
                    e.drain()
                    e.tensor_tensor(d, d, tmpb[:, 256:512], ALU.add)
                    e.drain()
                b.emit('vector', _qrope)
                cvq = b.mark('vector', 'sV')
                b.wait('tensor', 'sV', cvq)

            n_w = 1 if t == 0 else 2
            av_free = 0
            den_free = 0
            for h in range(H):
                for wi in range(n_w):
                    wbase = (t - (n_w - 1 - wi)) * 256
                    nk = min(512, S - wbase) // 128
                    avb = pb[4] if (n_w == 2 and wi == 0) else pb[5]
                    dnb = pb[6] if (n_w == 2 and wi == 0) else pb[7]
                    css = {}
                    pending = []

                    def emit_av(kc, asl, kb, avb=avb, dnb=dnb, nk=nk):
                        if kc == 0:
                            b.wait('tensor', 'sV', av_free)
                            b.wait('tensor', 'sSC', den_free)
                        b.wait('tensor', 'sSC', css[kc])
                        def _av(e, asl=asl, kb=kb, kc=kc, nk=nk, avb=avb, dnb=dnb):
                            e.matmul(avb[0:128, 0:TT], v0t[:, kb:kb + 128], asl,
                                     start=(kc == 0), stop=(kc == nk - 1))
                            return e.matmul(dnb[0:1, 0:TT], onec[:, :], asl,
                                            start=(kc == 0), stop=(kc == nk - 1))
                        b.emit_inc('tensor', _av, 'sMM')

                    for kc in range(nk):
                        kb = wbase + kc * 128
                        sb = pb[2 + (kc % 2)]
                        cms = b.emit_inc('tensor', lambda e, sb=sb, kb=kb, h=h:
                                         e.matmul(sb[0:128, 0:TT], k0r[:, kb:kb + 128],
                                                  a2[:, h * TT:(h + 1) * TT],
                                                  start=True, stop=True), 'sMM')
                        b.wait('scalar', 'sMM', cms)
                        asl = attns[:, (kc % 2) * TT:(kc % 2) * TT + TT]
                        b.emit('scalar', lambda e, sb=sb, asl=asl:
                               e.activation(asl, sb[0:128, 0:TT], AF.Sigmoid, bias=0.0, scale=SCALE))
                        css[kc] = b.mark('scalar', 'sSC')
                        pending.append((kc, asl, kb))
                        if len(pending) > 1:
                            emit_av(*pending.pop(0))
                    emit_av(*pending.pop(0))
                b.wait('vector', 'sMM', b.cnt['sMM'])

                def _cc(e, n_w=n_w):
                    if n_w == 2:
                        e.reciprocal(row(R_R1, TT), pb[6][0:1, 0:TT])
                        e.reciprocal(row(R_R2, TT), pb[7][0:1, 0:TT])
                        e.drain()
                        e.tensor_tensor(row(R_C1, TT), row(R_R1, TT), row(R_AL1, TT), ALU.mult)
                        e.tensor_tensor(row(R_C2, TT), row(R_R2, TT), row(R_AL, TT), ALU.mult)
                        e.drain()
                    else:
                        e.reciprocal(row(R_C2, TT), pb[7][0:1, 0:TT])
                        e.drain()
                b.emit('vector', _cc)
                cv_cc = b.mark('vector', 'sV')
                b.wait('tensor', 'sV', cv_cc)

                def _bc(e, n_w=n_w):
                    if n_w == 2:
                        e.matmul(pb[6][0:128, 0:TT], oner[:, :], row(R_C1, TT), start=True, stop=True)
                    return e.matmul(pb[7][0:128, 0:TT], oner[:, :], row(R_C2, TT),
                                    start=True, stop=True)
                cm_bc = b.emit_inc('tensor', _bc, 'sMM')
                b.wait('scalar', 'sMM', cm_bc)

                def _crep(e, n_w=n_w):
                    if n_w == 2:
                        e.copy(crep[:, 0:TT], pb[6][0:128, 0:TT])
                    e.copy(crep[:, TT:2 * TT], pb[7][0:128, 0:TT])
                b.emit('scalar', _crep)
                cs_cr = b.mark('scalar', 'sSC')
                den_free = cs_cr
                b.wait('vector', 'sSC', cs_cr)

                def _blend(e, h=h, n_w=n_w):
                    d = a1[:, h * TT:(h + 1) * TT]
                    if n_w == 2:
                        e.tensor_tensor(tmpb[:, 0:TT], pb[4][0:128, 0:TT], crep[:, 0:TT], ALU.mult)
                        e.drain()
                        e.tensor_tensor(d, pb[5][0:128, 0:TT], crep[:, TT:2 * TT], ALU.mult)
                        e.drain()
                        e.tensor_tensor(d, d, tmpb[:, 0:TT], ALU.add)
                        e.drain()
                    else:
                        e.tensor_tensor(d, pb[5][0:128, 0:TT], crep[:, TT:2 * TT], ALU.mult)
                        e.drain()
                b.emit('vector', _blend)
                cv_bl = b.mark('vector', 'sV')
                av_free = cv_bl

            proj(wo0, a1, TT, TT, lambda o: a2[:, o * TT:(o + 1) * TT],
                 dep_v=b.cnt['sV'], stats=True)
            cvr = rows_from_stats(TT)
            cbc2 = bcast_rows(TT, cvr)
            residual_update(a2, TT, TT, cbc2)

            fold = in_stats(xt, TT, TT, a1)
            cvr = rows_from_stats(TT, fold)
            cbc = bcast_rows(TT, cvr)
            cln = ln_into(a1, xt, TT, TT, cbc)

            cat = b.dma('sync', wba[:, :], atw, 'sW0', guard='wb0')
            crt = b.dma('sync', wrtb[:, :], wrt, 'sRT', guard=['bt0', 'bt1'])
            b.wait('tensor', 'sW0', cat)
            b.wait('tensor', 'sRT', crt)
            b.wait('tensor', 'sV', cln)

            def _hrt(e):
                for c in range(NCH):
                    e.matmul(pb[2][0:128, 0:TT], wba[:, c * 128:(c + 1) * 128],
                             a1[:, c * TT:(c + 1) * TT], start=(c == 0), stop=(c == NCH - 1))
                    last = e.matmul(pb[3][0:8, 0:TT], wrtb[:, c * 8:(c + 1) * 8],
                                    a1[:, c * TT:(c + 1) * TT], start=(c == 0), stop=(c == NCH - 1))
                return last
            cm_h = b.emit_inc('tensor', _hrt, 'sMM')
            b.slot_mm['wb0'] = ('sMM', cm_h)
            b.slot_mm['bt0'] = ('sMM', cm_h)
            b.slot_mm['bt1'] = ('sMM', cm_h)
            b.wait('scalar', 'sMM', cm_h)
            b.emit('scalar', lambda e: e.copy(tmpb[0:8, 0:TT], pb[3][0:8, 0:TT]))
            cs_l = b.mark('scalar', 'sSC')
            b.wait('tensor', 'sSC', cs_l)
            for half in range(2):
                cm_t = b.emit_inc('tensor', lambda e, half=half: e.transpose(
                    pb[3][0:128, 0:8], tmpb[0:8, half * 128:half * 128 + 128],
                    ident[0:8, 0:8]), 'sMM')
                b.wait('scalar', 'sMM', cm_t)
                b.emit('scalar', lambda e, half=half: e.copy(ltt[:, half * 8:half * 8 + 8],
                                                             pb[3][0:128, 0:8]))
                cs_t = b.mark('scalar', 'sSC')
                b.wait('tensor', 'sSC', cs_t)
            b.wait('vector', 'sSC', b.cnt['sSC'])

            def _gate_v1(e):
                for half in range(2):
                    lt = ltt[:, half * 8:half * 8 + 8]
                    m1 = rsc[:, half * 2 + 0:half * 2 + 1]
                    e.reduce_max(m1, lt, AX.X)
                    e.drain()
                    msk1 = rsc[:, 16 + half * 8:16 + half * 8 + 8]
                    e.tensor_scalar(msk1, lt, m1, None, ALU.is_ge)
                    e.drain()
                    l2 = rsc[:, 32 + half * 8:32 + half * 8 + 8]
                    e.scalar_tensor_tensor(l2, msk1, NEG_BIG, lt, ALU.mult, ALU.add)
                    e.drain()
                    m2 = rsc[:, half * 2 + 1:half * 2 + 2]
                    e.reduce_max(m2, l2, AX.X)
                    e.drain()
                    msk2 = rsc[:, 48 + half * 8:48 + half * 8 + 8]
                    e.tensor_scalar(msk2, l2, m2, None, ALU.is_ge)
                    e.drain()
                    dcol = rsc[:, 4 + half * 2:4 + half * 2 + 1]
                    e.tensor_tensor(dcol, m1, m2, ALU.subtract)
                    e.drain()
            b.emit('vector', _gate_v1)
            cv_g1 = b.mark('vector', 'sV')
            b.wait('scalar', 'sV', cv_g1)

            def _gate_s(e):
                for half in range(2):
                    dcol = rsc[:, 4 + half * 2:4 + half * 2 + 1]
                    e.activation(rsc[:, 8 + half * 2:8 + half * 2 + 1], dcol, AF.Sigmoid,
                                 bias=0.0, scale=1.0)
                    e.activation(rsc[:, 9 + half * 2:9 + half * 2 + 1], dcol, AF.Sigmoid,
                                 bias=0.0, scale=-1.0)
            b.emit('scalar', _gate_s)
            cs_g = b.mark('scalar', 'sSC')
            b.wait('vector', 'sSC', cs_g)

            def _gate_v2(e):
                for half in range(2):
                    w1 = rsc[:, 8 + half * 2:8 + half * 2 + 1]
                    w2 = rsc[:, 9 + half * 2:9 + half * 2 + 1]
                    msk1 = rsc[:, 16 + half * 8:16 + half * 8 + 8]
                    msk2 = rsc[:, 48 + half * 8:48 + half * 8 + 8]
                    t1 = rsc[:, 32 + half * 8:32 + half * 8 + 8]
                    e.tensor_scalar(t1, msk1, w1, None, ALU.mult)
                    e.tensor_scalar(msk2, msk2, w2, None, ALU.mult)
                    e.drain()
                    e.tensor_tensor(msk1, t1, msk2, ALU.add)
                    e.drain()
            b.emit('vector', _gate_v2)
            cv_g2 = b.mark('vector', 'sV')
            b.wait('tensor', 'sV', cv_g2)
            for half in range(2):
                cm_t = b.emit_inc('tensor', lambda e, half=half: e.transpose(
                    pb[3][0:8, 0:128], rsc[:, 16 + half * 8:16 + half * 8 + 8],
                    ident[:, :]), 'sMM')
                b.wait('scalar', 'sMM', cm_t)
                b.emit('scalar', lambda e, half=half: e.copy(wts[:, half * 128:half * 128 + 128],
                                                             pb[3][0:8, 0:128]))
                cs_t = b.mark('scalar', 'sSC')
                b.wait('tensor', 'sSC', cs_t)
            b.wait('tensor', 'sSC', b.cnt['sSC'])
            cm_rep = b.emit_inc('tensor', lambda e: e.matmul(pb[3][0:128, 0:TT], rep16[:, :],
                                                             wts[:, :], start=True, stop=True), 'sMM')
            b.wait('scalar', 'sMM', cm_rep)
            b.emit('scalar', lambda e: e.copy(wrep[:, :], pb[3][0:128, 0:TT]))
            cs_rep = b.mark('scalar', 'sSC')
            b.wait('vector', 'sSC', cs_rep)

            def _g(e):
                e.tensor_tensor(gbuf[:, :], pb[2][0:128, 0:TT], wrep[:, :], ALU.mult)
                e.drain()
            b.emit('vector', _g)
            cv_gg = b.mark('vector', 'sV')

            proj(wb0, a1, TT, TT, lambda o: a2[:, o * TT:(o + 1) * TT],
                 dep_v=cv_gg, bias_name='bb0', lora_dep=cv_gg, stats=True)
            cvr = rows_from_stats(TT)
            cbc2 = bcast_rows(TT, cvr)
            residual_update(a2, TT, TT, cbc2)

            fold = in_stats(xt, TT, TT, a1)
            cvr = rows_from_stats(TT, fold)
            cbc = bcast_rows(TT, cvr)
            cln = ln_into(a1, xt, TT, TT, cbc)

            cwk1 = b.dma('sync', wba[:, :], wk1, 'sW0', guard='wb0')
            cwv1 = b.dma('sync', wbb[:, :], wv1, 'sW1', guard='wb1')
            b.wait('tensor', 'sW0', cwk1)
            b.wait('tensor', 'sW1', cwv1)
            b.wait('tensor', 'sV', cln)

            def _kv1(e):
                for c in range(NCH):
                    e.matmul(pb[2][0:128, 0:TT], wba[:, c * 128:(c + 1) * 128],
                             a1[:, c * TT:(c + 1) * TT], start=(c == 0), stop=(c == NCH - 1))
                    last = e.matmul(pb[3][0:128, 0:TT], wbb[:, c * 128:(c + 1) * 128],
                                    a1[:, c * TT:(c + 1) * TT], start=(c == 0), stop=(c == NCH - 1))
                return last
            cm_kv1 = b.emit_inc('tensor', _kv1, 'sMM')
            b.slot_mm['wb0'] = ('sMM', cm_kv1)
            b.slot_mm['wb1'] = ('sMM', cm_kv1)
            b.wait('scalar', 'sMM', cm_kv1)
            b.emit('scalar', lambda e, t0=t0: e.activation(k1b[:, t0:t0 + TT], pb[2][0:128, 0:TT],
                                                           AF.Identity, bias=KCOL, scale=1.0))
            b.emit('scalar', lambda e: e.activation(tmpb[:, 0:TT], pb[3][0:128, 0:TT],
                                                    AF.Identity, bias=VCOL, scale=1.0))
            cs_kv1 = b.mark('scalar', 'sSC')
            b.wait('tensor', 'sSC', cs_kv1)
            for bb_ in range(2):
                cm_tr = b.emit_inc('tensor', lambda e, bb_=bb_: e.transpose(
                    pb[2][0:128, 0:128], tmpb[:, bb_ * 128:bb_ * 128 + 128],
                    ident[:, :]), 'sMM')
                b.wait('scalar', 'sMM', cm_tr)
                dst = v1t[:, (2 * t + bb_) * 128:(2 * t + bb_) * 128 + 128]
                b.emit('scalar', lambda e, dst=dst: e.copy(dst, pb[2][0:128, 0:128]))
                cs_tr = b.mark('scalar', 'sSC')
                b.wait('tensor', 'sSC', cs_tr)

            for o in range(H):
                par = o % 2
                buf = wba if par == 0 else wbb
                bank = pb[par]
                cw = b.dma('sync', buf[:, :], wq1[o], 'sW%d' % par, guard='wb%d' % par)
                b.wait('tensor', 'sW%d' % par, cw)

                def _grp(e, buf=buf, bank=bank):
                    for c in range(NCH):
                        last = e.matmul(bank[0:128, 0:128], buf[:, c * 128:(c + 1) * 128],
                                        a1[:, c * TT:c * TT + 128],
                                        start=(c == 0), stop=(c == NCH - 1))
                    return last
                cmm = b.emit_inc('tensor', _grp, 'sMM')
                b.slot_mm['wb%d' % par] = ('sMM', cmm)
                b.wait('scalar', 'sMM', cmm)
                dst = a2[:, o * 128:(o + 1) * 128]
                bc = bcol('qb1', o)
                b.emit('scalar', lambda e, dst=dst, bank=bank, bc=bc:
                       e.activation(dst, bank[0:128, 0:128], AF.Identity, bias=bc, scale=1.0))
                cs = b.mark('scalar', 'sSC')
                b.wait('tensor', 'sSC', cs)

            av_free = 0
            den_free = 0
            rep_free = 0
            for h in range(H):
                nkc = 2 * t + 2
                pending = []

                def emit_av(kc, esl, gate, nkc=nkc):
                    if kc == 0:
                        b.wait('tensor', 'sSC', av_free)
                        b.wait('tensor', 'sV', den_free)
                    b.wait('tensor', gate[0], gate[1])

                    def _av(e, esl=esl, kc=kc, nkc=nkc):
                        e.matmul(pb[4][0:128, 0:128], v1t[:, kc * 128:kc * 128 + 128], esl,
                                 start=(kc == 0), stop=(kc == nkc - 1))
                        return e.matmul(pb[6][0:1, 0:128], onec[:, :], esl,
                                        start=(kc == 0), stop=(kc == nkc - 1))
                    b.emit_inc('tensor', _av, 'sMM')

                for kc in range(nkc):
                    sb = pb[2 + (kc % 2)]
                    cms = b.emit_inc('tensor', lambda e, sb=sb, kc=kc, h=h:
                                     e.matmul(sb[0:128, 0:128], k1b[:, kc * 128:kc * 128 + 128],
                                              a2[:, h * 128:(h + 1) * 128],
                                              start=True, stop=True), 'sMM')
                    b.wait('scalar', 'sMM', cms)
                    esl = attns[:, (kc % 2) * TT:(kc % 2) * TT + 128]
                    b.emit('scalar', lambda e, sb=sb, esl=esl:
                           e.activation(esl, sb[0:128, 0:128], AF.Exp, bias=0.0, scale=SCALE))
                    css = b.mark('scalar', 'sSC')
                    if kc >= nkc - 2:
                        b.wait('vector', 'sSC', css)
                        if kc == nkc - 2:
                            def _msk(e, esl=esl):
                                e.tensor_tensor(esl, esl, diag[:, :], ALU.mult)
                                e.drain()
                        else:
                            def _msk(e, esl=esl):
                                e.tensor_scalar(esl, esl, MLCOL, None, ALU.mult)
                                e.drain()
                        b.emit('vector', _msk)
                        cvm = b.mark('vector', 'sV')
                        gate = ('sV', cvm)
                    else:
                        gate = ('sSC', css)
                    pending.append((kc, esl, gate))
                    if len(pending) > 1:
                        emit_av(*pending.pop(0))
                emit_av(*pending.pop(0))
                b.wait('vector', 'sMM', b.cnt['sMM'])

                def _r(e):
                    e.reciprocal(row(R_R1, 128), pb[6][0:1, 0:128])
                    e.drain()
                b.emit('vector', _r)
                cv_r = b.mark('vector', 'sV')
                den_free = cv_r
                b.wait('tensor', 'sV', cv_r)
                b.wait('tensor', 'sV', rep_free)
                cm_rb = b.emit_inc('tensor', lambda e: e.matmul(pb[7][0:128, 0:128], oner[:, :],
                                                                row(R_R1, 128),
                                                                start=True, stop=True), 'sMM')
                b.wait('scalar', 'sMM', cm_rb)
                b.emit('scalar', lambda e: e.copy(tmpb[:, 0:128], pb[4][0:128, 0:128]))
                cs_av = b.mark('scalar', 'sSC')
                av_free = cs_av
                b.wait('vector', 'sSC', cs_av)

                def _nrm(e, h=h):
                    e.tensor_tensor(a1[:, h * 128:(h + 1) * 128], tmpb[:, 0:128],
                                    pb[7][0:128, 0:128], ALU.mult)
                    e.drain()
                b.emit('vector', _nrm)
                cv_n = b.mark('vector', 'sV')
                rep_free = cv_n

            proj(wo1, a1, 128, 128, lambda o: a2[:, o * 128:(o + 1) * 128],
                 dep_v=b.cnt['sV'], bias_name='ob1', stats=True)
            cvr = rows_from_stats(128)
            cbc2 = bcast_rows(128, cvr)
            residual_update(a2, 128, TT, cbc2)

            in_stats(xt, 128, TT, a1)
            cvr = rows_from_stats(128)
            cbc = bcast_rows(128, cvr)
            cln = ln_into(a1, xt, 128, TT, cbc)
            b.wait('vector', 'sV', cln)

            def _split(e):
                e.tensor_copy(lnpair[:, 0:W_], a1[:, 0:W_])
                e.drain()
                e.tensor_tensor(lnpair[:, W_:2 * W_], a1[:, 0:W_], lnpair[:, 0:W_], ALU.subtract)
                e.drain()
            b.emit('vector', _split)
            cv_sp = b.mark('vector', 'sV')
            b.wait('tensor', 'sV', cv_sp)

            tern_sc_done = {0: 0, 1: 0}
            b.wait('scalar', 'sV', cv_sp)
            for o in range(NCH):
                bank = pb[o % 2]
                if t == 0:
                    par = o % 2
                    buf = wba if par == 0 else wbb
                    b.wait('sync', 'sSC', tern_sc_done[par])
                    cw = b.dma('sync', buf[:, :], wf1[o], 'sW%d' % par, guard='wb%d' % par)
                    b.wait('scalar', 'sW%d' % par, cw)

                    def _tern_s(e, buf=buf):
                        e.activation(a1[:, 0:W_], buf[:, :], AF.Abs, bias=0.0, scale=1.0)
                        e.activation(a1[:, W_:2 * W_], buf[:, :], AF.Sign, bias=0.0, scale=1.0)
                        e.drain()
                    b.emit('scalar', _tern_s)
                    cs_t = b.mark('scalar', 'sSC')
                    tern_sc_done[par] = cs_t
                    b.slot_mm['wb%d' % par] = ('sSC', cs_t)
                    b.wait('vector', 'sSC', cs_t)
                    b.wait('vector', 'sTS', b.cnt['sTS'])

                    def _tern_v(e):
                        e.tensor_single_scalar(a1[:, 0:W_], a1[:, 0:W_], 0.1, ALU.is_ge)
                        e.drain()
                        e.tensor_tensor(ternb[:, :], a1[:, W_:2 * W_], a1[:, 0:W_], ALU.mult)
                        e.drain()
                    b.emit('vector', _tern_v)
                    cv_t = b.mark('vector', 'sV')
                    b.wait('gpsimd', 'sV', cv_t)
                    b.dma('gpsimd', tern_sc[o], ternb[:, :], 'sTS')
                    b.wait('tensor', 'sV', cv_t)
                else:
                    if o == 0:
                        b.wait('sync', 'sTS', 32 * 16)
                    cw = b.dma('sync', ternb[:, :], tern_sc[o], 'sW0',
                               guard=(['tern', 'wb0'] if o == 0 else 'tern'))
                    b.wait('tensor', 'sW0', cw)

                def _grp(e, bank=bank):
                    for c in range(NCH):
                        e.matmul(bank[0:128, 0:128], ternb[:, c * 128:(c + 1) * 128],
                                 lnpair[:, c * 128:c * 128 + 128], start=(c == 0), stop=False)
                    for c in range(NCH):
                        last = e.matmul(bank[0:128, 0:128], ternb[:, c * 128:(c + 1) * 128],
                                        lnpair[:, W_ + c * 128:W_ + c * 128 + 128],
                                        start=False, stop=(c == NCH - 1))
                    return last
                cmm = b.emit_inc('tensor', _grp, 'sMM')
                b.slot_mm['tern'] = ('sMM', cmm)
                b.wait('vector', 'sMM', cmm)
                dst = a2[:, o * 128:(o + 1) * 128]

                def _ev(e, dst=dst, bank=bank, o=o):
                    e.tensor_scalar(dst, bank[0:128, 0:128], bcol('fb1', o), bcol('fs1', o),
                                    ALU.add, ALU.mult)
                    e.drain()
                b.emit('vector', _ev)
                cvv = b.mark('vector', 'sV')
                b.wait('scalar', 'sV', cvv)
                b.emit('scalar', lambda e, dst=dst: e.activation(tmpb[:, 256:384], dst,
                                                                 AF.Square, bias=0.0, scale=1.0))
                cs2 = b.mark('scalar', 'sSC')
                b.wait('tensor', 'sSC', cs2)

                def _stat(e, dst=dst, o=o):
                    e.matmul(pb[6][0:1, 0:128], onec[:, :], dst, start=(o == 0), stop=(o == NCH - 1))
                    return e.matmul(pb[7][0:1, 0:128], onec[:, :], tmpb[:, 256:384],
                                    start=(o == 0), stop=(o == NCH - 1))
                b.emit_inc('tensor', _stat, 'sMM')

            b.slot_mm['wb0'] = b.slot_mm['tern']
            cvr = rows_from_stats(128)
            cbc3 = bcast_rows(128, cvr)
            b.wait('vector', 'sMM', cbc3)

            def _lnaff(e):
                for c in range(NCH):
                    yc = a2[:, c * 128:(c + 1) * 128]
                    e.tensor_tensor(yc, yc, pb[0][0:128, 0:128], ALU.mult)
                    e.drain()
                    e.tensor_tensor(yc, yc, pb[1][0:128, 0:128], ALU.subtract)
                    e.drain()
                    e.tensor_scalar(yc, yc, bcol('fg1', c), bcol('fbt1', c), ALU.mult, ALU.add)
                    e.drain()
            b.emit('vector', _lnaff)
            cv_la = b.mark('vector', 'sV')
            b.wait('scalar', 'sV', cv_la)
            b.emit('scalar', lambda e: e.activation(a1[:, 0:W_], a2[:, 0:W_], AF.Tanh,
                                                    bias=0.0, scale=QSCOL))
            cs_th = b.mark('scalar', 'sSC')
            b.wait('vector', 'sSC', cs_th)

            def _round2(e):
                e.tensor_scalar(a2[:, 0:W_], a1[:, 0:W_], 127.0, 128.5, ALU.mult, ALU.add)
                e.drain()
                e.tensor_copy(a1[:, 0:W_], a2[:, 0:W_])
                e.drain()
                for k in (128.0, 64.0, 32.0, 16.0, 8.0, 4.0, 2.0, 1.0):
                    e.tensor_scalar(a1[:, W_:2 * W_], a1[:, 0:W_], k, k, ALU.is_ge, ALU.mult)
                    e.drain()
                    e.tensor_tensor(a1[:, 0:W_], a1[:, 0:W_], a1[:, W_:2 * W_], ALU.subtract)
                    e.drain()
                e.tensor_tensor(a2[:, 0:W_], a2[:, 0:W_], a1[:, 0:W_], ALU.subtract)
                e.drain()
                e.tensor_scalar(a2[:, 0:W_], a2[:, 0:W_], -128.0, 1.0 / 127.0, ALU.add, ALU.mult)
                e.drain()
            b.emit('vector', _round2)
            cv_rd = b.mark('vector', 'sV')

            b.wait('scalar', 'sV', cv_rd)
            b.emit('scalar', lambda e: e.activation(a1[:, 0:W_], a2[:, 0:W_], AF.Square,
                                                    bias=0.0, scale=1.0))
            cs3 = b.mark('scalar', 'sSC')
            b.wait('tensor', 'sSC', cs3)

            def _st3(e):
                for c4 in range(NCH // 4):
                    e.matmul(pb[6][0:1, 0:512], onec[:, :], a2[:, c4 * 512:(c4 + 1) * 512],
                             start=(c4 == 0), stop=(c4 == NCH // 4 - 1))
                    last = e.matmul(pb[7][0:1, 0:512], onec[:, :], a1[:, c4 * 512:(c4 + 1) * 512],
                                    start=(c4 == 0), stop=(c4 == NCH // 4 - 1))
                return last
            b.emit_inc('tensor', _st3, 'sMM')
            cvr = rows_from_stats(128, fold=4)
            cbc4 = bcast_rows(128, cvr)
            residual_update(a2, 128, TT, cbc4)

            b.wait('vector', 'sV', b.cnt['sV'])

            def _stage(e):
                e.tensor_copy(a2[:, 0:W_].rearrange('p (c w) -> p c w', c=NCH),
                              xt[:].rearrange('p (c w) -> p c w', c=NCH)[:, :, 0:128])
                e.drain()
            b.emit('vector', _stage)
            cv_stage_prev = b.mark('vector', 'sV')
            b.wait('gpsimd', 'sV', cv_stage_prev)
            cx_out_prev = b.dma('gpsimd', y_out[t],
                                a2[:, 0:W_].rearrange('p (c w) -> p c w', c=NCH), 'sOUT')
            b.wait('vector', 'sOUT', cx_out_prev)

        b.wait('gpsimd', 'sOUT', b.cnt['sOUT'])
        for s in ('sTS', 'sCS', 'sXT', 'sREP'):
            if b.cnt[s]:
                b.wait('gpsimd', s, b.cnt[s])

        with nc.Block() as block:
            @block.sync
            def _(e):
                for f in b.q['sync']:
                    f(e)

            @block.tensor
            def _(e):
                for f in b.q['tensor']:
                    f(e)

            @block.vector
            def _(e):
                for f in b.q['vector']:
                    f(e)

            @block.scalar
            def _(e):
                for f in b.q['scalar']:
                    f(e)

            @block.gpsimd
            def _(e):
                for f in b.q['gpsimd']:
                    f(e)

    return nc



def _rope_cache():
    base = 10000.0
    inv_freq = 1.0 / base ** (np.arange(0, DH, 2, dtype=np.float32) / DH)
    t = np.arange(S, dtype=np.float32)
    f = np.outer(t, inv_freq)
    cos = np.zeros((S, DH), dtype=np.float32)
    sin = np.zeros((S, DH), dtype=np.float32)
    cos[:, 0::2] = np.cos(f)
    cos[:, 1::2] = np.cos(f)
    sin[:, 0::2] = np.sin(f)
    sin[:, 1::2] = np.sin(f)
    return cos, sin


def _pack_w(wt):
    return np.ascontiguousarray(
        wt.reshape(NCH, 128, NCH, 128).transpose(2, 1, 0, 3).reshape(NCH, 128, W_))


def _pack_w_small(wt, outw):
    return np.ascontiguousarray(
        wt.reshape(NCH, 128, outw).transpose(1, 0, 2).reshape(128, NCH * outw))


def kernel(**inputs):
    ins = {k: np.asarray(v) for k, v in inputs.items()}
    x0 = ins['L0_attn_x']

    cos, sin = _rope_cache()

    sh = {}
    sh['wq0'] = _pack_w(np.ascontiguousarray(ins['L0_attn_Wq'].T))
    sh['wo0'] = _pack_w(np.ascontiguousarray(ins['L0_attn_Wo'].T))
    sh['wb0'] = _pack_w(np.ascontiguousarray(ins['L0_ffn_W_base'].T))
    sh['wq1'] = _pack_w(np.ascontiguousarray(ins['L1_attn_wq_w'].T))
    sh['wo1'] = _pack_w(np.ascontiguousarray(ins['L1_attn_wo_w'].T))
    sh['wf1'] = _pack_w(np.ascontiguousarray(ins['L1_ffn_weight'].T))
    sh['wk0'] = _pack_w_small(np.ascontiguousarray(ins['L0_attn_Wk'].T), 128)
    sh['wv0'] = _pack_w_small(np.ascontiguousarray(ins['L0_attn_Wv'].T), 128)
    sh['wk1'] = _pack_w_small(np.ascontiguousarray(ins['L1_attn_wk_w'].T), 128)
    sh['wv1'] = _pack_w_small(np.ascontiguousarray(ins['L1_attn_wv_w'].T), 128)
    sh['wrt'] = _pack_w_small(np.ascontiguousarray(ins['L0_ffn_W_router'].T), 8)
    sh['atw'] = _pack_w_small(np.ascontiguousarray(ins['L0_ffn_A'].reshape(128, 4096).T), 128)
    bt = np.ascontiguousarray(ins['L0_ffn_B'].transpose(0, 2, 1).reshape(128, 4096))
    sh['btw'] = np.ascontiguousarray(bt.reshape(128, NCH, 128).transpose(1, 0, 2))
    for nm, key in (('bb0', 'L0_ffn_b_base'), ('qb1', 'L1_attn_wq_b'), ('ob1', 'L1_attn_wo_b'),
                    ('fb1', 'L1_ffn_bias'), ('fs1', 'L1_ffn_scale'),
                    ('fg1', 'L1_ffn_ln_gamma'), ('fbt1', 'L1_ffn_ln_beta')):
        sh[nm] = np.ascontiguousarray(ins[key].reshape(NCH, 128).T)
    sh['kb1'] = np.ascontiguousarray(ins['L1_attn_wk_b'].reshape(128, 1))
    sh['vb1'] = np.ascontiguousarray(ins['L1_attn_wv_b'].reshape(128, 1))
    sh['qs1'] = np.ascontiguousarray(ins['L1_ffn_quant_scale'].reshape(1, 1))
    kq = np.arange(128)
    sh['diagi'] = (kq[:, None] <= kq[None, :]).astype(np.float32)
    sh['identi'] = np.eye(128, dtype=np.float32)
    rotm = np.zeros((128, 128), np.float32)
    for d in range(64):
        rotm[d, 64 + d] = -1.0
        rotm[64 + d, d] = 1.0
    sh['rotmi'] = np.ascontiguousarray(rotm.T)
    rep16 = np.zeros((8, 128), np.float32)
    for e_ in range(8):
        rep16[e_, e_ * 16:(e_ + 1) * 16] = 2.0
    sh['rep16i'] = rep16

    in_maps = []
    for core in range(8):
        seq, role = core // 2, core % 2
        perm = np.arange(S)
        if role == 1:
            perm = np.ascontiguousarray(perm.reshape(-1, 2, 128)[:, ::-1, :]).reshape(-1)
        xl = x0[seq][perm]
        m = dict(sh)
        m['x0p'] = np.ascontiguousarray(xl.T.reshape(NCH, 128, S))
        m['cosp'] = np.ascontiguousarray(cos[perm].T)
        m['sinp'] = np.ascontiguousarray(sin[perm].T)
        if role == 0:
            m['rampi'] = np.arange(TT, dtype=np.float32).reshape(1, TT)
        else:
            m['rampi'] = np.concatenate([np.arange(128, 256), np.arange(0, 128)]
                                        ).astype(np.float32).reshape(1, TT)
        m['mlasti'] = np.full((128, 1), float(role), np.float32)
        in_maps.append(m)

    nc = build_nc()
    res = run_bass_kernel_spmd(nc, in_maps, core_ids=list(range(8)))

    out = np.empty((4, S, D), np.float32)
    for core in range(8):
        seq, role = core // 2, core % 2
        yo = res.results[core]['y_out']
        for t in range(NT):
            g = 2 * t + role
            out[seq, g * 128:(g + 1) * 128, :] = yo[t].transpose(2, 1, 0).reshape(128, D)
    return out


# version 04
# speedup vs baseline: 1.1815x; 1.0342x over previous
import math
import numpy as np
from contextlib import ExitStack

import concourse.bass as bass
import concourse.mybir as mybir
from concourse.bass_utils import run_bass_kernel_spmd

F32 = mybir.dt.float32
BF16 = mybir.dt.bfloat16
ALU = mybir.AluOpType
AF = mybir.ActivationFunctionType
AX = mybir.AxisListType

D = 4096
NCH = 32
S = 2048
TT = 256
NT = 8
H = 32
DH = 128
EPS = 1e-5
SCALE = 1.0 / math.sqrt(128.0)
NEG_BIG = -1e30
W_ = NCH * 128

CFG_NT = NT


class Bld:
    ENGS = ('sync', 'tensor', 'vector', 'scalar', 'gpsimd')

    def __init__(self, nc, es):
        self.nc = nc
        self.q = {e: [] for e in self.ENGS}
        self.h = {}
        self.cnt = {}
        for name in ('sW0', 'sW1', 'sP0', 'sP1', 'sB0', 'sB1', 'sRT',
                     'sC', 'sXT', 'sCS', 'sREP', 'sTS', 'sOUT',
                     'sMM', 'sV', 'sSC'):
            self.h[name] = es.enter_context(nc.semaphore(name))
            self.cnt[name] = 0
        self.slot_mm = {}

    def emit(self, eng, fn):
        self.q[eng].append(fn)

    def wait(self, eng, sname, val):
        assert val <= self.cnt[sname], (sname, val, self.cnt[sname])
        if val <= 0:
            return
        h = self.h[sname]
        self.q[eng].append(lambda e, h=h, v=val: e.wait_ge(h, v))

    def dma(self, eng, out, in_, sname, guard=None):
        if guard is not None:
            for g in ([guard] if isinstance(guard, str) else guard):
                gs, gv = self.slot_mm.get(g, ('sMM', 0))
                self.wait(eng, gs, gv)
        h = self.h[sname]
        self.cnt[sname] += 16
        self.q[eng].append(lambda e, o=out, i=in_, h=h: e.dma_start(out=o, in_=i).then_inc(h, 16))
        return self.cnt[sname]

    def mark(self, eng, sname):
        h = self.h[sname]
        self.cnt[sname] += 1
        self.q[eng].append(lambda e, h=h: e.drain().then_inc(h, 1))
        return self.cnt[sname]

    def inc(self, eng, sname):
        h = self.h[sname]
        self.cnt[sname] += 1
        self.q[eng].append(lambda e, h=h: e.drain().then_inc(h, 1))
        return self.cnt[sname]

    def emit_inc(self, eng, fn, sname):
        h = self.h[sname]
        self.cnt[sname] += 1

        def g(e, fn=fn, h=h):
            inst = fn(e)
            inst.then_inc(h, 1)
        self.q[eng].append(g)
        return self.cnt[sname]


def build_nc():
    nc = bass.Bass('TRN2', target_bir_lowering=False, debug=False)
    n_t = CFG_NT

    def din(name, shape):
        return nc.dram_tensor(name, list(shape), F32, kind='ExternalInput').ap()

    x0p = din('x0p', (NCH, 128, S))
    cosp = din('cosp', (128, S))
    sinp = din('sinp', (128, S))
    rampi = din('rampi', (1, TT))
    mlasti = din('mlasti', (128, 1))
    diagi = din('diagi', (128, 128))
    identi = din('identi', (128, 128))
    rotmi = din('rotmi', (128, 128))
    rep16i = din('rep16i', (8, 128))
    wq0 = din('wq0', (NCH, 128, W_))
    wo0 = din('wo0', (NCH, 128, W_))
    wb0 = din('wb0', (NCH, 128, W_))
    wq1 = din('wq1', (NCH, 128, W_))
    wo1 = din('wo1', (NCH, 128, W_))
    wf1 = din('wf1', (NCH, 128, W_))
    wk0 = din('wk0', (128, W_))
    wv0 = din('wv0', (128, W_))
    wk1 = din('wk1', (128, W_))
    wv1 = din('wv1', (128, W_))
    wrt = din('wrt', (128, NCH * 8))
    atw = din('atw', (128, W_))
    btw = din('btw', (NCH, 128, 128))
    bb0 = din('bb0', (128, NCH))
    qb1 = din('qb1', (128, NCH))
    ob1 = din('ob1', (128, NCH))
    fb1 = din('fb1', (128, NCH))
    fs1 = din('fs1', (128, NCH))
    fg1 = din('fg1', (128, NCH))
    fbt1 = din('fbt1', (128, NCH))
    kb1 = din('kb1', (128, 1))
    vb1 = din('vb1', (128, 1))
    qs1 = din('qs1', (1, 1))
    y_out = nc.dram_tensor('y_out', [NT, 128, NCH, 128], F32, kind='ExternalOutput').ap()
    tern_sc = nc.dram_tensor('tern_sc', [NCH, 128, W_], BF16, kind='Internal').ap()

    es = ExitStack()
    with es:
        SB = lambda n, shp, dt=F32: es.enter_context(nc.sbuf_tensor(n, list(shp), dt))

        xt = SB('xt', (128, NCH * TT))
        a1 = SB('a1', (128, NCH * TT))
        a2 = SB('a2', (128, NCH * TT))
        wba = SB('wba', (128, W_))
        wbb = SB('wbb', (128, W_))
        k0r = SB('k0r', (128, S))
        v0t = SB('v0t', (128, S))
        k1b = SB('k1b', (128, S))
        v1t = SB('v1t', (128, S))
        costl = SB('costl', (128, TT))
        sintl = SB('sintl', (128, TT))
        attns = SB('attns', (128, 512))
        crep = SB('crep', (128, 512))
        tmpb = SB('tmpb', (128, 512))
        btb = SB('btb', (128, 256))
        gbuf = SB('gbuf', (128, TT))
        wrep = SB('wrep', (128, TT))
        rows = SB('rows', (1, 2048))
        ternb = SB('ternb', (128, W_), BF16)
        lnpair = SB('lnpair', (128, 2 * W_), BF16)
        biases = SB('biases', (128, 8 * NCH))
        smallc = SB('smallc', (128, 8))
        diag = SB('diag', (128, 128))
        ident = SB('ident', (128, 128))
        rotm = SB('rotm', (128, 128))
        rep16 = SB('rep16', (8, 128))
        onec = SB('onec', (128, 1))
        oner = SB('oner', (1, 128))
        ltt = SB('ltt', (128, 16))
        rsc = SB('rsc', (128, 64))

        wrtb = btb
        wts = tmpb[0:8, 256:512]

        pb = [es.enter_context(nc.psum_tensor('pb%d' % i, [128, 512], F32)) for i in range(8)]

        b = Bld(nc, es)

        def row(off, w):
            return rows[0:1, off:off + w]

        R_AL, R_AL1, R_M, R_V, R_TMP = 0, 256, 512, 1024, 1536
        R_R1, R_R2, R_C1, R_C2 = 512, 768, 1024, 1280

        BIAS = {'bb0': 0, 'qb1': NCH, 'ob1': 2 * NCH, 'fb1': 3 * NCH,
                'fs1': 4 * NCH, 'fg1': 5 * NCH, 'fbt1': 6 * NCH}

        def bcol(name, o):
            return biases[:, BIAS[name] + o: BIAS[name] + o + 1]

        KCOL = smallc[:, 0:1]
        VCOL = smallc[:, 1:2]
        QSCOL = smallc[:, 2:3]
        MLCOL = smallc[:, 3:4]

        for dst, src in ((diag[:, :], diagi), (ident[:, :], identi), (rotm[:, :], rotmi),
                         (rep16[:, :], rep16i),
                         (biases[:, 0:NCH], bb0), (biases[:, NCH:2 * NCH], qb1),
                         (biases[:, 2 * NCH:3 * NCH], ob1), (biases[:, 3 * NCH:4 * NCH], fb1),
                         (biases[:, 4 * NCH:5 * NCH], fs1), (biases[:, 5 * NCH:6 * NCH], fg1),
                         (biases[:, 6 * NCH:7 * NCH], fbt1),
                         (smallc[:, 0:1], kb1), (smallc[:, 1:2], vb1), (smallc[:, 3:4], mlasti),
                         (rows[0:1, 1792:1792 + TT], rampi), (rows[0:1, R_M:R_M + 1], qs1)):
            b.dma('gpsimd', dst, src, 'sC')
        c_load = b.cnt['sC']

        b.wait('vector', 'sC', c_load)

        def _init_v(e):
            e.memset(onec[:, :], 1.0)
            e.memset(oner[:, :], 1.0)
            e.drain()
            e.tensor_scalar(row(R_AL, TT), row(1792, TT), 1.0 / 255.0, None, ALU.mult)
            e.tensor_scalar(row(R_AL1, TT), row(1792, TT), -1.0 / 255.0, 1.0, ALU.mult, ALU.add)
            e.drain()
            e.reciprocal(row(R_M + 1, 1), row(R_M, 1))
            e.drain()
        b.emit('vector', _init_v)
        cv = b.mark('vector', 'sV')
        b.wait('tensor', 'sV', cv)
        cm = b.emit_inc('tensor', lambda e: e.matmul(pb[0][0:128, 0:1], oner[:, :],
                                                     row(R_M + 1, 1), start=True, stop=True), 'sMM')
        b.wait('scalar', 'sMM', cm)
        b.emit('scalar', lambda e: e.copy(smallc[:, 2:3], pb[0][0:128, 0:1]))
        cs = b.mark('scalar', 'sSC')
        b.wait('tensor', 'sSC', cs)

        def rows_from_stats(ntok, fold=1):
            b.wait('vector', 'sMM', b.cnt['sMM'])

            def _rows(e):
                if fold == 1:
                    e.tensor_scalar(row(R_M, ntok), pb[6][0:1, 0:ntok], 1.0 / 4096.0, None, ALU.mult)
                    e.tensor_scalar(row(R_V, ntok), pb[7][0:1, 0:ntok], 1.0 / 4096.0, None, ALU.mult)
                    e.drain()
                else:
                    for bank_i, dst in ((6, R_M), (7, R_V)):
                        e.tensor_copy(row(dst, ntok), pb[bank_i][0:1, 0:ntok])
                        e.drain()
                        for f in range(1, fold):
                            e.tensor_tensor(row(dst, ntok), row(dst, ntok),
                                            pb[bank_i][0:1, f * ntok:(f + 1) * ntok], ALU.add)
                            e.drain()
                        e.tensor_scalar(row(dst, ntok), row(dst, ntok), 1.0 / 4096.0, None, ALU.mult)
                        e.drain()
                e.tensor_tensor(row(R_TMP, ntok), row(R_M, ntok), row(R_M, ntok), ALU.mult)
                e.drain()
                e.tensor_tensor(row(R_V, ntok), row(R_V, ntok), row(R_TMP, ntok), ALU.subtract)
                e.drain()
                e.tensor_scalar(row(R_V, ntok), row(R_V, ntok), EPS, None, ALU.add)
                e.drain()
            b.emit('vector', _rows)
            cv = b.mark('vector', 'sV')
            b.wait('scalar', 'sV', cv)
            b.emit('scalar', lambda e: e.activation(row(R_TMP, ntok), row(R_V, ntok),
                                                    AF.Sqrt, bias=0.0, scale=1.0))
            cs = b.mark('scalar', 'sSC')
            b.wait('vector', 'sSC', cs)

            def _rr(e):
                e.reciprocal(row(R_V, ntok), row(R_TMP, ntok))
                e.drain()
                e.tensor_tensor(row(R_TMP, ntok), row(R_M, ntok), row(R_V, ntok), ALU.mult)
                e.drain()
            b.emit('vector', _rr)
            return b.mark('vector', 'sV')

        def bcast_rows(ntok, dep_v):
            b.wait('tensor', 'sV', dep_v)

            def _bc(e):
                e.matmul(pb[0][0:128, 0:ntok], oner[:, :], row(R_V, ntok), start=True, stop=True)
                return e.matmul(pb[1][0:128, 0:ntok], oner[:, :], row(R_TMP, ntok), start=True, stop=True)
            return b.emit_inc('tensor', _bc, 'sMM')

        def in_stats(src_sb, ntok, chunk_w, sq_buf):
            b.wait('scalar', 'sV', b.cnt['sV'])
            if ntok == chunk_w:
                b.emit('scalar', lambda e: e.activation(sq_buf[:, 0:NCH * ntok],
                                                        src_sb[:, 0:NCH * ntok], AF.Square,
                                                        bias=0.0, scale=1.0))
            else:
                def _sq(e):
                    for c in range(NCH):
                        e.activation(sq_buf[:, c * ntok:(c + 1) * ntok],
                                     src_sb[:, c * chunk_w:c * chunk_w + ntok],
                                     AF.Square, bias=0.0, scale=1.0)
                b.emit('scalar', _sq)
            csq = b.mark('scalar', 'sSC')
            b.wait('tensor', 'sSC', csq)
            if ntok == chunk_w:
                npair = NCH // 2

                def _sums(e):
                    for c2 in range(npair):
                        e.matmul(pb[6][0:1, 0:2 * ntok], onec[:, :],
                                 src_sb[:, c2 * 2 * ntok:(c2 + 1) * 2 * ntok],
                                 start=(c2 == 0), stop=(c2 == npair - 1))
                        last = e.matmul(pb[7][0:1, 0:2 * ntok], onec[:, :],
                                        sq_buf[:, c2 * 2 * ntok:(c2 + 1) * 2 * ntok],
                                        start=(c2 == 0), stop=(c2 == npair - 1))
                    return last
                b.emit_inc('tensor', _sums, 'sMM')
                return 2

            def _sums(e):
                for c in range(NCH):
                    e.matmul(pb[6][0:1, 0:ntok], onec[:, :],
                             src_sb[:, c * chunk_w:c * chunk_w + ntok],
                             start=(c == 0), stop=(c == NCH - 1))
                    last = e.matmul(pb[7][0:1, 0:ntok], onec[:, :],
                                    sq_buf[:, c * ntok:(c + 1) * ntok],
                                    start=(c == 0), stop=(c == NCH - 1))
                return last
            b.emit_inc('tensor', _sums, 'sMM')
            return 1

        def ln_into(dst, src_sb, ntok, chunk_w, cmm_bcast):
            b.wait('vector', 'sMM', cmm_bcast)

            def _ln(e):
                for c in range(NCH):
                    s = src_sb[:, c * chunk_w:c * chunk_w + ntok]
                    d = dst[:, c * ntok:(c + 1) * ntok]
                    e.tensor_tensor(d, s, pb[0][0:128, 0:ntok], ALU.mult)
                    e.drain()
                    e.tensor_tensor(d, d, pb[1][0:128, 0:ntok], ALU.subtract)
                    e.drain()
            b.emit('vector', _ln)
            return b.mark('vector', 'sV')

        def residual_update(y_sb, ntok, chunk_w, cmm_bcast):
            b.wait('vector', 'sMM', cmm_bcast)

            def _res(e):
                for c in range(NCH):
                    yc = y_sb[:, c * ntok:(c + 1) * ntok]
                    t = tmpb[:, 0:ntok]
                    e.tensor_tensor(t, yc, pb[0][0:128, 0:ntok], ALU.mult)
                    e.drain()
                    e.tensor_tensor(t, t, pb[1][0:128, 0:ntok], ALU.subtract)
                    e.drain()
                    dc = xt[:, c * chunk_w:c * chunk_w + ntok]
                    e.tensor_tensor(dc, dc, t, ALU.add)
                    e.drain()
            b.emit('vector', _res)
            return b.mark('vector', 'sV')

        def proj(w_dram, rhs_sb, ntok, chunk_w, out_cb, dep_v, bias_name=None,
                 lora_dep=None, stats=False):
            pend = []
            csc_hist = {}

            def emit_stat(e_o, e_dst):
                def _stat(e, dst=e_dst, o=e_o):
                    e.matmul(pb[6][0:1, 0:ntok], onec[:, :], dst,
                             start=(o == 0), stop=(o == NCH - 1))
                    return e.matmul(pb[7][0:1, 0:ntok], onec[:, :],
                                    tmpb[:, (e_o % 2) * ntok:(e_o % 2) * ntok + ntok],
                                    start=(o == 0), stop=(o == NCH - 1))
                b.emit_inc('tensor', _stat, 'sMM')

            for o in range(NCH):
                par = o % 2
                buf = wba if par == 0 else wbb
                bank = pb[par]
                cw = b.dma('sync', buf[:, :], w_dram[o], 'sW%d' % par, guard='wb%d' % par)
                b.wait('tensor', 'sW%d' % par, cw)
                if lora_dep is not None:
                    cbt = b.dma('sync', btb[:, par * 128:par * 128 + 128], btw[o], 'sB%d' % par,
                                guard='bt%d' % par)
                    b.wait('tensor', 'sB%d' % par, cbt)
                if o == 0 and dep_v:
                    b.wait('tensor', 'sV', dep_v)
                if o == 0 and lora_dep:
                    b.wait('tensor', 'sV', lora_dep)
                if o >= 2:
                    b.wait('tensor', 'sSC', csc_hist[o - 2])

                def _grp(e, buf=buf, bank=bank, par=par):
                    for c in range(NCH):
                        last = e.matmul(bank[0:128, 0:ntok], buf[:, c * 128:(c + 1) * 128],
                                        rhs_sb[:, c * chunk_w:c * chunk_w + ntok],
                                        start=(c == 0), stop=(c == NCH - 1 and lora_dep is None))
                    if lora_dep is not None:
                        last = e.matmul(bank[0:128, 0:ntok], btb[:, par * 128:par * 128 + 128],
                                        gbuf[:, 0:ntok], start=False, stop=True)
                    return last
                cmm = b.emit_inc('tensor', _grp, 'sMM')
                b.slot_mm['wb%d' % par] = ('sMM', cmm)
                b.slot_mm['bt%d' % par] = ('sMM', cmm)
                if stats and pend:
                    po, pdst, pcsc = pend.pop(0)
                    b.wait('tensor', 'sSC', pcsc)
                    emit_stat(po, pdst)
                b.wait('scalar', 'sMM', cmm)
                dst = out_cb(o)
                if bias_name is None:
                    b.emit('scalar', lambda e, dst=dst, bank=bank: e.copy(dst, bank[0:128, 0:ntok]))
                else:
                    bc = bcol(bias_name, o)
                    b.emit('scalar', lambda e, dst=dst, bank=bank, bc=bc:
                           e.activation(dst, bank[0:128, 0:ntok], AF.Identity, bias=bc, scale=1.0))
                if stats:
                    b.emit('scalar', lambda e, bank=bank, o=o:
                           e.activation(tmpb[:, (o % 2) * ntok:(o % 2) * ntok + ntok],
                                        bank[0:128, 0:ntok], AF.Square, bias=0.0, scale=1.0))
                csc = b.mark('scalar', 'sSC')
                csc_hist[o] = csc
                if stats:
                    pend.append((o, dst, csc))
            if stats:
                po, pdst, pcsc = pend.pop(0)
                b.wait('tensor', 'sSC', pcsc)
                emit_stat(po, pdst)

        cv_rope_prev = 0
        cwk0 = b.dma('sync', wba[:, :], wk0, 'sW0', guard='wb0')
        cwv0 = b.dma('sync', wbb[:, :], wv0, 'sW1', guard='wb1')
        b.wait('tensor', 'sW0', cwk0)
        b.wait('tensor', 'sW1', cwv0)
        for g in range(4):
            g0 = g * 512
            for c in range(NCH):
                slot = c % 2
                xc = a1[:, slot * 512: slot * 512 + 512]
                sq = a1[:, 1024 + slot * 512: 1024 + slot * 512 + 512]
                cx = b.dma('sync', xc, x0p[c, :, g0:g0 + 512], 'sP%d' % slot, guard='p0x%d' % slot)
                b.wait('scalar', 'sP%d' % slot, cx)
                b.emit('scalar', lambda e, xc=xc, sq=sq: e.activation(sq, xc, AF.Square,
                                                                     bias=0.0, scale=1.0))
                csq = b.mark('scalar', 'sSC')
                b.wait('tensor', 'sSC', csq)

                def _st(e, xc=xc, sq=sq, c=c):
                    e.matmul(pb[6][0:1, 0:512], onec[:, :], xc, start=(c == 0), stop=(c == NCH - 1))
                    return e.matmul(pb[7][0:1, 0:512], onec[:, :], sq,
                                    start=(c == 0), stop=(c == NCH - 1))
                b.slot_mm['p0x%d' % slot] = ('sMM', b.emit_inc('tensor', _st, 'sMM'))
            cvr = rows_from_stats(512)
            cbc = bcast_rows(512, cvr)
            for c in range(NCH):
                slot = c % 2
                xc = a1[:, slot * 512: slot * 512 + 512]
                lnc = a1[:, 1024 + slot * 512: 1024 + slot * 512 + 512]
                cx = b.dma('sync', xc, x0p[c, :, g0:g0 + 512], 'sP%d' % slot, guard='p0x%d' % slot)
                b.wait('vector', 'sP%d' % slot, cx)
                if c == 0:
                    b.wait('vector', 'sMM', cbc)

                def _ln(e, xc=xc, lnc=lnc):
                    e.tensor_tensor(lnc, xc, pb[0][0:128, 0:512], ALU.mult)
                    e.drain()
                    e.tensor_tensor(lnc, lnc, pb[1][0:128, 0:512], ALU.subtract)
                    e.drain()
                b.emit('vector', _ln)
                cvl = b.mark('vector', 'sV')
                b.wait('tensor', 'sV', cvl)

                def _kv(e, lnc=lnc, c=c):
                    e.matmul(pb[2][0:128, 0:512], wba[:, c * 128:(c + 1) * 128], lnc,
                             start=(c == 0), stop=(c == NCH - 1))
                    return e.matmul(pb[3][0:128, 0:512], wbb[:, c * 128:(c + 1) * 128], lnc,
                                    start=(c == 0), stop=(c == NCH - 1))
                b.slot_mm['p0x%d' % slot] = ('sMM', b.emit_inc('tensor', _kv, 'sMM'))
            cm_kv = b.cnt['sMM']
            b.wait('gpsimd', 'sV', cv_rope_prev)
            ccos = b.dma('gpsimd', a1[:, 2048:2560], cosp[:, g0:g0 + 512], 'sCS')
            csin = b.dma('gpsimd', a1[:, 2560:3072], sinp[:, g0:g0 + 512], 'sCS')
            b.wait('scalar', 'sMM', cm_kv)
            b.emit('scalar', lambda e: e.copy(a2[:, 0:512], pb[2][0:128, 0:512]))
            b.emit('scalar', lambda e: e.copy(a2[:, 512:1024], pb[3][0:128, 0:512]))
            cs_kv = b.mark('scalar', 'sSC')
            b.wait('tensor', 'sSC', cs_kv)
            cm_rot = b.emit_inc('tensor', lambda e: e.matmul(pb[2][0:128, 0:512], rotm[:, :],
                                                             a2[:, 0:512], start=True, stop=True), 'sMM')
            b.wait('vector', 'sMM', cm_rot)
            b.wait('vector', 'sCS', max(ccos, csin))

            def _rope(e, g0=g0):
                e.tensor_tensor(a1[:, 3072:3584], a2[:, 0:512], a1[:, 2048:2560], ALU.mult)
                e.drain()
                e.tensor_tensor(a1[:, 3584:4096], pb[2][0:128, 0:512], a1[:, 2560:3072], ALU.mult)
                e.drain()
                e.tensor_tensor(k0r[:, g0:g0 + 512], a1[:, 3072:3584], a1[:, 3584:4096], ALU.add)
                e.drain()
            b.emit('vector', _rope)
            cv_rope = b.mark('vector', 'sV')
            cv_rope_prev = cv_rope
            b.wait('tensor', 'sV', cv_rope)
            for bb_ in range(4):
                cmt = b.emit_inc('tensor', lambda e, bb_=bb_: e.transpose(
                    pb[4][0:128, 0:128], a2[:, 512 + bb_ * 128:512 + bb_ * 128 + 128],
                    ident[:, :]), 'sMM')
                b.wait('scalar', 'sMM', cmt)
                dst = v0t[:, (g * 4 + bb_) * 128:(g * 4 + bb_) * 128 + 128]
                b.emit('scalar', lambda e, dst=dst: e.copy(dst, pb[4][0:128, 0:128]))
                cst = b.mark('scalar', 'sSC')
                b.wait('tensor', 'sSC', cst)
        b.slot_mm['wb0'] = ('sMM', b.cnt['sMM'])
        b.slot_mm['wb1'] = ('sMM', b.cnt['sMM'])

        cv_stage_prev = cv_rope_prev
        cx_out_prev = 0
        for t in range(n_t):
            t0 = t * TT

            b.wait('gpsimd', 'sV', cv_stage_prev)
            cxt = b.dma('gpsimd', xt[:].rearrange('p (c w) -> p c w', c=NCH),
                        x0p[:, :, t0:t0 + TT].rearrange('c p w -> p c w'), 'sXT')
            ccos = b.dma('gpsimd', costl[:, :], cosp[:, t0:t0 + TT], 'sCS')
            csin = b.dma('gpsimd', sintl[:, :], sinp[:, t0:t0 + TT], 'sCS')

            b.wait('scalar', 'sXT', cxt)
            fold = in_stats(xt, TT, TT, a1)
            cvr = rows_from_stats(TT, fold)
            cbc = bcast_rows(TT, cvr)
            cln = ln_into(a1, xt, TT, TT, cbc)

            for o in range(H):
                par = o % 2
                buf = wba if par == 0 else wbb
                bank = pb[par]
                cw = b.dma('sync', buf[:, :], wq0[o], 'sW%d' % par, guard='wb%d' % par)
                b.wait('tensor', 'sW%d' % par, cw)
                if o == 0:
                    b.wait('tensor', 'sV', cln)

                def _grp(e, buf=buf, bank=bank):
                    for c in range(NCH):
                        last = e.matmul(bank[0:128, 0:TT], buf[:, c * 128:(c + 1) * 128],
                                        a1[:, c * TT:(c + 1) * TT],
                                        start=(c == 0), stop=(c == NCH - 1))
                    return last
                cmm = b.emit_inc('tensor', _grp, 'sMM')
                b.slot_mm['wb%d' % par] = ('sMM', cmm)
                b.wait('scalar', 'sMM', cmm)
                b.emit('scalar', lambda e, bank=bank: e.copy(tmpb[:, 0:TT], bank[0:128, 0:TT]))
                cs = b.mark('scalar', 'sSC')
                b.wait('tensor', 'sSC', cs)
                cmr = b.emit_inc('tensor', lambda e: e.matmul(pb[2][0:128, 0:TT], rotm[:, :],
                                                              tmpb[:, 0:TT], start=True, stop=True), 'sMM')
                b.wait('vector', 'sMM', cmr)
                if o == 0:
                    b.wait('vector', 'sCS', max(ccos, csin))

                def _qrope(e, o=o):
                    d = a2[:, o * TT:(o + 1) * TT]
                    e.tensor_tensor(d, tmpb[:, 0:TT], costl[:, :], ALU.mult)
                    e.drain()
                    e.tensor_tensor(tmpb[:, 256:512], pb[2][0:128, 0:TT], sintl[:, :], ALU.mult)
                    e.drain()
                    e.tensor_tensor(d, d, tmpb[:, 256:512], ALU.add)
                    e.drain()
                b.emit('vector', _qrope)
                cvq = b.mark('vector', 'sV')
                b.wait('tensor', 'sV', cvq)

            n_w = 1 if t == 0 else 2
            av_free = 0
            den_free = 0
            combine_q = []

            def flush_combine():
                while combine_q:
                    combine_q.pop(0)()
            for h in range(H):
                for wi in range(n_w):
                    wbase = (t - (n_w - 1 - wi)) * 256
                    nk = min(512, S - wbase) // 128
                    avb = pb[4] if (n_w == 2 and wi == 0) else pb[5]
                    dnb = pb[6] if (n_w == 2 and wi == 0) else pb[7]
                    css = {}
                    pending = []

                    def emit_av(kc, asl, kb, avb=avb, dnb=dnb, nk=nk):
                        if kc == 0:
                            b.wait('tensor', 'sV', av_free)
                            b.wait('tensor', 'sSC', den_free)
                        b.wait('tensor', 'sSC', css[kc])
                        def _av(e, asl=asl, kb=kb, kc=kc, nk=nk, avb=avb, dnb=dnb):
                            e.matmul(avb[0:128, 0:TT], v0t[:, kb:kb + 128], asl,
                                     start=(kc == 0), stop=(kc == nk - 1))
                            return e.matmul(dnb[0:1, 0:TT], onec[:, :], asl,
                                            start=(kc == 0), stop=(kc == nk - 1))
                        b.emit_inc('tensor', _av, 'sMM')

                    for kc in range(nk):
                        kb = wbase + kc * 128
                        sb = pb[2 + (kc % 2)]
                        cms = b.emit_inc('tensor', lambda e, sb=sb, kb=kb, h=h:
                                         e.matmul(sb[0:128, 0:TT], k0r[:, kb:kb + 128],
                                                  a2[:, h * TT:(h + 1) * TT],
                                                  start=True, stop=True), 'sMM')
                        if wi == 0 and kc == 1:
                            flush_combine()
                        b.wait('scalar', 'sMM', cms)
                        asl = attns[:, (kc % 2) * TT:(kc % 2) * TT + TT]
                        b.emit('scalar', lambda e, sb=sb, asl=asl:
                               e.activation(asl, sb[0:128, 0:TT], AF.Sigmoid, bias=0.0, scale=SCALE))
                        css[kc] = b.mark('scalar', 'sSC')
                        pending.append((kc, asl, kb))
                        if len(pending) > 1:
                            emit_av(*pending.pop(0))
                    emit_av(*pending.pop(0))
                cm_att = b.cnt['sMM']

                def _combine(h=h, cm_att=cm_att):
                    nonlocal av_free, den_free
                    b.wait('vector', 'sMM', cm_att)

                    def _cc(e, n_w=n_w):
                        if n_w == 2:
                            e.reciprocal(row(R_R1, TT), pb[6][0:1, 0:TT])
                            e.reciprocal(row(R_R2, TT), pb[7][0:1, 0:TT])
                            e.drain()
                            e.tensor_tensor(row(R_C1, TT), row(R_R1, TT), row(R_AL1, TT), ALU.mult)
                            e.tensor_tensor(row(R_C2, TT), row(R_R2, TT), row(R_AL, TT), ALU.mult)
                            e.drain()
                        else:
                            e.reciprocal(row(R_C2, TT), pb[7][0:1, 0:TT])
                            e.drain()
                    b.emit('vector', _cc)
                    cv_cc = b.mark('vector', 'sV')
                    b.wait('tensor', 'sV', cv_cc)

                    def _bc(e, n_w=n_w):
                        if n_w == 2:
                            e.matmul(pb[6][0:128, 0:TT], oner[:, :], row(R_C1, TT),
                                     start=True, stop=True)
                        return e.matmul(pb[7][0:128, 0:TT], oner[:, :], row(R_C2, TT),
                                        start=True, stop=True)
                    cm_bc = b.emit_inc('tensor', _bc, 'sMM')
                    b.wait('scalar', 'sMM', cm_bc)

                    def _crep(e, n_w=n_w):
                        if n_w == 2:
                            e.copy(crep[:, 0:TT], pb[6][0:128, 0:TT])
                        e.copy(crep[:, TT:2 * TT], pb[7][0:128, 0:TT])
                    b.emit('scalar', _crep)
                    cs_cr = b.mark('scalar', 'sSC')
                    den_free = cs_cr
                    b.wait('vector', 'sSC', cs_cr)

                    def _blend(e, h=h, n_w=n_w):
                        d = a1[:, h * TT:(h + 1) * TT]
                        if n_w == 2:
                            e.tensor_tensor(tmpb[:, 0:TT], pb[4][0:128, 0:TT], crep[:, 0:TT],
                                            ALU.mult)
                            e.drain()
                            e.tensor_tensor(d, pb[5][0:128, 0:TT], crep[:, TT:2 * TT], ALU.mult)
                            e.drain()
                            e.tensor_tensor(d, d, tmpb[:, 0:TT], ALU.add)
                            e.drain()
                        else:
                            e.tensor_tensor(d, pb[5][0:128, 0:TT], crep[:, TT:2 * TT], ALU.mult)
                            e.drain()
                    b.emit('vector', _blend)
                    cv_bl = b.mark('vector', 'sV')
                    av_free = cv_bl
                combine_q.append(_combine)
            flush_combine()

            proj(wo0, a1, TT, TT, lambda o: a2[:, o * TT:(o + 1) * TT],
                 dep_v=b.cnt['sV'], stats=True)
            cvr = rows_from_stats(TT)
            cbc2 = bcast_rows(TT, cvr)
            residual_update(a2, TT, TT, cbc2)

            fold = in_stats(xt, TT, TT, a1)
            cvr = rows_from_stats(TT, fold)
            cbc = bcast_rows(TT, cvr)
            cln = ln_into(a1, xt, TT, TT, cbc)

            cat = b.dma('sync', wba[:, :], atw, 'sW0', guard='wb0')
            crt = b.dma('sync', wrtb[:, :], wrt, 'sRT', guard=['bt0', 'bt1'])
            b.wait('tensor', 'sW0', cat)
            b.wait('tensor', 'sRT', crt)
            b.wait('tensor', 'sV', cln)

            def _hrt(e):
                for c in range(NCH):
                    e.matmul(pb[2][0:128, 0:TT], wba[:, c * 128:(c + 1) * 128],
                             a1[:, c * TT:(c + 1) * TT], start=(c == 0), stop=(c == NCH - 1))
                    last = e.matmul(pb[3][0:8, 0:TT], wrtb[:, c * 8:(c + 1) * 8],
                                    a1[:, c * TT:(c + 1) * TT], start=(c == 0), stop=(c == NCH - 1))
                return last
            cm_h = b.emit_inc('tensor', _hrt, 'sMM')
            b.slot_mm['wb0'] = ('sMM', cm_h)
            b.slot_mm['bt0'] = ('sMM', cm_h)
            b.slot_mm['bt1'] = ('sMM', cm_h)
            b.wait('scalar', 'sMM', cm_h)
            b.emit('scalar', lambda e: e.copy(tmpb[0:8, 0:TT], pb[3][0:8, 0:TT]))
            cs_l = b.mark('scalar', 'sSC')
            b.wait('tensor', 'sSC', cs_l)
            for half in range(2):
                cm_t = b.emit_inc('tensor', lambda e, half=half: e.transpose(
                    pb[3][0:128, 0:8], tmpb[0:8, half * 128:half * 128 + 128],
                    ident[0:8, 0:8]), 'sMM')
                b.wait('scalar', 'sMM', cm_t)
                b.emit('scalar', lambda e, half=half: e.copy(ltt[:, half * 8:half * 8 + 8],
                                                             pb[3][0:128, 0:8]))
                cs_t = b.mark('scalar', 'sSC')
                b.wait('tensor', 'sSC', cs_t)
            b.wait('vector', 'sSC', b.cnt['sSC'])

            def _gate_v1(e):
                for half in range(2):
                    lt = ltt[:, half * 8:half * 8 + 8]
                    m1 = rsc[:, half * 2 + 0:half * 2 + 1]
                    e.reduce_max(m1, lt, AX.X)
                    e.drain()
                    msk1 = rsc[:, 16 + half * 8:16 + half * 8 + 8]
                    e.tensor_scalar(msk1, lt, m1, None, ALU.is_ge)
                    e.drain()
                    l2 = rsc[:, 32 + half * 8:32 + half * 8 + 8]
                    e.scalar_tensor_tensor(l2, msk1, NEG_BIG, lt, ALU.mult, ALU.add)
                    e.drain()
                    m2 = rsc[:, half * 2 + 1:half * 2 + 2]
                    e.reduce_max(m2, l2, AX.X)
                    e.drain()
                    msk2 = rsc[:, 48 + half * 8:48 + half * 8 + 8]
                    e.tensor_scalar(msk2, l2, m2, None, ALU.is_ge)
                    e.drain()
                    dcol = rsc[:, 4 + half * 2:4 + half * 2 + 1]
                    e.tensor_tensor(dcol, m1, m2, ALU.subtract)
                    e.drain()
            b.emit('vector', _gate_v1)
            cv_g1 = b.mark('vector', 'sV')
            b.wait('scalar', 'sV', cv_g1)

            def _gate_s(e):
                for half in range(2):
                    dcol = rsc[:, 4 + half * 2:4 + half * 2 + 1]
                    e.activation(rsc[:, 8 + half * 2:8 + half * 2 + 1], dcol, AF.Sigmoid,
                                 bias=0.0, scale=1.0)
                    e.activation(rsc[:, 9 + half * 2:9 + half * 2 + 1], dcol, AF.Sigmoid,
                                 bias=0.0, scale=-1.0)
            b.emit('scalar', _gate_s)
            cs_g = b.mark('scalar', 'sSC')
            b.wait('vector', 'sSC', cs_g)

            def _gate_v2(e):
                for half in range(2):
                    w1 = rsc[:, 8 + half * 2:8 + half * 2 + 1]
                    w2 = rsc[:, 9 + half * 2:9 + half * 2 + 1]
                    msk1 = rsc[:, 16 + half * 8:16 + half * 8 + 8]
                    msk2 = rsc[:, 48 + half * 8:48 + half * 8 + 8]
                    t1 = rsc[:, 32 + half * 8:32 + half * 8 + 8]
                    e.tensor_scalar(t1, msk1, w1, None, ALU.mult)
                    e.tensor_scalar(msk2, msk2, w2, None, ALU.mult)
                    e.drain()
                    e.tensor_tensor(msk1, t1, msk2, ALU.add)
                    e.drain()
            b.emit('vector', _gate_v2)
            cv_g2 = b.mark('vector', 'sV')
            b.wait('tensor', 'sV', cv_g2)
            for half in range(2):
                cm_t = b.emit_inc('tensor', lambda e, half=half: e.transpose(
                    pb[3][0:8, 0:128], rsc[:, 16 + half * 8:16 + half * 8 + 8],
                    ident[:, :]), 'sMM')
                b.wait('scalar', 'sMM', cm_t)
                b.emit('scalar', lambda e, half=half: e.copy(wts[:, half * 128:half * 128 + 128],
                                                             pb[3][0:8, 0:128]))
                cs_t = b.mark('scalar', 'sSC')
                b.wait('tensor', 'sSC', cs_t)
            b.wait('tensor', 'sSC', b.cnt['sSC'])
            cm_rep = b.emit_inc('tensor', lambda e: e.matmul(pb[3][0:128, 0:TT], rep16[:, :],
                                                             wts[:, :], start=True, stop=True), 'sMM')
            b.wait('scalar', 'sMM', cm_rep)
            b.emit('scalar', lambda e: e.copy(wrep[:, :], pb[3][0:128, 0:TT]))
            cs_rep = b.mark('scalar', 'sSC')
            b.wait('vector', 'sSC', cs_rep)

            def _g(e):
                e.tensor_tensor(gbuf[:, :], pb[2][0:128, 0:TT], wrep[:, :], ALU.mult)
                e.drain()
            b.emit('vector', _g)
            cv_gg = b.mark('vector', 'sV')

            proj(wb0, a1, TT, TT, lambda o: a2[:, o * TT:(o + 1) * TT],
                 dep_v=cv_gg, bias_name='bb0', lora_dep=cv_gg, stats=True)
            cvr = rows_from_stats(TT)
            cbc2 = bcast_rows(TT, cvr)
            residual_update(a2, TT, TT, cbc2)

            fold = in_stats(xt, TT, TT, a1)
            cvr = rows_from_stats(TT, fold)
            cbc = bcast_rows(TT, cvr)
            cln = ln_into(a1, xt, TT, TT, cbc)

            cwk1 = b.dma('sync', wba[:, :], wk1, 'sW0', guard='wb0')
            cwv1 = b.dma('sync', wbb[:, :], wv1, 'sW1', guard='wb1')
            b.wait('tensor', 'sW0', cwk1)
            b.wait('tensor', 'sW1', cwv1)
            b.wait('tensor', 'sV', cln)

            def _kv1(e):
                for c in range(NCH):
                    e.matmul(pb[2][0:128, 0:TT], wba[:, c * 128:(c + 1) * 128],
                             a1[:, c * TT:(c + 1) * TT], start=(c == 0), stop=(c == NCH - 1))
                    last = e.matmul(pb[3][0:128, 0:TT], wbb[:, c * 128:(c + 1) * 128],
                                    a1[:, c * TT:(c + 1) * TT], start=(c == 0), stop=(c == NCH - 1))
                return last
            cm_kv1 = b.emit_inc('tensor', _kv1, 'sMM')
            b.slot_mm['wb0'] = ('sMM', cm_kv1)
            b.slot_mm['wb1'] = ('sMM', cm_kv1)
            b.wait('scalar', 'sMM', cm_kv1)
            b.emit('scalar', lambda e, t0=t0: e.activation(k1b[:, t0:t0 + TT], pb[2][0:128, 0:TT],
                                                           AF.Identity, bias=KCOL, scale=1.0))
            b.emit('scalar', lambda e: e.activation(tmpb[:, 0:TT], pb[3][0:128, 0:TT],
                                                    AF.Identity, bias=VCOL, scale=1.0))
            cs_kv1 = b.mark('scalar', 'sSC')
            b.wait('tensor', 'sSC', cs_kv1)
            for bb_ in range(2):
                cm_tr = b.emit_inc('tensor', lambda e, bb_=bb_: e.transpose(
                    pb[2][0:128, 0:128], tmpb[:, bb_ * 128:bb_ * 128 + 128],
                    ident[:, :]), 'sMM')
                b.wait('scalar', 'sMM', cm_tr)
                dst = v1t[:, (2 * t + bb_) * 128:(2 * t + bb_) * 128 + 128]
                b.emit('scalar', lambda e, dst=dst: e.copy(dst, pb[2][0:128, 0:128]))
                cs_tr = b.mark('scalar', 'sSC')
                b.wait('tensor', 'sSC', cs_tr)

            for o in range(H):
                par = o % 2
                buf = wba if par == 0 else wbb
                bank = pb[par]
                cw = b.dma('sync', buf[:, :], wq1[o], 'sW%d' % par, guard='wb%d' % par)
                b.wait('tensor', 'sW%d' % par, cw)

                def _grp(e, buf=buf, bank=bank):
                    for c in range(NCH):
                        last = e.matmul(bank[0:128, 0:128], buf[:, c * 128:(c + 1) * 128],
                                        a1[:, c * TT:c * TT + 128],
                                        start=(c == 0), stop=(c == NCH - 1))
                    return last
                cmm = b.emit_inc('tensor', _grp, 'sMM')
                b.slot_mm['wb%d' % par] = ('sMM', cmm)
                b.wait('scalar', 'sMM', cmm)
                dst = a2[:, o * 128:(o + 1) * 128]
                bc = bcol('qb1', o)
                b.emit('scalar', lambda e, dst=dst, bank=bank, bc=bc:
                       e.activation(dst, bank[0:128, 0:128], AF.Identity, bias=bc, scale=1.0))
                cs = b.mark('scalar', 'sSC')
                b.wait('tensor', 'sSC', cs)

            av_free = 0
            den_free = 0
            rep_free = 0
            combine_q = []

            def flush_combine():
                while combine_q:
                    combine_q.pop(0)()
            for h in range(H):
                nkc = 2 * t + 2
                pending = []

                def emit_av(kc, esl, gate, nkc=nkc):
                    if kc == 0:
                        b.wait('tensor', 'sSC', av_free)
                        b.wait('tensor', 'sV', den_free)
                    b.wait('tensor', gate[0], gate[1])

                    def _av(e, esl=esl, kc=kc, nkc=nkc):
                        e.matmul(pb[4][0:128, 0:128], v1t[:, kc * 128:kc * 128 + 128], esl,
                                 start=(kc == 0), stop=(kc == nkc - 1))
                        return e.matmul(pb[6][0:1, 0:128], onec[:, :], esl,
                                        start=(kc == 0), stop=(kc == nkc - 1))
                    b.emit_inc('tensor', _av, 'sMM')

                for kc in range(nkc):
                    sb = pb[2 + (kc % 2)]
                    cms = b.emit_inc('tensor', lambda e, sb=sb, kc=kc, h=h:
                                     e.matmul(sb[0:128, 0:128], k1b[:, kc * 128:kc * 128 + 128],
                                              a2[:, h * 128:(h + 1) * 128],
                                              start=True, stop=True), 'sMM')
                    b.wait('scalar', 'sMM', cms)
                    if kc == 1:
                        flush_combine()
                    esl = attns[:, (kc % 2) * TT:(kc % 2) * TT + 128]
                    b.emit('scalar', lambda e, sb=sb, esl=esl:
                           e.activation(esl, sb[0:128, 0:128], AF.Exp, bias=0.0, scale=SCALE))
                    css = b.mark('scalar', 'sSC')
                    if kc >= nkc - 2:
                        b.wait('vector', 'sSC', css)
                        if kc == nkc - 2:
                            def _msk(e, esl=esl):
                                e.tensor_tensor(esl, esl, diag[:, :], ALU.mult)
                                e.drain()
                        else:
                            def _msk(e, esl=esl):
                                e.tensor_scalar(esl, esl, MLCOL, None, ALU.mult)
                                e.drain()
                        b.emit('vector', _msk)
                        cvm = b.mark('vector', 'sV')
                        gate = ('sV', cvm)
                    else:
                        gate = ('sSC', css)
                    pending.append((kc, esl, gate))
                    if len(pending) > 1:
                        emit_av(*pending.pop(0))
                emit_av(*pending.pop(0))
                cm_att = b.cnt['sMM']

                def _combine(h=h, cm_att=cm_att):
                    nonlocal av_free, den_free, rep_free
                    b.wait('vector', 'sMM', cm_att)

                    def _r(e):
                        e.reciprocal(row(R_R1, 128), pb[6][0:1, 0:128])
                        e.drain()
                    b.emit('vector', _r)
                    cv_r = b.mark('vector', 'sV')
                    den_free = cv_r
                    b.wait('tensor', 'sV', cv_r)
                    b.wait('tensor', 'sV', rep_free)
                    cm_rb = b.emit_inc('tensor', lambda e: e.matmul(pb[7][0:128, 0:128],
                                                                    oner[:, :], row(R_R1, 128),
                                                                    start=True, stop=True), 'sMM')
                    b.wait('scalar', 'sMM', cm_rb)
                    b.emit('scalar', lambda e: e.copy(tmpb[:, 0:128], pb[4][0:128, 0:128]))
                    cs_av = b.mark('scalar', 'sSC')
                    av_free = cs_av
                    b.wait('vector', 'sSC', cs_av)

                    def _nrm(e, h=h):
                        e.tensor_tensor(a1[:, h * 128:(h + 1) * 128], tmpb[:, 0:128],
                                        pb[7][0:128, 0:128], ALU.mult)
                        e.drain()
                    b.emit('vector', _nrm)
                    cv_n = b.mark('vector', 'sV')
                    rep_free = cv_n
                combine_q.append(_combine)
            flush_combine()

            proj(wo1, a1, 128, 128, lambda o: a2[:, o * 128:(o + 1) * 128],
                 dep_v=b.cnt['sV'], bias_name='ob1', stats=True)
            cvr = rows_from_stats(128)
            cbc2 = bcast_rows(128, cvr)
            residual_update(a2, 128, TT, cbc2)

            in_stats(xt, 128, TT, a1)
            cvr = rows_from_stats(128)
            cbc = bcast_rows(128, cvr)
            cln = ln_into(a1, xt, 128, TT, cbc)
            b.wait('vector', 'sV', cln)

            def _split(e):
                e.tensor_copy(lnpair[:, 0:W_], a1[:, 0:W_])
                e.drain()
                e.tensor_tensor(lnpair[:, W_:2 * W_], a1[:, 0:W_], lnpair[:, 0:W_], ALU.subtract)
                e.drain()
            b.emit('vector', _split)
            cv_sp = b.mark('vector', 'sV')
            b.wait('tensor', 'sV', cv_sp)

            tern_sc_done = {0: 0, 1: 0}
            tern_cvv = {}
            tern_pend = []
            b.wait('scalar', 'sV', cv_sp)
            for o in range(NCH):
                bank = pb[o % 2]
                if t == 0:
                    par = o % 2
                    buf = wba if par == 0 else wbb
                    b.wait('sync', 'sSC', tern_sc_done[par])
                    cw = b.dma('sync', buf[:, :], wf1[o], 'sW%d' % par, guard='wb%d' % par)
                    b.wait('scalar', 'sW%d' % par, cw)

                    def _tern_s(e, buf=buf):
                        e.activation(a1[:, 0:W_], buf[:, :], AF.Abs, bias=0.0, scale=1.0)
                        e.activation(a1[:, W_:2 * W_], buf[:, :], AF.Sign, bias=0.0, scale=1.0)
                        e.drain()
                    b.emit('scalar', _tern_s)
                    cs_t = b.mark('scalar', 'sSC')
                    tern_sc_done[par] = cs_t
                    b.slot_mm['wb%d' % par] = ('sSC', cs_t)
                    b.wait('vector', 'sSC', cs_t)
                    b.wait('vector', 'sTS', b.cnt['sTS'])

                    def _tern_v(e):
                        e.tensor_single_scalar(a1[:, 0:W_], a1[:, 0:W_], 0.1, ALU.is_ge)
                        e.drain()
                        e.tensor_tensor(ternb[:, :], a1[:, W_:2 * W_], a1[:, 0:W_], ALU.mult)
                        e.drain()
                    b.emit('vector', _tern_v)
                    cv_t = b.mark('vector', 'sV')
                    b.wait('gpsimd', 'sV', cv_t)
                    b.dma('gpsimd', tern_sc[o], ternb[:, :], 'sTS')
                    b.wait('tensor', 'sV', cv_t)
                else:
                    if o == 0:
                        b.wait('sync', 'sTS', 32 * 16)
                    cw = b.dma('sync', ternb[:, :], tern_sc[o], 'sW0',
                               guard=(['tern', 'wb0'] if o == 0 else 'tern'))
                    b.wait('tensor', 'sW0', cw)

                if o >= 2:
                    b.wait('tensor', 'sV', tern_cvv[o - 2])

                def _grp(e, bank=bank):
                    for c in range(NCH):
                        e.matmul(bank[0:128, 0:128], ternb[:, c * 128:(c + 1) * 128],
                                 lnpair[:, c * 128:c * 128 + 128], start=(c == 0), stop=False)
                    for c in range(NCH):
                        last = e.matmul(bank[0:128, 0:128], ternb[:, c * 128:(c + 1) * 128],
                                        lnpair[:, W_ + c * 128:W_ + c * 128 + 128],
                                        start=False, stop=(c == NCH - 1))
                    return last
                cmm = b.emit_inc('tensor', _grp, 'sMM')
                b.slot_mm['tern'] = ('sMM', cmm)
                if tern_pend:
                    po, pdst, pcs2 = tern_pend.pop(0)
                    b.wait('tensor', 'sSC', pcs2)

                    def _stat(e, dst=pdst, o=po):
                        e.matmul(pb[6][0:1, 0:128], onec[:, :], dst,
                                 start=(o == 0), stop=(o == NCH - 1))
                        return e.matmul(pb[7][0:1, 0:128], onec[:, :],
                                        tmpb[:, 256 + (o % 2) * 128:384 + (o % 2) * 128],
                                        start=(o == 0), stop=(o == NCH - 1))
                    b.emit_inc('tensor', _stat, 'sMM')
                b.wait('vector', 'sMM', cmm)
                dst = a2[:, o * 128:(o + 1) * 128]

                def _ev(e, dst=dst, bank=bank, o=o):
                    e.tensor_scalar(dst, bank[0:128, 0:128], bcol('fb1', o), bcol('fs1', o),
                                    ALU.add, ALU.mult)
                    e.drain()
                b.emit('vector', _ev)
                cvv = b.mark('vector', 'sV')
                tern_cvv[o] = cvv
                b.wait('scalar', 'sV', cvv)
                b.emit('scalar', lambda e, dst=dst, o=o:
                       e.activation(tmpb[:, 256 + (o % 2) * 128:384 + (o % 2) * 128], dst,
                                    AF.Square, bias=0.0, scale=1.0))
                cs2 = b.mark('scalar', 'sSC')
                tern_pend.append((o, dst, cs2))

            if tern_pend:
                po, pdst, pcs2 = tern_pend.pop(0)
                b.wait('tensor', 'sSC', pcs2)

                def _stat(e, dst=pdst, o=po):
                    e.matmul(pb[6][0:1, 0:128], onec[:, :], dst,
                             start=(o == 0), stop=(o == NCH - 1))
                    return e.matmul(pb[7][0:1, 0:128], onec[:, :],
                                    tmpb[:, 256 + (o % 2) * 128:384 + (o % 2) * 128],
                                    start=(o == 0), stop=(o == NCH - 1))
                b.emit_inc('tensor', _stat, 'sMM')
            b.slot_mm['wb0'] = b.slot_mm['tern']
            cvr = rows_from_stats(128)
            cbc3 = bcast_rows(128, cvr)
            b.wait('vector', 'sMM', cbc3)

            def _lnaff(e):
                for c in range(NCH):
                    yc = a2[:, c * 128:(c + 1) * 128]
                    e.tensor_tensor(yc, yc, pb[0][0:128, 0:128], ALU.mult)
                    e.drain()
                    e.tensor_tensor(yc, yc, pb[1][0:128, 0:128], ALU.subtract)
                    e.drain()
                    e.tensor_scalar(yc, yc, bcol('fg1', c), bcol('fbt1', c), ALU.mult, ALU.add)
                    e.drain()
            b.emit('vector', _lnaff)
            cv_la = b.mark('vector', 'sV')
            b.wait('scalar', 'sV', cv_la)
            b.emit('scalar', lambda e: e.activation(a1[:, 0:W_], a2[:, 0:W_], AF.Tanh,
                                                    bias=0.0, scale=QSCOL))
            cs_th = b.mark('scalar', 'sSC')
            b.wait('vector', 'sSC', cs_th)

            def _round2(e):
                e.tensor_scalar(a2[:, 0:W_], a1[:, 0:W_], 127.0, 128.5, ALU.mult, ALU.add)
                e.drain()
                e.tensor_copy(a1[:, 0:W_], a2[:, 0:W_])
                e.drain()
                for k in (128.0, 64.0, 32.0, 16.0, 8.0, 4.0, 2.0, 1.0):
                    e.tensor_scalar(a1[:, W_:2 * W_], a1[:, 0:W_], k, k, ALU.is_ge, ALU.mult)
                    e.drain()
                    e.tensor_tensor(a1[:, 0:W_], a1[:, 0:W_], a1[:, W_:2 * W_], ALU.subtract)
                    e.drain()
                e.tensor_tensor(a2[:, 0:W_], a2[:, 0:W_], a1[:, 0:W_], ALU.subtract)
                e.drain()
                e.tensor_scalar(a2[:, 0:W_], a2[:, 0:W_], -128.0, 1.0 / 127.0, ALU.add, ALU.mult)
                e.drain()
            b.emit('vector', _round2)
            cv_rd = b.mark('vector', 'sV')

            b.wait('scalar', 'sV', cv_rd)
            b.emit('scalar', lambda e: e.activation(a1[:, 0:W_], a2[:, 0:W_], AF.Square,
                                                    bias=0.0, scale=1.0))
            cs3 = b.mark('scalar', 'sSC')
            b.wait('tensor', 'sSC', cs3)

            def _st3(e):
                for c4 in range(NCH // 4):
                    e.matmul(pb[6][0:1, 0:512], onec[:, :], a2[:, c4 * 512:(c4 + 1) * 512],
                             start=(c4 == 0), stop=(c4 == NCH // 4 - 1))
                    last = e.matmul(pb[7][0:1, 0:512], onec[:, :], a1[:, c4 * 512:(c4 + 1) * 512],
                                    start=(c4 == 0), stop=(c4 == NCH // 4 - 1))
                return last
            b.emit_inc('tensor', _st3, 'sMM')
            cvr = rows_from_stats(128, fold=4)
            cbc4 = bcast_rows(128, cvr)
            residual_update(a2, 128, TT, cbc4)

            b.wait('vector', 'sV', b.cnt['sV'])

            def _stage(e):
                e.tensor_copy(a2[:, 0:W_].rearrange('p (c w) -> p c w', c=NCH),
                              xt[:].rearrange('p (c w) -> p c w', c=NCH)[:, :, 0:128])
                e.drain()
            b.emit('vector', _stage)
            cv_stage_prev = b.mark('vector', 'sV')
            b.wait('gpsimd', 'sV', cv_stage_prev)
            cx_out_prev = b.dma('gpsimd', y_out[t],
                                a2[:, 0:W_].rearrange('p (c w) -> p c w', c=NCH), 'sOUT')
            b.wait('vector', 'sOUT', cx_out_prev)

        b.wait('gpsimd', 'sOUT', b.cnt['sOUT'])
        for s in ('sTS', 'sCS', 'sXT', 'sREP'):
            if b.cnt[s]:
                b.wait('gpsimd', s, b.cnt[s])

        with nc.Block() as block:
            @block.sync
            def _(e):
                for f in b.q['sync']:
                    f(e)

            @block.tensor
            def _(e):
                for f in b.q['tensor']:
                    f(e)

            @block.vector
            def _(e):
                for f in b.q['vector']:
                    f(e)

            @block.scalar
            def _(e):
                for f in b.q['scalar']:
                    f(e)

            @block.gpsimd
            def _(e):
                for f in b.q['gpsimd']:
                    f(e)

    return nc



def _rope_cache():
    base = 10000.0
    inv_freq = 1.0 / base ** (np.arange(0, DH, 2, dtype=np.float32) / DH)
    t = np.arange(S, dtype=np.float32)
    f = np.outer(t, inv_freq)
    cos = np.zeros((S, DH), dtype=np.float32)
    sin = np.zeros((S, DH), dtype=np.float32)
    cos[:, 0::2] = np.cos(f)
    cos[:, 1::2] = np.cos(f)
    sin[:, 0::2] = np.sin(f)
    sin[:, 1::2] = np.sin(f)
    return cos, sin


def _pack_w(wt):
    return np.ascontiguousarray(
        wt.reshape(NCH, 128, NCH, 128).transpose(2, 1, 0, 3).reshape(NCH, 128, W_))


def _pack_w_small(wt, outw):
    return np.ascontiguousarray(
        wt.reshape(NCH, 128, outw).transpose(1, 0, 2).reshape(128, NCH * outw))


def kernel(**inputs):
    ins = {k: np.asarray(v) for k, v in inputs.items()}
    x0 = ins['L0_attn_x']

    cos, sin = _rope_cache()

    sh = {}
    sh['wq0'] = _pack_w(np.ascontiguousarray(ins['L0_attn_Wq'].T))
    sh['wo0'] = _pack_w(np.ascontiguousarray(ins['L0_attn_Wo'].T))
    sh['wb0'] = _pack_w(np.ascontiguousarray(ins['L0_ffn_W_base'].T))
    sh['wq1'] = _pack_w(np.ascontiguousarray(ins['L1_attn_wq_w'].T))
    sh['wo1'] = _pack_w(np.ascontiguousarray(ins['L1_attn_wo_w'].T))
    sh['wf1'] = _pack_w(np.ascontiguousarray(ins['L1_ffn_weight'].T))
    sh['wk0'] = _pack_w_small(np.ascontiguousarray(ins['L0_attn_Wk'].T), 128)
    sh['wv0'] = _pack_w_small(np.ascontiguousarray(ins['L0_attn_Wv'].T), 128)
    sh['wk1'] = _pack_w_small(np.ascontiguousarray(ins['L1_attn_wk_w'].T), 128)
    sh['wv1'] = _pack_w_small(np.ascontiguousarray(ins['L1_attn_wv_w'].T), 128)
    sh['wrt'] = _pack_w_small(np.ascontiguousarray(ins['L0_ffn_W_router'].T), 8)
    sh['atw'] = _pack_w_small(np.ascontiguousarray(ins['L0_ffn_A'].reshape(128, 4096).T), 128)
    bt = np.ascontiguousarray(ins['L0_ffn_B'].transpose(0, 2, 1).reshape(128, 4096))
    sh['btw'] = np.ascontiguousarray(bt.reshape(128, NCH, 128).transpose(1, 0, 2))
    for nm, key in (('bb0', 'L0_ffn_b_base'), ('qb1', 'L1_attn_wq_b'), ('ob1', 'L1_attn_wo_b'),
                    ('fb1', 'L1_ffn_bias'), ('fs1', 'L1_ffn_scale'),
                    ('fg1', 'L1_ffn_ln_gamma'), ('fbt1', 'L1_ffn_ln_beta')):
        sh[nm] = np.ascontiguousarray(ins[key].reshape(NCH, 128).T)
    sh['kb1'] = np.ascontiguousarray(ins['L1_attn_wk_b'].reshape(128, 1))
    sh['vb1'] = np.ascontiguousarray(ins['L1_attn_wv_b'].reshape(128, 1))
    sh['qs1'] = np.ascontiguousarray(ins['L1_ffn_quant_scale'].reshape(1, 1))
    kq = np.arange(128)
    sh['diagi'] = (kq[:, None] <= kq[None, :]).astype(np.float32)
    sh['identi'] = np.eye(128, dtype=np.float32)
    rotm = np.zeros((128, 128), np.float32)
    for d in range(64):
        rotm[d, 64 + d] = -1.0
        rotm[64 + d, d] = 1.0
    sh['rotmi'] = np.ascontiguousarray(rotm.T)
    rep16 = np.zeros((8, 128), np.float32)
    for e_ in range(8):
        rep16[e_, e_ * 16:(e_ + 1) * 16] = 2.0
    sh['rep16i'] = rep16

    in_maps = []
    for core in range(8):
        seq, role = core // 2, core % 2
        perm = np.arange(S)
        if role == 1:
            perm = np.ascontiguousarray(perm.reshape(-1, 2, 128)[:, ::-1, :]).reshape(-1)
        xl = x0[seq][perm]
        m = dict(sh)
        m['x0p'] = np.ascontiguousarray(xl.T.reshape(NCH, 128, S))
        m['cosp'] = np.ascontiguousarray(cos[perm].T)
        m['sinp'] = np.ascontiguousarray(sin[perm].T)
        if role == 0:
            m['rampi'] = np.arange(TT, dtype=np.float32).reshape(1, TT)
        else:
            m['rampi'] = np.concatenate([np.arange(128, 256), np.arange(0, 128)]
                                        ).astype(np.float32).reshape(1, TT)
        m['mlasti'] = np.full((128, 1), float(role), np.float32)
        in_maps.append(m)

    nc = build_nc()
    res = run_bass_kernel_spmd(nc, in_maps, core_ids=list(range(8)))

    out = np.empty((4, S, D), np.float32)
    for core in range(8):
        seq, role = core // 2, core % 2
        yo = res.results[core]['y_out']
        for t in range(NT):
            g = 2 * t + role
            out[seq, g * 128:(g + 1) * 128, :] = yo[t].transpose(2, 1, 0).reshape(128, D)
    return out


# version 06
# speedup vs baseline: 1.1924x; 1.0092x over previous
import math
import numpy as np
from contextlib import ExitStack

import concourse.bass as bass
import concourse.mybir as mybir
from concourse.bass_utils import run_bass_kernel_spmd

F32 = mybir.dt.float32
BF16 = mybir.dt.bfloat16
ALU = mybir.AluOpType
AF = mybir.ActivationFunctionType
AX = mybir.AxisListType

D = 4096
NCH = 32
S = 2048
TT = 256
NT = 8
H = 32
DH = 128
EPS = 1e-5
SCALE = 1.0 / math.sqrt(128.0)
NEG_BIG = -1e30
W_ = NCH * 128

CFG_NT = NT


class Bld:
    ENGS = ('sync', 'tensor', 'vector', 'scalar', 'gpsimd')

    def __init__(self, nc, es):
        self.nc = nc
        self.q = {e: [] for e in self.ENGS}
        self.h = {}
        self.cnt = {}
        for name in ('sW0', 'sW1', 'sP0', 'sP1', 'sB0', 'sB1', 'sRT',
                     'sC', 'sXT', 'sCS', 'sREP', 'sTS', 'sOUT',
                     'sMM', 'sV', 'sSC'):
            self.h[name] = es.enter_context(nc.semaphore(name))
            self.cnt[name] = 0
        self.slot_mm = {}

    def emit(self, eng, fn):
        self.q[eng].append(fn)

    def wait(self, eng, sname, val):
        assert val <= self.cnt[sname], (sname, val, self.cnt[sname])
        if val <= 0:
            return
        h = self.h[sname]
        self.q[eng].append(lambda e, h=h, v=val: e.wait_ge(h, v))

    def dma(self, eng, out, in_, sname, guard=None):
        if guard is not None:
            for g in ([guard] if isinstance(guard, str) else guard):
                gs, gv = self.slot_mm.get(g, ('sMM', 0))
                self.wait(eng, gs, gv)
        h = self.h[sname]
        self.cnt[sname] += 16
        self.q[eng].append(lambda e, o=out, i=in_, h=h: e.dma_start(out=o, in_=i).then_inc(h, 16))
        return self.cnt[sname]

    def mark(self, eng, sname):
        h = self.h[sname]
        self.cnt[sname] += 1
        self.q[eng].append(lambda e, h=h: e.drain().then_inc(h, 1))
        return self.cnt[sname]

    def inc(self, eng, sname):
        h = self.h[sname]
        self.cnt[sname] += 1
        self.q[eng].append(lambda e, h=h: e.drain().then_inc(h, 1))
        return self.cnt[sname]

    def emit_inc(self, eng, fn, sname):
        h = self.h[sname]
        self.cnt[sname] += 1

        def g(e, fn=fn, h=h):
            inst = fn(e)
            inst.then_inc(h, 1)
        self.q[eng].append(g)
        return self.cnt[sname]


def build_nc():
    nc = bass.Bass('TRN2', target_bir_lowering=False, debug=False)
    n_t = CFG_NT

    def din(name, shape):
        return nc.dram_tensor(name, list(shape), F32, kind='ExternalInput').ap()

    x0p = din('x0p', (NCH, 128, S))
    cosp = din('cosp', (128, S))
    sinp = din('sinp', (128, S))
    rampi = din('rampi', (1, TT))
    mlasti = din('mlasti', (128, 1))
    diagi = din('diagi', (128, 128))
    identi = din('identi', (128, 128))
    rotmi = din('rotmi', (128, 128))
    rep16i = din('rep16i', (8, 128))
    wq0 = din('wq0', (NCH, 128, W_))
    wo0 = din('wo0', (NCH, 128, W_))
    wb0 = din('wb0', (NCH, 128, W_))
    wq1 = din('wq1', (NCH, 128, W_))
    wo1 = din('wo1', (NCH, 128, W_))
    wf1 = din('wf1', (NCH, 128, W_))
    wk0 = din('wk0', (128, W_))
    wv0 = din('wv0', (128, W_))
    wk1 = din('wk1', (128, W_))
    wv1 = din('wv1', (128, W_))
    wrt = din('wrt', (128, NCH * 8))
    atw = din('atw', (128, W_))
    btw = din('btw', (NCH, 128, 128))
    bb0 = din('bb0', (128, NCH))
    qb1 = din('qb1', (128, NCH))
    ob1 = din('ob1', (128, NCH))
    fb1 = din('fb1', (128, NCH))
    fs1 = din('fs1', (128, NCH))
    fg1 = din('fg1', (128, NCH))
    fbt1 = din('fbt1', (128, NCH))
    kb1 = din('kb1', (128, 1))
    vb1 = din('vb1', (128, 1))
    qs1 = din('qs1', (1, 1))
    y_out = nc.dram_tensor('y_out', [NT, 128, NCH, 128], F32, kind='ExternalOutput').ap()
    tern_sc = nc.dram_tensor('tern_sc', [NCH, 128, W_], BF16, kind='Internal').ap()

    es = ExitStack()
    with es:
        SB = lambda n, shp, dt=F32: es.enter_context(nc.sbuf_tensor(n, list(shp), dt))

        xt = SB('xt', (128, NCH * TT))
        a1 = SB('a1', (128, NCH * TT))
        a2 = SB('a2', (128, NCH * TT))
        wba = SB('wba', (128, W_))
        wbb = SB('wbb', (128, W_))
        k0r = SB('k0r', (128, S))
        v0t = SB('v0t', (128, S))
        k1b = SB('k1b', (128, S))
        v1t = SB('v1t', (128, S))
        costl = SB('costl', (128, TT))
        sintl = SB('sintl', (128, TT))
        attns = SB('attns', (128, 1024))
        crep = SB('crep', (128, 512))
        tmpb = SB('tmpb', (128, 512))
        btb = SB('btb', (128, 256))
        rows = SB('rows', (1, 2048))
        ternb = SB('ternb', (128, W_), BF16)
        lnpair = SB('lnpair', (128, 2 * W_), BF16)
        biases = SB('biases', (128, 8 * NCH))
        smallc = SB('smallc', (128, 8))
        diag = SB('diag', (128, 128))
        ident = SB('ident', (128, 128))
        rotm = SB('rotm', (128, 128))
        rep16 = SB('rep16', (8, 128))
        onec = SB('onec', (128, 1))
        oner = SB('oner', (1, 128))
        ltt = SB('ltt', (128, 16))
        rsc = SB('rsc', (128, 64))

        wrtb = btb
        gbuf = crep[:, 0:TT]
        wrep = crep[:, TT:2 * TT]
        wts = tmpb[0:8, 256:512]

        pb = [es.enter_context(nc.psum_tensor('pb%d' % i, [128, 512], F32)) for i in range(8)]

        b = Bld(nc, es)

        def row(off, w):
            return rows[0:1, off:off + w]

        R_AL, R_AL1, R_M, R_V, R_TMP = 0, 256, 512, 1024, 1536
        R_R1, R_R2, R_C1, R_C2 = 512, 768, 1024, 1280

        BIAS = {'bb0': 0, 'qb1': NCH, 'ob1': 2 * NCH, 'fb1': 3 * NCH,
                'fs1': 4 * NCH, 'fg1': 5 * NCH, 'fbt1': 6 * NCH}

        def bcol(name, o):
            return biases[:, BIAS[name] + o: BIAS[name] + o + 1]

        KCOL = smallc[:, 0:1]
        VCOL = smallc[:, 1:2]
        QSCOL = smallc[:, 2:3]
        MLCOL = smallc[:, 3:4]

        for dst, src in ((diag[:, :], diagi), (ident[:, :], identi), (rotm[:, :], rotmi),
                         (rep16[:, :], rep16i),
                         (biases[:, 0:NCH], bb0), (biases[:, NCH:2 * NCH], qb1),
                         (biases[:, 2 * NCH:3 * NCH], ob1), (biases[:, 3 * NCH:4 * NCH], fb1),
                         (biases[:, 4 * NCH:5 * NCH], fs1), (biases[:, 5 * NCH:6 * NCH], fg1),
                         (biases[:, 6 * NCH:7 * NCH], fbt1),
                         (smallc[:, 0:1], kb1), (smallc[:, 1:2], vb1), (smallc[:, 3:4], mlasti),
                         (rows[0:1, 1792:1792 + TT], rampi), (rows[0:1, R_M:R_M + 1], qs1)):
            b.dma('gpsimd', dst, src, 'sC')
        c_load = b.cnt['sC']

        b.wait('vector', 'sC', c_load)

        def _init_v(e):
            e.memset(onec[:, :], 1.0)
            e.memset(oner[:, :], 1.0)
            e.drain()
            e.tensor_scalar(row(R_AL, TT), row(1792, TT), 1.0 / 255.0, None, ALU.mult)
            e.tensor_scalar(row(R_AL1, TT), row(1792, TT), -1.0 / 255.0, 1.0, ALU.mult, ALU.add)
            e.drain()
            e.reciprocal(row(R_M + 1, 1), row(R_M, 1))
            e.drain()
        b.emit('vector', _init_v)
        cv = b.mark('vector', 'sV')
        b.wait('tensor', 'sV', cv)
        cm = b.emit_inc('tensor', lambda e: e.matmul(pb[0][0:128, 0:1], oner[:, :],
                                                     row(R_M + 1, 1), start=True, stop=True), 'sMM')
        b.wait('scalar', 'sMM', cm)
        b.emit('scalar', lambda e: e.copy(smallc[:, 2:3], pb[0][0:128, 0:1]))
        cs = b.mark('scalar', 'sSC')
        b.wait('tensor', 'sSC', cs)

        def rows_from_stats(ntok, fold=1):
            b.wait('vector', 'sMM', b.cnt['sMM'])

            def _rows(e):
                if fold == 1:
                    e.tensor_scalar(row(R_M, ntok), pb[6][0:1, 0:ntok], 1.0 / 4096.0, None, ALU.mult)
                    e.tensor_scalar(row(R_V, ntok), pb[7][0:1, 0:ntok], 1.0 / 4096.0, None, ALU.mult)
                    e.drain()
                else:
                    for bank_i, dst in ((6, R_M), (7, R_V)):
                        e.tensor_copy(row(dst, ntok), pb[bank_i][0:1, 0:ntok])
                        e.drain()
                        for f in range(1, fold):
                            e.tensor_tensor(row(dst, ntok), row(dst, ntok),
                                            pb[bank_i][0:1, f * ntok:(f + 1) * ntok], ALU.add)
                            e.drain()
                        e.tensor_scalar(row(dst, ntok), row(dst, ntok), 1.0 / 4096.0, None, ALU.mult)
                        e.drain()
                e.tensor_tensor(row(R_TMP, ntok), row(R_M, ntok), row(R_M, ntok), ALU.mult)
                e.drain()
                e.tensor_tensor(row(R_V, ntok), row(R_V, ntok), row(R_TMP, ntok), ALU.subtract)
                e.drain()
                e.tensor_scalar(row(R_V, ntok), row(R_V, ntok), EPS, None, ALU.add)
                e.drain()
            b.emit('vector', _rows)
            cv = b.mark('vector', 'sV')
            b.wait('scalar', 'sV', cv)
            b.emit('scalar', lambda e: e.activation(row(R_TMP, ntok), row(R_V, ntok),
                                                    AF.Sqrt, bias=0.0, scale=1.0))
            cs = b.mark('scalar', 'sSC')
            b.wait('vector', 'sSC', cs)

            def _rr(e):
                e.reciprocal(row(R_V, ntok), row(R_TMP, ntok))
                e.drain()
                e.tensor_tensor(row(R_TMP, ntok), row(R_M, ntok), row(R_V, ntok), ALU.mult)
                e.drain()
            b.emit('vector', _rr)
            return b.mark('vector', 'sV')

        def bcast_rows(ntok, dep_v, banks=(2, 3)):
            b.wait('tensor', 'sV', dep_v)

            def _bc(e, banks=banks):
                e.matmul(pb[banks[0]][0:128, 0:ntok], oner[:, :], row(R_V, ntok),
                         start=True, stop=True)
                return e.matmul(pb[banks[1]][0:128, 0:ntok], oner[:, :], row(R_TMP, ntok),
                                start=True, stop=True)
            return b.emit_inc('tensor', _bc, 'sMM')

        def in_stats(src_sb, ntok, chunk_w, sq_buf):
            b.wait('scalar', 'sV', b.cnt['sV'])
            if ntok == chunk_w:
                b.emit('scalar', lambda e: e.activation(sq_buf[:, 0:NCH * ntok],
                                                        src_sb[:, 0:NCH * ntok], AF.Square,
                                                        bias=0.0, scale=1.0))
            else:
                def _sq(e):
                    for c in range(NCH):
                        e.activation(sq_buf[:, c * ntok:(c + 1) * ntok],
                                     src_sb[:, c * chunk_w:c * chunk_w + ntok],
                                     AF.Square, bias=0.0, scale=1.0)
                b.emit('scalar', _sq)
            csq = b.mark('scalar', 'sSC')
            b.wait('tensor', 'sSC', csq)
            if ntok == chunk_w:
                npair = NCH // 2

                def _sums(e):
                    for c2 in range(npair):
                        e.matmul(pb[6][0:1, 0:2 * ntok], onec[:, :],
                                 src_sb[:, c2 * 2 * ntok:(c2 + 1) * 2 * ntok],
                                 start=(c2 == 0), stop=(c2 == npair - 1))
                        last = e.matmul(pb[7][0:1, 0:2 * ntok], onec[:, :],
                                        sq_buf[:, c2 * 2 * ntok:(c2 + 1) * 2 * ntok],
                                        start=(c2 == 0), stop=(c2 == npair - 1))
                    return last
                b.emit_inc('tensor', _sums, 'sMM')
                return 2

            def _sums(e):
                for c in range(NCH):
                    e.matmul(pb[6][0:1, 0:ntok], onec[:, :],
                             src_sb[:, c * chunk_w:c * chunk_w + ntok],
                             start=(c == 0), stop=(c == NCH - 1))
                    last = e.matmul(pb[7][0:1, 0:ntok], onec[:, :],
                                    sq_buf[:, c * ntok:(c + 1) * ntok],
                                    start=(c == 0), stop=(c == NCH - 1))
                return last
            b.emit_inc('tensor', _sums, 'sMM')
            return 1

        def ln_into(dst, src_sb, ntok, chunk_w, cmm_bcast):
            b.wait('vector', 'sMM', cmm_bcast)
            marks = []
            hv = b.h['sV']
            for c in range(NCH):
                b.cnt['sV'] += 1
                marks.append(b.cnt['sV'])

                def _ln(e, c=c, v=b.cnt['sV']):
                    s = src_sb[:, c * chunk_w:c * chunk_w + ntok]
                    d = dst[:, c * ntok:(c + 1) * ntok]
                    e.tensor_tensor(d, s, pb[2][0:128, 0:ntok], ALU.mult)
                    e.drain()
                    e.tensor_tensor(d, d, pb[3][0:128, 0:ntok], ALU.subtract)
                    e.drain().then_inc(hv, 1)
                b.emit('vector', _ln)
            return marks[-1], marks

        def residual_update(y_sb, ntok, chunk_w, cmm_bcast):
            b.wait('vector', 'sMM', cmm_bcast)

            def _res(e):
                for c in range(NCH):
                    yc = y_sb[:, c * ntok:(c + 1) * ntok]
                    t = tmpb[:, 0:ntok]
                    e.tensor_tensor(t, yc, pb[2][0:128, 0:ntok], ALU.mult)
                    e.drain()
                    e.tensor_tensor(t, t, pb[3][0:128, 0:ntok], ALU.subtract)
                    e.drain()
                    dc = xt[:, c * chunk_w:c * chunk_w + ntok]
                    e.tensor_tensor(dc, dc, t, ALU.add)
                    e.drain()
            b.emit('vector', _res)
            return b.mark('vector', 'sV')

        def proj(w_dram, rhs_sb, ntok, chunk_w, out_cb, dep_v, bias_name=None,
                 lora_dep=None, stats=False, dep_chunks=None):
            pend = []
            csc_hist = {}

            def emit_stat(e_o, e_dst):
                def _stat(e, dst=e_dst, o=e_o):
                    e.matmul(pb[6][0:1, 0:ntok], onec[:, :], dst,
                             start=(o == 0), stop=(o == NCH - 1))
                    return e.matmul(pb[7][0:1, 0:ntok], onec[:, :],
                                    tmpb[:, (e_o % 2) * ntok:(e_o % 2) * ntok + ntok],
                                    start=(o == 0), stop=(o == NCH - 1))
                b.emit_inc('tensor', _stat, 'sMM')

            for o in range(NCH):
                par = o % 2
                buf = wba if par == 0 else wbb
                bank = pb[par]
                cw = b.dma('sync', buf[:, :], w_dram[o], 'sW%d' % par, guard='wb%d' % par)
                b.wait('tensor', 'sW%d' % par, cw)
                if lora_dep is not None:
                    cbt = b.dma('sync', btb[:, par * 128:par * 128 + 128], btw[o], 'sB%d' % par,
                                guard='bt%d' % par)
                    b.wait('tensor', 'sB%d' % par, cbt)
                if o == 0 and dep_v and dep_chunks is None:
                    b.wait('tensor', 'sV', dep_v)
                if o == 0 and lora_dep:
                    b.wait('tensor', 'sV', lora_dep)
                if o >= 2:
                    b.wait('tensor', 'sSC', csc_hist[o - 2])

                def _grp(e, buf=buf, bank=bank, par=par, first=(o == 0)):
                    for c in range(NCH):
                        if first and dep_chunks is not None:
                            e.wait_ge(b.h['sV'], dep_chunks[c])
                        last = e.matmul(bank[0:128, 0:ntok], buf[:, c * 128:(c + 1) * 128],
                                        rhs_sb[:, c * chunk_w:c * chunk_w + ntok],
                                        start=(c == 0), stop=(c == NCH - 1 and lora_dep is None))
                    if lora_dep is not None:
                        last = e.matmul(bank[0:128, 0:ntok], btb[:, par * 128:par * 128 + 128],
                                        gbuf[:, 0:ntok], start=False, stop=True)
                    return last
                cmm = b.emit_inc('tensor', _grp, 'sMM')
                b.slot_mm['wb%d' % par] = ('sMM', cmm)
                b.slot_mm['bt%d' % par] = ('sMM', cmm)
                if stats and pend:
                    po, pdst, pcsc = pend.pop(0)
                    b.wait('tensor', 'sSC', pcsc)
                    emit_stat(po, pdst)
                b.wait('scalar', 'sMM', cmm)
                dst = out_cb(o)
                if bias_name is None:
                    b.emit('scalar', lambda e, dst=dst, bank=bank: e.copy(dst, bank[0:128, 0:ntok]))
                else:
                    bc = bcol(bias_name, o)
                    b.emit('scalar', lambda e, dst=dst, bank=bank, bc=bc:
                           e.activation(dst, bank[0:128, 0:ntok], AF.Identity, bias=bc, scale=1.0))
                if stats:
                    b.emit('scalar', lambda e, bank=bank, o=o:
                           e.activation(tmpb[:, (o % 2) * ntok:(o % 2) * ntok + ntok],
                                        bank[0:128, 0:ntok], AF.Square, bias=0.0, scale=1.0))
                csc = b.mark('scalar', 'sSC')
                csc_hist[o] = csc
                if stats:
                    pend.append((o, dst, csc))
            if stats:
                po, pdst, pcsc = pend.pop(0)
                b.wait('tensor', 'sSC', pcsc)
                emit_stat(po, pdst)

        cv_rope_prev = 0
        cwk0 = b.dma('sync', wba[:, :], wk0, 'sW0', guard='wb0')
        cwv0 = b.dma('sync', wbb[:, :], wv0, 'sW1', guard='wb1')
        b.wait('tensor', 'sW0', cwk0)
        b.wait('tensor', 'sW1', cwv0)
        for g in range(4):
            g0 = g * 512
            for c in range(NCH):
                slot = c % 2
                xc = a1[:, slot * 512: slot * 512 + 512]
                sq = a1[:, 1024 + slot * 512: 1024 + slot * 512 + 512]
                cx = b.dma('sync', xc, x0p[c, :, g0:g0 + 512], 'sP%d' % slot, guard='p0x%d' % slot)
                b.wait('scalar', 'sP%d' % slot, cx)
                b.emit('scalar', lambda e, xc=xc, sq=sq: e.activation(sq, xc, AF.Square,
                                                                     bias=0.0, scale=1.0))
                csq = b.mark('scalar', 'sSC')
                b.wait('tensor', 'sSC', csq)

                def _st(e, xc=xc, sq=sq, c=c):
                    e.matmul(pb[6][0:1, 0:512], onec[:, :], xc, start=(c == 0), stop=(c == NCH - 1))
                    return e.matmul(pb[7][0:1, 0:512], onec[:, :], sq,
                                    start=(c == 0), stop=(c == NCH - 1))
                b.slot_mm['p0x%d' % slot] = ('sMM', b.emit_inc('tensor', _st, 'sMM'))
            cvr = rows_from_stats(512)
            cbc = bcast_rows(512, cvr, banks=(0, 1))
            for c in range(NCH):
                slot = c % 2
                xc = a1[:, slot * 512: slot * 512 + 512]
                lnc = a1[:, 1024 + slot * 512: 1024 + slot * 512 + 512]
                cx = b.dma('sync', xc, x0p[c, :, g0:g0 + 512], 'sP%d' % slot, guard='p0x%d' % slot)
                b.wait('vector', 'sP%d' % slot, cx)
                if c == 0:
                    b.wait('vector', 'sMM', cbc)

                def _ln(e, xc=xc, lnc=lnc):
                    e.tensor_tensor(lnc, xc, pb[0][0:128, 0:512], ALU.mult)
                    e.drain()
                    e.tensor_tensor(lnc, lnc, pb[1][0:128, 0:512], ALU.subtract)
                    e.drain()
                b.emit('vector', _ln)
                cvl = b.mark('vector', 'sV')
                b.wait('tensor', 'sV', cvl)

                def _kv(e, lnc=lnc, c=c):
                    e.matmul(pb[2][0:128, 0:512], wba[:, c * 128:(c + 1) * 128], lnc,
                             start=(c == 0), stop=(c == NCH - 1))
                    return e.matmul(pb[3][0:128, 0:512], wbb[:, c * 128:(c + 1) * 128], lnc,
                                    start=(c == 0), stop=(c == NCH - 1))
                b.slot_mm['p0x%d' % slot] = ('sMM', b.emit_inc('tensor', _kv, 'sMM'))
            cm_kv = b.cnt['sMM']
            b.wait('gpsimd', 'sV', cv_rope_prev)
            ccos = b.dma('gpsimd', a1[:, 2048:2560], cosp[:, g0:g0 + 512], 'sCS')
            csin = b.dma('gpsimd', a1[:, 2560:3072], sinp[:, g0:g0 + 512], 'sCS')
            b.wait('scalar', 'sMM', cm_kv)
            b.emit('scalar', lambda e: e.copy(a2[:, 0:512], pb[2][0:128, 0:512]))
            b.emit('scalar', lambda e: e.copy(a2[:, 512:1024], pb[3][0:128, 0:512]))
            cs_kv = b.mark('scalar', 'sSC')
            b.wait('tensor', 'sSC', cs_kv)
            cm_rot = b.emit_inc('tensor', lambda e: e.matmul(pb[2][0:128, 0:512], rotm[:, :],
                                                             a2[:, 0:512], start=True, stop=True), 'sMM')
            b.wait('vector', 'sMM', cm_rot)
            b.wait('vector', 'sCS', max(ccos, csin))

            def _rope(e, g0=g0):
                e.tensor_tensor(a1[:, 3072:3584], a2[:, 0:512], a1[:, 2048:2560], ALU.mult)
                e.drain()
                e.tensor_tensor(a1[:, 3584:4096], pb[2][0:128, 0:512], a1[:, 2560:3072], ALU.mult)
                e.drain()
                e.tensor_tensor(k0r[:, g0:g0 + 512], a1[:, 3072:3584], a1[:, 3584:4096], ALU.add)
                e.drain()
            b.emit('vector', _rope)
            cv_rope = b.mark('vector', 'sV')
            cv_rope_prev = cv_rope
            b.wait('tensor', 'sV', cv_rope)
            for bb_ in range(4):
                cmt = b.emit_inc('tensor', lambda e, bb_=bb_: e.transpose(
                    pb[4][0:128, 0:128], a2[:, 512 + bb_ * 128:512 + bb_ * 128 + 128],
                    ident[:, :]), 'sMM')
                b.wait('scalar', 'sMM', cmt)
                dst = v0t[:, (g * 4 + bb_) * 128:(g * 4 + bb_) * 128 + 128]
                b.emit('scalar', lambda e, dst=dst: e.copy(dst, pb[4][0:128, 0:128]))
                cst = b.mark('scalar', 'sSC')
                b.wait('tensor', 'sSC', cst)
        b.slot_mm['wb0'] = ('sMM', b.cnt['sMM'])
        b.slot_mm['wb1'] = ('sMM', b.cnt['sMM'])

        cv_stage_prev = cv_rope_prev
        cx_out_prev = 0
        for t in range(n_t):
            t0 = t * TT

            b.wait('gpsimd', 'sV', cv_stage_prev)
            cxt = b.dma('gpsimd', xt[:].rearrange('p (c w) -> p c w', c=NCH),
                        x0p[:, :, t0:t0 + TT].rearrange('c p w -> p c w'), 'sXT')
            ccos = b.dma('gpsimd', costl[:, :], cosp[:, t0:t0 + TT], 'sCS')
            csin = b.dma('gpsimd', sintl[:, :], sinp[:, t0:t0 + TT], 'sCS')

            b.wait('scalar', 'sXT', cxt)
            fold = in_stats(xt, TT, TT, a1)
            cvr = rows_from_stats(TT, fold)
            cbc = bcast_rows(TT, cvr)
            cln, cln_marks = ln_into(a1, xt, TT, TT, cbc)

            for o in range(H):
                par = o % 2
                buf = wba if par == 0 else wbb
                bank = pb[par]
                cw = b.dma('sync', buf[:, :], wq0[o], 'sW%d' % par, guard='wb%d' % par)
                b.wait('tensor', 'sW%d' % par, cw)

                def _grp(e, buf=buf, bank=bank, first=(o == 0), marks=cln_marks):
                    for c in range(NCH):
                        if first:
                            e.wait_ge(b.h['sV'], marks[c])
                        last = e.matmul(bank[0:128, 0:TT], buf[:, c * 128:(c + 1) * 128],
                                        a1[:, c * TT:(c + 1) * TT],
                                        start=(c == 0), stop=(c == NCH - 1))
                    return last
                cmm = b.emit_inc('tensor', _grp, 'sMM')
                b.slot_mm['wb%d' % par] = ('sMM', cmm)
                b.wait('scalar', 'sMM', cmm)
                b.emit('scalar', lambda e, bank=bank: e.copy(tmpb[:, 0:TT], bank[0:128, 0:TT]))
                cs = b.mark('scalar', 'sSC')
                b.wait('tensor', 'sSC', cs)
                cmr = b.emit_inc('tensor', lambda e: e.matmul(pb[4][0:128, 0:TT], rotm[:, :],
                                                              tmpb[:, 0:TT], start=True, stop=True), 'sMM')
                b.wait('vector', 'sMM', cmr)
                if o == 0:
                    b.wait('vector', 'sCS', max(ccos, csin))

                def _qrope(e, o=o):
                    d = a2[:, o * TT:(o + 1) * TT]
                    e.tensor_tensor(d, tmpb[:, 0:TT], costl[:, :], ALU.mult)
                    e.drain()
                    e.tensor_tensor(tmpb[:, 256:512], pb[4][0:128, 0:TT], sintl[:, :], ALU.mult)
                    e.drain()
                    e.tensor_tensor(d, d, tmpb[:, 256:512], ALU.add)
                    e.drain()
                b.emit('vector', _qrope)
                cvq = b.mark('vector', 'sV')
                b.wait('tensor', 'sV', cvq)
            cvq_last = cvq
            if True:
                pass

            n_w = 1 if t == 0 else 2
            av_free = cvq_last
            den_free = 0
            sig_last = {0: 0, 1: 0}
            combine_q = []

            def flush_combine():
                while combine_q:
                    combine_q.pop(0)()
            for h in range(H):
                for wi in range(n_w):
                    wbase = (t - (n_w - 1 - wi)) * 256
                    nk = min(512, S - wbase) // 128
                    npair = nk // 2
                    avb = pb[4] if (n_w == 2 and wi == 0) else pb[5]
                    dnb = pb[6] if (n_w == 2 and wi == 0) else pb[7]
                    css = {}
                    pend_av = []
                    pend_den = []

                    def emit_av(kc, avb=avb, nk=nk, wbase=wbase):
                        if kc == 0:
                            b.wait('tensor', 'sV', av_free)
                        b.wait('tensor', 'sSC', css[kc])
                        asl = attns[:, (kc % 4) * TT:(kc % 4) * TT + TT]
                        kb = wbase + kc * 128

                        def _av(e, asl=asl, kb=kb, kc=kc, nk=nk, avb=avb):
                            return e.matmul(avb[0:128, 0:TT], v0t[:, kb:kb + 128], asl,
                                            start=(kc == 0), stop=(kc == nk - 1))
                        b.emit_inc('tensor', _av, 'sMM')

                    def emit_den(p, dnb=dnb, npair=npair):
                        if p == 0:
                            b.wait('tensor', 'sSC', den_free)
                        b.wait('tensor', 'sSC', css[2 * p + 1])
                        sl = (2 * p) % 4

                        def _den(e, p=p, sl=sl, npair=npair, dnb=dnb):
                            return e.matmul(dnb[0:1, 0:512], onec[:, :],
                                            attns[:, sl * TT:sl * TT + 2 * TT],
                                            start=(p == 0), stop=(p == npair - 1))
                        b.emit_inc('tensor', _den, 'sMM')

                    for kc in range(nk):
                        kb = wbase + kc * 128
                        sb = pb[2 + (kc % 2)]
                        b.wait('tensor', 'sSC', sig_last[kc % 2])
                        cms = b.emit_inc('tensor', lambda e, sb=sb, kb=kb, h=h:
                                         e.matmul(sb[0:128, 0:TT], k0r[:, kb:kb + 128],
                                                  a2[:, h * TT:(h + 1) * TT],
                                                  start=True, stop=True), 'sMM')
                        if wi == 0 and kc == min(2, nk - 1):
                            flush_combine()
                        b.wait('scalar', 'sMM', cms)
                        asl = attns[:, (kc % 4) * TT:(kc % 4) * TT + TT]
                        b.emit('scalar', lambda e, sb=sb, asl=asl:
                               e.activation(asl, sb[0:128, 0:TT], AF.Sigmoid, bias=0.0, scale=SCALE))
                        css[kc] = b.mark('scalar', 'sSC')
                        sig_last[kc % 2] = css[kc]
                        pend_av.append(kc)
                        if len(pend_av) > 2:
                            emit_av(pend_av.pop(0))
                        if kc % 2 == 1:
                            pend_den.append(kc // 2)
                            if len(pend_den) > 1:
                                emit_den(pend_den.pop(0))
                    while pend_av:
                        emit_av(pend_av.pop(0))
                    while pend_den:
                        emit_den(pend_den.pop(0))
                cm_att = b.cnt['sMM']

                def _combine(h=h, cm_att=cm_att):
                    nonlocal av_free, den_free
                    b.wait('vector', 'sMM', cm_att)

                    def _cc(e, n_w=n_w):
                        if n_w == 2:
                            e.tensor_copy(row(R_R1, TT), pb[6][0:1, 0:TT])
                            e.tensor_copy(row(R_R2, TT), pb[7][0:1, 0:TT])
                            e.drain()
                            e.tensor_tensor(row(R_R1, TT), row(R_R1, TT), pb[6][0:1, TT:2 * TT], ALU.add)
                            e.tensor_tensor(row(R_R2, TT), row(R_R2, TT), pb[7][0:1, TT:2 * TT], ALU.add)
                            e.drain()
                            e.reciprocal(row(R_R1, TT), row(R_R1, TT))
                            e.reciprocal(row(R_R2, TT), row(R_R2, TT))
                            e.drain()
                            e.tensor_tensor(row(R_C1, TT), row(R_R1, TT), row(R_AL1, TT), ALU.mult)
                            e.tensor_tensor(row(R_C2, TT), row(R_R2, TT), row(R_AL, TT), ALU.mult)
                            e.drain()
                        else:
                            e.tensor_copy(row(R_R2, TT), pb[7][0:1, 0:TT])
                            e.drain()
                            e.tensor_tensor(row(R_R2, TT), row(R_R2, TT), pb[7][0:1, TT:2 * TT], ALU.add)
                            e.drain()
                            e.reciprocal(row(R_C2, TT), row(R_R2, TT))
                            e.drain()
                    b.emit('vector', _cc)
                    cv_cc = b.mark('vector', 'sV')
                    b.wait('tensor', 'sV', cv_cc)

                    def _bc(e, n_w=n_w):
                        if n_w == 2:
                            e.matmul(pb[6][0:128, 0:TT], oner[:, :], row(R_C1, TT),
                                     start=True, stop=True)
                        return e.matmul(pb[7][0:128, 0:TT], oner[:, :], row(R_C2, TT),
                                        start=True, stop=True)
                    cm_bc = b.emit_inc('tensor', _bc, 'sMM')
                    b.wait('scalar', 'sMM', cm_bc)

                    def _crep(e, n_w=n_w):
                        if n_w == 2:
                            e.copy(crep[:, 0:TT], pb[6][0:128, 0:TT])
                        e.copy(crep[:, TT:2 * TT], pb[7][0:128, 0:TT])
                    b.emit('scalar', _crep)
                    cs_cr = b.mark('scalar', 'sSC')
                    den_free = cs_cr
                    b.wait('vector', 'sSC', cs_cr)

                    def _blend(e, h=h, n_w=n_w):
                        d = a1[:, h * TT:(h + 1) * TT]
                        if n_w == 2:
                            e.tensor_tensor(tmpb[:, 0:TT], pb[4][0:128, 0:TT], crep[:, 0:TT],
                                            ALU.mult)
                            e.drain()
                            e.tensor_tensor(d, pb[5][0:128, 0:TT], crep[:, TT:2 * TT], ALU.mult)
                            e.drain()
                            e.tensor_tensor(d, d, tmpb[:, 0:TT], ALU.add)
                            e.drain()
                        else:
                            e.tensor_tensor(d, pb[5][0:128, 0:TT], crep[:, TT:2 * TT], ALU.mult)
                            e.drain()
                    b.emit('vector', _blend)
                    cv_bl = b.mark('vector', 'sV')
                    av_free = cv_bl
                combine_q.append(_combine)
            flush_combine()

            proj(wo0, a1, TT, TT, lambda o: a2[:, o * TT:(o + 1) * TT],
                 dep_v=b.cnt['sV'], stats=True)
            cvr = rows_from_stats(TT)
            cbc2 = bcast_rows(TT, cvr)
            residual_update(a2, TT, TT, cbc2)

            fold = in_stats(xt, TT, TT, a1)
            cvr = rows_from_stats(TT, fold)
            cbc = bcast_rows(TT, cvr)
            cln, cln_marks = ln_into(a1, xt, TT, TT, cbc)

            cat = b.dma('sync', wba[:, :], atw, 'sW0', guard='wb0')
            crt = b.dma('sync', wrtb[:, :], wrt, 'sRT', guard=['bt0', 'bt1'])
            b.wait('tensor', 'sW0', cat)
            b.wait('tensor', 'sRT', crt)

            def _hrt(e, marks=cln_marks):
                for c in range(NCH):
                    e.wait_ge(b.h['sV'], marks[c])
                    e.matmul(pb[4][0:128, 0:TT], wba[:, c * 128:(c + 1) * 128],
                             a1[:, c * TT:(c + 1) * TT], start=(c == 0), stop=(c == NCH - 1))
                    last = e.matmul(pb[5][0:8, 0:TT], wrtb[:, c * 8:(c + 1) * 8],
                                    a1[:, c * TT:(c + 1) * TT], start=(c == 0), stop=(c == NCH - 1))
                return last
            cm_h = b.emit_inc('tensor', _hrt, 'sMM')
            b.slot_mm['wb0'] = ('sMM', cm_h)
            b.slot_mm['bt0'] = ('sMM', cm_h)
            b.slot_mm['bt1'] = ('sMM', cm_h)
            b.wait('scalar', 'sMM', cm_h)
            b.emit('scalar', lambda e: e.copy(tmpb[0:8, 0:TT], pb[5][0:8, 0:TT]))
            cs_l = b.mark('scalar', 'sSC')
            b.wait('tensor', 'sSC', cs_l)
            for half in range(2):
                cm_t = b.emit_inc('tensor', lambda e, half=half: e.transpose(
                    pb[3][0:128, 0:8], tmpb[0:8, half * 128:half * 128 + 128],
                    ident[0:8, 0:8]), 'sMM')
                b.wait('scalar', 'sMM', cm_t)
                b.emit('scalar', lambda e, half=half: e.copy(ltt[:, half * 8:half * 8 + 8],
                                                             pb[3][0:128, 0:8]))
                cs_t = b.mark('scalar', 'sSC')
                b.wait('tensor', 'sSC', cs_t)
            b.wait('vector', 'sSC', b.cnt['sSC'])

            def _gate_v1(e):
                for half in range(2):
                    lt = ltt[:, half * 8:half * 8 + 8]
                    m1 = rsc[:, half * 2 + 0:half * 2 + 1]
                    e.reduce_max(m1, lt, AX.X)
                    e.drain()
                    msk1 = rsc[:, 16 + half * 8:16 + half * 8 + 8]
                    e.tensor_scalar(msk1, lt, m1, None, ALU.is_ge)
                    e.drain()
                    l2 = rsc[:, 32 + half * 8:32 + half * 8 + 8]
                    e.scalar_tensor_tensor(l2, msk1, NEG_BIG, lt, ALU.mult, ALU.add)
                    e.drain()
                    m2 = rsc[:, half * 2 + 1:half * 2 + 2]
                    e.reduce_max(m2, l2, AX.X)
                    e.drain()
                    msk2 = rsc[:, 48 + half * 8:48 + half * 8 + 8]
                    e.tensor_scalar(msk2, l2, m2, None, ALU.is_ge)
                    e.drain()
                    dcol = rsc[:, 4 + half * 2:4 + half * 2 + 1]
                    e.tensor_tensor(dcol, m1, m2, ALU.subtract)
                    e.drain()
            b.emit('vector', _gate_v1)
            cv_g1 = b.mark('vector', 'sV')
            b.wait('scalar', 'sV', cv_g1)

            def _gate_s(e):
                for half in range(2):
                    dcol = rsc[:, 4 + half * 2:4 + half * 2 + 1]
                    e.activation(rsc[:, 8 + half * 2:8 + half * 2 + 1], dcol, AF.Sigmoid,
                                 bias=0.0, scale=1.0)
                    e.activation(rsc[:, 9 + half * 2:9 + half * 2 + 1], dcol, AF.Sigmoid,
                                 bias=0.0, scale=-1.0)
            b.emit('scalar', _gate_s)
            cs_g = b.mark('scalar', 'sSC')
            b.wait('vector', 'sSC', cs_g)

            def _gate_v2(e):
                for half in range(2):
                    w1 = rsc[:, 8 + half * 2:8 + half * 2 + 1]
                    w2 = rsc[:, 9 + half * 2:9 + half * 2 + 1]
                    msk1 = rsc[:, 16 + half * 8:16 + half * 8 + 8]
                    msk2 = rsc[:, 48 + half * 8:48 + half * 8 + 8]
                    t1 = rsc[:, 32 + half * 8:32 + half * 8 + 8]
                    e.tensor_scalar(t1, msk1, w1, None, ALU.mult)
                    e.tensor_scalar(msk2, msk2, w2, None, ALU.mult)
                    e.drain()
                    e.tensor_tensor(msk1, t1, msk2, ALU.add)
                    e.drain()
            b.emit('vector', _gate_v2)
            cv_g2 = b.mark('vector', 'sV')
            b.wait('tensor', 'sV', cv_g2)
            for half in range(2):
                cm_t = b.emit_inc('tensor', lambda e, half=half: e.transpose(
                    pb[3][0:8, 0:128], rsc[:, 16 + half * 8:16 + half * 8 + 8],
                    ident[:, :]), 'sMM')
                b.wait('scalar', 'sMM', cm_t)
                b.emit('scalar', lambda e, half=half: e.copy(wts[:, half * 128:half * 128 + 128],
                                                             pb[3][0:8, 0:128]))
                cs_t = b.mark('scalar', 'sSC')
                b.wait('tensor', 'sSC', cs_t)
            b.wait('tensor', 'sSC', b.cnt['sSC'])
            cm_rep = b.emit_inc('tensor', lambda e: e.matmul(pb[3][0:128, 0:TT], rep16[:, :],
                                                             wts[:, :], start=True, stop=True), 'sMM')
            b.wait('scalar', 'sMM', cm_rep)
            b.emit('scalar', lambda e: e.copy(wrep[:, :], pb[3][0:128, 0:TT]))
            cs_rep = b.mark('scalar', 'sSC')
            b.wait('vector', 'sSC', cs_rep)

            def _g(e):
                e.tensor_tensor(gbuf[:, :], pb[4][0:128, 0:TT], wrep[:, :], ALU.mult)
                e.drain()
            b.emit('vector', _g)
            cv_gg = b.mark('vector', 'sV')

            proj(wb0, a1, TT, TT, lambda o: a2[:, o * TT:(o + 1) * TT],
                 dep_v=cv_gg, bias_name='bb0', lora_dep=cv_gg, stats=True,
                 dep_chunks=cln_marks)
            cvr = rows_from_stats(TT)
            cbc2 = bcast_rows(TT, cvr)
            residual_update(a2, TT, TT, cbc2)

            fold = in_stats(xt, TT, TT, a1)
            cvr = rows_from_stats(TT, fold)
            cbc = bcast_rows(TT, cvr)
            cln, cln_marks = ln_into(a1, xt, TT, TT, cbc)

            cwk1 = b.dma('sync', wba[:, :], wk1, 'sW0', guard='wb0')
            cwv1 = b.dma('sync', wbb[:, :], wv1, 'sW1', guard='wb1')
            b.wait('tensor', 'sW0', cwk1)
            b.wait('tensor', 'sW1', cwv1)

            def _kv1(e, marks=cln_marks):
                for c in range(NCH):
                    e.wait_ge(b.h['sV'], marks[c])
                    e.matmul(pb[4][0:128, 0:TT], wba[:, c * 128:(c + 1) * 128],
                             a1[:, c * TT:(c + 1) * TT], start=(c == 0), stop=(c == NCH - 1))
                    last = e.matmul(pb[5][0:128, 0:TT], wbb[:, c * 128:(c + 1) * 128],
                                    a1[:, c * TT:(c + 1) * TT], start=(c == 0), stop=(c == NCH - 1))
                return last
            cm_kv1 = b.emit_inc('tensor', _kv1, 'sMM')
            b.slot_mm['wb0'] = ('sMM', cm_kv1)
            b.slot_mm['wb1'] = ('sMM', cm_kv1)
            b.wait('scalar', 'sMM', cm_kv1)
            b.emit('scalar', lambda e, t0=t0: e.activation(k1b[:, t0:t0 + TT], pb[4][0:128, 0:TT],
                                                           AF.Identity, bias=KCOL, scale=1.0))
            b.emit('scalar', lambda e: e.activation(tmpb[:, 0:TT], pb[5][0:128, 0:TT],
                                                    AF.Identity, bias=VCOL, scale=1.0))
            cs_kv1 = b.mark('scalar', 'sSC')
            b.wait('tensor', 'sSC', cs_kv1)
            for bb_ in range(2):
                cm_tr = b.emit_inc('tensor', lambda e, bb_=bb_: e.transpose(
                    pb[2][0:128, 0:128], tmpb[:, bb_ * 128:bb_ * 128 + 128],
                    ident[:, :]), 'sMM')
                b.wait('scalar', 'sMM', cm_tr)
                dst = v1t[:, (2 * t + bb_) * 128:(2 * t + bb_) * 128 + 128]
                b.emit('scalar', lambda e, dst=dst: e.copy(dst, pb[2][0:128, 0:128]))
                cs_tr = b.mark('scalar', 'sSC')
                b.wait('tensor', 'sSC', cs_tr)

            for o in range(H):
                par = o % 2
                buf = wba if par == 0 else wbb
                bank = pb[par]
                cw = b.dma('sync', buf[:, :], wq1[o], 'sW%d' % par, guard='wb%d' % par)
                b.wait('tensor', 'sW%d' % par, cw)

                def _grp(e, buf=buf, bank=bank):
                    for c in range(NCH):
                        last = e.matmul(bank[0:128, 0:128], buf[:, c * 128:(c + 1) * 128],
                                        a1[:, c * TT:c * TT + 128],
                                        start=(c == 0), stop=(c == NCH - 1))
                    return last
                cmm = b.emit_inc('tensor', _grp, 'sMM')
                b.slot_mm['wb%d' % par] = ('sMM', cmm)
                b.wait('scalar', 'sMM', cmm)
                dst = a2[:, o * 128:(o + 1) * 128]
                bc = bcol('qb1', o)
                b.emit('scalar', lambda e, dst=dst, bank=bank, bc=bc:
                       e.activation(dst, bank[0:128, 0:128], AF.Identity, bias=bc, scale=1.0))
                cs = b.mark('scalar', 'sSC')
                b.wait('tensor', 'sSC', cs)

            av_free = 0
            den_free = 0
            rep_free = 0
            sig_last = {0: 0, 1: 0}
            combine_q = []

            def flush_combine():
                while combine_q:
                    combine_q.pop(0)()
            for h in range(H):
                nkc = 2 * t + 2
                gates = {}
                pend_av = []
                pend_den = []

                def emit_av(kc, nkc=nkc):
                    if kc == 0:
                        b.wait('tensor', 'sSC', av_free)
                    b.wait('tensor', gates[kc][0], gates[kc][1])
                    esl = attns[:, (kc % 4) * 128:(kc % 4) * 128 + 128]

                    def _av(e, esl=esl, kc=kc, nkc=nkc):
                        return e.matmul(pb[4][0:128, 0:128], v1t[:, kc * 128:kc * 128 + 128], esl,
                                        start=(kc == 0), stop=(kc == nkc - 1))
                    b.emit_inc('tensor', _av, 'sMM')

                def emit_den(p, nkc=nkc):
                    if p == 0:
                        b.wait('tensor', 'sV', den_free)
                    for kk in (2 * p, 2 * p + 1):
                        b.wait('tensor', gates[kk][0], gates[kk][1])
                    sl = (2 * p) % 4

                    def _den(e, p=p, sl=sl, nkc=nkc):
                        return e.matmul(pb[6][0:1, 0:256], onec[:, :],
                                        attns[:, sl * 128:sl * 128 + 256],
                                        start=(p == 0), stop=(p == nkc // 2 - 1))
                    b.emit_inc('tensor', _den, 'sMM')

                for kc in range(nkc):
                    sb = pb[2 + (kc % 2)]
                    b.wait('tensor', 'sSC', sig_last[kc % 2])
                    cms = b.emit_inc('tensor', lambda e, sb=sb, kc=kc, h=h:
                                     e.matmul(sb[0:128, 0:128], k1b[:, kc * 128:kc * 128 + 128],
                                              a2[:, h * 128:(h + 1) * 128],
                                              start=True, stop=True), 'sMM')
                    if kc == min(2, nkc - 1):
                        flush_combine()
                    b.wait('scalar', 'sMM', cms)
                    esl = attns[:, (kc % 4) * 128:(kc % 4) * 128 + 128]
                    b.emit('scalar', lambda e, sb=sb, esl=esl:
                           e.activation(esl, sb[0:128, 0:128], AF.Exp, bias=0.0, scale=SCALE))
                    css = b.mark('scalar', 'sSC')
                    sig_last[kc % 2] = css
                    if kc >= nkc - 2:
                        b.wait('vector', 'sSC', css)
                        if kc == nkc - 2:
                            def _msk(e, esl=esl):
                                e.tensor_tensor(esl, esl, diag[:, :], ALU.mult)
                                e.drain()
                        else:
                            def _msk(e, esl=esl):
                                e.tensor_scalar(esl, esl, MLCOL, None, ALU.mult)
                                e.drain()
                        b.emit('vector', _msk)
                        cvm = b.mark('vector', 'sV')
                        gates[kc] = ('sV', cvm)
                    else:
                        gates[kc] = ('sSC', css)
                    pend_av.append(kc)
                    if len(pend_av) > 2:
                        emit_av(pend_av.pop(0))
                    if kc % 2 == 1:
                        pend_den.append(kc // 2)
                        if len(pend_den) > 1:
                            emit_den(pend_den.pop(0))
                while pend_av:
                    emit_av(pend_av.pop(0))
                while pend_den:
                    emit_den(pend_den.pop(0))
                cm_att = b.cnt['sMM']

                def _combine(h=h, cm_att=cm_att):
                    nonlocal av_free, den_free, rep_free
                    b.wait('vector', 'sMM', cm_att)

                    def _r(e):
                        e.tensor_copy(row(R_R1, 128), pb[6][0:1, 0:128])
                        e.drain()
                        e.tensor_tensor(row(R_R1, 128), row(R_R1, 128), pb[6][0:1, 128:256], ALU.add)
                        e.drain()
                        e.reciprocal(row(R_R1, 128), row(R_R1, 128))
                        e.drain()
                    b.emit('vector', _r)
                    cv_r = b.mark('vector', 'sV')
                    den_free = cv_r
                    b.wait('tensor', 'sV', cv_r)
                    b.wait('tensor', 'sV', rep_free)
                    cm_rb = b.emit_inc('tensor', lambda e: e.matmul(pb[7][0:128, 0:128],
                                                                    oner[:, :], row(R_R1, 128),
                                                                    start=True, stop=True), 'sMM')
                    b.wait('scalar', 'sMM', cm_rb)
                    b.emit('scalar', lambda e: e.copy(tmpb[:, 0:128], pb[4][0:128, 0:128]))
                    cs_av = b.mark('scalar', 'sSC')
                    av_free = cs_av
                    b.wait('vector', 'sSC', cs_av)

                    def _nrm(e, h=h):
                        e.tensor_tensor(a1[:, h * 128:(h + 1) * 128], tmpb[:, 0:128],
                                        pb[7][0:128, 0:128], ALU.mult)
                        e.drain()
                    b.emit('vector', _nrm)
                    cv_n = b.mark('vector', 'sV')
                    rep_free = cv_n
                combine_q.append(_combine)
            flush_combine()

            proj(wo1, a1, 128, 128, lambda o: a2[:, o * 128:(o + 1) * 128],
                 dep_v=b.cnt['sV'], bias_name='ob1', stats=True)
            cvr = rows_from_stats(128)
            cbc2 = bcast_rows(128, cvr)
            residual_update(a2, 128, TT, cbc2)

            in_stats(xt, 128, TT, a1)
            cvr = rows_from_stats(128)
            cbc = bcast_rows(128, cvr)
            cln, _ = ln_into(a1, xt, 128, TT, cbc)
            b.wait('vector', 'sV', cln)

            def _split(e):
                e.tensor_copy(lnpair[:, 0:W_], a1[:, 0:W_])
                e.drain()
                e.tensor_tensor(lnpair[:, W_:2 * W_], a1[:, 0:W_], lnpair[:, 0:W_], ALU.subtract)
                e.drain()
            b.emit('vector', _split)
            cv_sp = b.mark('vector', 'sV')
            b.wait('tensor', 'sV', cv_sp)

            tern_sc_done = {0: 0, 1: 0}
            tern_cvv = {}
            tern_pend = []
            b.wait('scalar', 'sV', cv_sp)
            for o in range(NCH):
                bank = pb[o % 2]
                if t == 0:
                    par = o % 2
                    buf = wba if par == 0 else wbb
                    b.wait('sync', 'sSC', tern_sc_done[par])
                    cw = b.dma('sync', buf[:, :], wf1[o], 'sW%d' % par, guard='wb%d' % par)
                    b.wait('scalar', 'sW%d' % par, cw)

                    def _tern_s(e, buf=buf):
                        e.activation(a1[:, 0:W_], buf[:, :], AF.Abs, bias=0.0, scale=1.0)
                        e.activation(a1[:, W_:2 * W_], buf[:, :], AF.Sign, bias=0.0, scale=1.0)
                        e.drain()
                    b.emit('scalar', _tern_s)
                    cs_t = b.mark('scalar', 'sSC')
                    tern_sc_done[par] = cs_t
                    b.slot_mm['wb%d' % par] = ('sSC', cs_t)
                    b.wait('vector', 'sSC', cs_t)
                    b.wait('vector', 'sTS', b.cnt['sTS'])

                    def _tern_v(e):
                        e.tensor_single_scalar(a1[:, 0:W_], a1[:, 0:W_], 0.1, ALU.is_ge)
                        e.drain()
                        e.tensor_tensor(ternb[:, :], a1[:, W_:2 * W_], a1[:, 0:W_], ALU.mult)
                        e.drain()
                    b.emit('vector', _tern_v)
                    cv_t = b.mark('vector', 'sV')
                    b.wait('gpsimd', 'sV', cv_t)
                    b.dma('gpsimd', tern_sc[o], ternb[:, :], 'sTS')
                    b.wait('tensor', 'sV', cv_t)
                else:
                    if o == 0:
                        b.wait('sync', 'sTS', 32 * 16)
                    cw = b.dma('sync', ternb[:, :], tern_sc[o], 'sW0',
                               guard=(['tern', 'wb0'] if o == 0 else 'tern'))
                    b.wait('tensor', 'sW0', cw)

                if o >= 2:
                    b.wait('tensor', 'sV', tern_cvv[o - 2])

                def _grp(e, bank=bank):
                    for c in range(NCH):
                        e.matmul(bank[0:128, 0:128], ternb[:, c * 128:(c + 1) * 128],
                                 lnpair[:, c * 128:c * 128 + 128], start=(c == 0), stop=False)
                    for c in range(NCH):
                        last = e.matmul(bank[0:128, 0:128], ternb[:, c * 128:(c + 1) * 128],
                                        lnpair[:, W_ + c * 128:W_ + c * 128 + 128],
                                        start=False, stop=(c == NCH - 1))
                    return last
                cmm = b.emit_inc('tensor', _grp, 'sMM')
                b.slot_mm['tern'] = ('sMM', cmm)
                if tern_pend:
                    po, pdst, pcs2 = tern_pend.pop(0)
                    b.wait('tensor', 'sSC', pcs2)

                    def _stat(e, dst=pdst, o=po):
                        e.matmul(pb[6][0:1, 0:128], onec[:, :], dst,
                                 start=(o == 0), stop=(o == NCH - 1))
                        return e.matmul(pb[7][0:1, 0:128], onec[:, :],
                                        tmpb[:, 256 + (o % 2) * 128:384 + (o % 2) * 128],
                                        start=(o == 0), stop=(o == NCH - 1))
                    b.emit_inc('tensor', _stat, 'sMM')
                b.wait('vector', 'sMM', cmm)
                dst = a2[:, o * 128:(o + 1) * 128]

                def _ev(e, dst=dst, bank=bank, o=o):
                    e.tensor_scalar(dst, bank[0:128, 0:128], bcol('fb1', o), bcol('fs1', o),
                                    ALU.add, ALU.mult)
                    e.drain()
                b.emit('vector', _ev)
                cvv = b.mark('vector', 'sV')
                tern_cvv[o] = cvv
                b.wait('scalar', 'sV', cvv)
                b.emit('scalar', lambda e, dst=dst, o=o:
                       e.activation(tmpb[:, 256 + (o % 2) * 128:384 + (o % 2) * 128], dst,
                                    AF.Square, bias=0.0, scale=1.0))
                cs2 = b.mark('scalar', 'sSC')
                tern_pend.append((o, dst, cs2))

            if tern_pend:
                po, pdst, pcs2 = tern_pend.pop(0)
                b.wait('tensor', 'sSC', pcs2)

                def _stat(e, dst=pdst, o=po):
                    e.matmul(pb[6][0:1, 0:128], onec[:, :], dst,
                             start=(o == 0), stop=(o == NCH - 1))
                    return e.matmul(pb[7][0:1, 0:128], onec[:, :],
                                    tmpb[:, 256 + (o % 2) * 128:384 + (o % 2) * 128],
                                    start=(o == 0), stop=(o == NCH - 1))
                b.emit_inc('tensor', _stat, 'sMM')
            b.slot_mm['wb0'] = b.slot_mm['tern']
            cvr = rows_from_stats(128)
            cbc3 = bcast_rows(128, cvr)
            b.wait('vector', 'sMM', cbc3)

            def _lnaff(e):
                for c in range(NCH):
                    yc = a2[:, c * 128:(c + 1) * 128]
                    e.tensor_tensor(yc, yc, pb[2][0:128, 0:128], ALU.mult)
                    e.drain()
                    e.tensor_tensor(yc, yc, pb[3][0:128, 0:128], ALU.subtract)
                    e.drain()
                    e.tensor_scalar(yc, yc, bcol('fg1', c), bcol('fbt1', c), ALU.mult, ALU.add)
                    e.drain()
            b.emit('vector', _lnaff)
            cv_la = b.mark('vector', 'sV')
            b.wait('scalar', 'sV', cv_la)
            b.emit('scalar', lambda e: e.activation(a1[:, 0:W_], a2[:, 0:W_], AF.Tanh,
                                                    bias=0.0, scale=QSCOL))
            cs_th = b.mark('scalar', 'sSC')
            b.wait('vector', 'sSC', cs_th)

            def _round2(e):
                e.tensor_scalar(a2[:, 0:W_], a1[:, 0:W_], 127.0, 128.5, ALU.mult, ALU.add)
                e.drain()
                e.tensor_copy(a1[:, 0:W_], a2[:, 0:W_])
                e.drain()
                for k in (128.0, 64.0, 32.0, 16.0, 8.0, 4.0, 2.0, 1.0):
                    e.tensor_scalar(a1[:, W_:2 * W_], a1[:, 0:W_], k, k, ALU.is_ge, ALU.mult)
                    e.drain()
                    e.tensor_tensor(a1[:, 0:W_], a1[:, 0:W_], a1[:, W_:2 * W_], ALU.subtract)
                    e.drain()
                e.tensor_tensor(a2[:, 0:W_], a2[:, 0:W_], a1[:, 0:W_], ALU.subtract)
                e.drain()
                e.tensor_scalar(a2[:, 0:W_], a2[:, 0:W_], -128.0, 1.0 / 127.0, ALU.add, ALU.mult)
                e.drain()
            b.emit('vector', _round2)
            cv_rd = b.mark('vector', 'sV')

            b.wait('scalar', 'sV', cv_rd)
            b.emit('scalar', lambda e: e.activation(a1[:, 0:W_], a2[:, 0:W_], AF.Square,
                                                    bias=0.0, scale=1.0))
            cs3 = b.mark('scalar', 'sSC')
            b.wait('tensor', 'sSC', cs3)

            def _st3(e):
                for c4 in range(NCH // 4):
                    e.matmul(pb[6][0:1, 0:512], onec[:, :], a2[:, c4 * 512:(c4 + 1) * 512],
                             start=(c4 == 0), stop=(c4 == NCH // 4 - 1))
                    last = e.matmul(pb[7][0:1, 0:512], onec[:, :], a1[:, c4 * 512:(c4 + 1) * 512],
                                    start=(c4 == 0), stop=(c4 == NCH // 4 - 1))
                return last
            b.emit_inc('tensor', _st3, 'sMM')
            cvr = rows_from_stats(128, fold=4)
            cbc4 = bcast_rows(128, cvr)
            residual_update(a2, 128, TT, cbc4)

            b.wait('vector', 'sV', b.cnt['sV'])

            def _stage(e):
                e.tensor_copy(a2[:, 0:W_].rearrange('p (c w) -> p c w', c=NCH),
                              xt[:].rearrange('p (c w) -> p c w', c=NCH)[:, :, 0:128])
                e.drain()
            b.emit('vector', _stage)
            cv_stage_prev = b.mark('vector', 'sV')
            b.wait('gpsimd', 'sV', cv_stage_prev)
            cx_out_prev = b.dma('gpsimd', y_out[t],
                                a2[:, 0:W_].rearrange('p (c w) -> p c w', c=NCH), 'sOUT')
            b.wait('vector', 'sOUT', cx_out_prev)

        globals()['LAST_CNT'] = dict(b.cnt)
        b.wait('gpsimd', 'sOUT', b.cnt['sOUT'])
        for s in ('sTS', 'sCS', 'sXT', 'sREP'):
            if b.cnt[s]:
                b.wait('gpsimd', s, b.cnt[s])

        with nc.Block() as block:
            @block.sync
            def _(e):
                for f in b.q['sync']:
                    f(e)

            @block.tensor
            def _(e):
                for f in b.q['tensor']:
                    f(e)

            @block.vector
            def _(e):
                for f in b.q['vector']:
                    f(e)

            @block.scalar
            def _(e):
                for f in b.q['scalar']:
                    f(e)

            @block.gpsimd
            def _(e):
                for f in b.q['gpsimd']:
                    f(e)

    return nc



def _rope_cache():
    base = 10000.0
    inv_freq = 1.0 / base ** (np.arange(0, DH, 2, dtype=np.float32) / DH)
    t = np.arange(S, dtype=np.float32)
    f = np.outer(t, inv_freq)
    cos = np.zeros((S, DH), dtype=np.float32)
    sin = np.zeros((S, DH), dtype=np.float32)
    cos[:, 0::2] = np.cos(f)
    cos[:, 1::2] = np.cos(f)
    sin[:, 0::2] = np.sin(f)
    sin[:, 1::2] = np.sin(f)
    return cos, sin


def _pack_w(wt):
    return np.ascontiguousarray(
        wt.reshape(NCH, 128, NCH, 128).transpose(2, 1, 0, 3).reshape(NCH, 128, W_))


def _pack_w_small(wt, outw):
    return np.ascontiguousarray(
        wt.reshape(NCH, 128, outw).transpose(1, 0, 2).reshape(128, NCH * outw))


def kernel(**inputs):
    ins = {k: np.asarray(v) for k, v in inputs.items()}
    x0 = ins['L0_attn_x']

    cos, sin = _rope_cache()

    sh = {}
    sh['wq0'] = _pack_w(np.ascontiguousarray(ins['L0_attn_Wq'].T))
    sh['wo0'] = _pack_w(np.ascontiguousarray(ins['L0_attn_Wo'].T))
    sh['wb0'] = _pack_w(np.ascontiguousarray(ins['L0_ffn_W_base'].T))
    sh['wq1'] = _pack_w(np.ascontiguousarray(ins['L1_attn_wq_w'].T))
    sh['wo1'] = _pack_w(np.ascontiguousarray(ins['L1_attn_wo_w'].T))
    sh['wf1'] = _pack_w(np.ascontiguousarray(ins['L1_ffn_weight'].T))
    sh['wk0'] = _pack_w_small(np.ascontiguousarray(ins['L0_attn_Wk'].T), 128)
    sh['wv0'] = _pack_w_small(np.ascontiguousarray(ins['L0_attn_Wv'].T), 128)
    sh['wk1'] = _pack_w_small(np.ascontiguousarray(ins['L1_attn_wk_w'].T), 128)
    sh['wv1'] = _pack_w_small(np.ascontiguousarray(ins['L1_attn_wv_w'].T), 128)
    sh['wrt'] = _pack_w_small(np.ascontiguousarray(ins['L0_ffn_W_router'].T), 8)
    sh['atw'] = _pack_w_small(np.ascontiguousarray(ins['L0_ffn_A'].reshape(128, 4096).T), 128)
    bt = np.ascontiguousarray(ins['L0_ffn_B'].transpose(0, 2, 1).reshape(128, 4096))
    sh['btw'] = np.ascontiguousarray(bt.reshape(128, NCH, 128).transpose(1, 0, 2))
    for nm, key in (('bb0', 'L0_ffn_b_base'), ('qb1', 'L1_attn_wq_b'), ('ob1', 'L1_attn_wo_b'),
                    ('fb1', 'L1_ffn_bias'), ('fs1', 'L1_ffn_scale'),
                    ('fg1', 'L1_ffn_ln_gamma'), ('fbt1', 'L1_ffn_ln_beta')):
        sh[nm] = np.ascontiguousarray(ins[key].reshape(NCH, 128).T)
    sh['kb1'] = np.ascontiguousarray(ins['L1_attn_wk_b'].reshape(128, 1))
    sh['vb1'] = np.ascontiguousarray(ins['L1_attn_wv_b'].reshape(128, 1))
    sh['qs1'] = np.ascontiguousarray(ins['L1_ffn_quant_scale'].reshape(1, 1))
    kq = np.arange(128)
    sh['diagi'] = (kq[:, None] <= kq[None, :]).astype(np.float32)
    sh['identi'] = np.eye(128, dtype=np.float32)
    rotm = np.zeros((128, 128), np.float32)
    for d in range(64):
        rotm[d, 64 + d] = -1.0
        rotm[64 + d, d] = 1.0
    sh['rotmi'] = np.ascontiguousarray(rotm.T)
    rep16 = np.zeros((8, 128), np.float32)
    for e_ in range(8):
        rep16[e_, e_ * 16:(e_ + 1) * 16] = 2.0
    sh['rep16i'] = rep16

    in_maps = []
    for core in range(8):
        seq, role = core // 2, core % 2
        perm = np.arange(S)
        if role == 1:
            perm = np.ascontiguousarray(perm.reshape(-1, 2, 128)[:, ::-1, :]).reshape(-1)
        xl = x0[seq][perm]
        m = dict(sh)
        m['x0p'] = np.ascontiguousarray(xl.T.reshape(NCH, 128, S))
        m['cosp'] = np.ascontiguousarray(cos[perm].T)
        m['sinp'] = np.ascontiguousarray(sin[perm].T)
        if role == 0:
            m['rampi'] = np.arange(TT, dtype=np.float32).reshape(1, TT)
        else:
            m['rampi'] = np.concatenate([np.arange(128, 256), np.arange(0, 128)]
                                        ).astype(np.float32).reshape(1, TT)
        m['mlasti'] = np.full((128, 1), float(role), np.float32)
        in_maps.append(m)

    nc = build_nc()
    res = run_bass_kernel_spmd(nc, in_maps, core_ids=list(range(8)))

    out = np.empty((4, S, D), np.float32)
    for core in range(8):
        seq, role = core // 2, core % 2
        yo = res.results[core]['y_out']
        for t in range(NT):
            g = 2 * t + role
            out[seq, g * 128:(g + 1) * 128, :] = yo[t].transpose(2, 1, 0).reshape(128, D)
    return out


# version 07
# speedup vs baseline: 1.1962x; 1.0032x over previous
import math
import numpy as np
from contextlib import ExitStack

import concourse.bass as bass
import concourse.mybir as mybir
from concourse.bass_utils import run_bass_kernel_spmd

F32 = mybir.dt.float32
BF16 = mybir.dt.bfloat16
ALU = mybir.AluOpType
AF = mybir.ActivationFunctionType
AX = mybir.AxisListType

D = 4096
NCH = 32
S = 2048
TT = 256
NT = 8
H = 32
DH = 128
EPS = 1e-5
SCALE = 1.0 / math.sqrt(128.0)
NEG_BIG = -1e30
W_ = NCH * 128

CFG_NT = NT


class Bld:
    ENGS = ('sync', 'tensor', 'vector', 'scalar', 'gpsimd')

    def __init__(self, nc, es):
        self.nc = nc
        self.q = {e: [] for e in self.ENGS}
        self.h = {}
        self.cnt = {}
        for name in ('sW0', 'sW1', 'sP0', 'sP1', 'sB0', 'sB1', 'sRT',
                     'sC', 'sXT', 'sCS', 'sREP', 'sTS', 'sOUT',
                     'sMM', 'sV', 'sSC'):
            self.h[name] = es.enter_context(nc.semaphore(name))
            self.cnt[name] = 0
        self.slot_mm = {}

    def emit(self, eng, fn):
        self.q[eng].append(fn)

    def wait(self, eng, sname, val):
        assert val <= self.cnt[sname], (sname, val, self.cnt[sname])
        if val <= 0:
            return
        h = self.h[sname]
        self.q[eng].append(lambda e, h=h, v=val: e.wait_ge(h, v))

    def dma(self, eng, out, in_, sname, guard=None):
        if guard is not None:
            for g in ([guard] if isinstance(guard, str) else guard):
                gs, gv = self.slot_mm.get(g, ('sMM', 0))
                self.wait(eng, gs, gv)
        h = self.h[sname]
        self.cnt[sname] += 16
        self.q[eng].append(lambda e, o=out, i=in_, h=h: e.dma_start(out=o, in_=i).then_inc(h, 16))
        return self.cnt[sname]

    def mark(self, eng, sname):
        h = self.h[sname]
        self.cnt[sname] += 1
        self.q[eng].append(lambda e, h=h: e.drain().then_inc(h, 1))
        return self.cnt[sname]

    def inc(self, eng, sname):
        h = self.h[sname]
        self.cnt[sname] += 1
        self.q[eng].append(lambda e, h=h: e.drain().then_inc(h, 1))
        return self.cnt[sname]

    def emit_inc(self, eng, fn, sname):
        h = self.h[sname]
        self.cnt[sname] += 1

        def g(e, fn=fn, h=h):
            inst = fn(e)
            inst.then_inc(h, 1)
        self.q[eng].append(g)
        return self.cnt[sname]


def build_nc():
    nc = bass.Bass('TRN2', target_bir_lowering=False, debug=False)
    n_t = CFG_NT

    def din(name, shape):
        return nc.dram_tensor(name, list(shape), F32, kind='ExternalInput').ap()

    x0p = din('x0p', (NCH, 128, S))
    cosp = din('cosp', (128, S))
    sinp = din('sinp', (128, S))
    rampi = din('rampi', (1, TT))
    mlasti = din('mlasti', (128, 1))
    diagi = din('diagi', (128, 128))
    identi = din('identi', (128, 128))
    rotmi = din('rotmi', (128, 128))
    rep16i = din('rep16i', (8, 128))
    wq0 = din('wq0', (NCH, 128, W_))
    wo0 = din('wo0', (NCH, 128, W_))
    wb0 = din('wb0', (NCH, 128, W_))
    wq1 = din('wq1', (NCH, 128, W_))
    wo1 = din('wo1', (NCH, 128, W_))
    wf1 = din('wf1', (NCH, 128, W_))
    wk0 = din('wk0', (128, W_))
    wv0 = din('wv0', (128, W_))
    wk1 = din('wk1', (128, W_))
    wv1 = din('wv1', (128, W_))
    wrt = din('wrt', (128, NCH * 8))
    atw = din('atw', (128, W_))
    btw = din('btw', (NCH, 128, 128))
    bb0 = din('bb0', (128, NCH))
    qb1 = din('qb1', (128, NCH))
    ob1 = din('ob1', (128, NCH))
    fb1 = din('fb1', (128, NCH))
    fs1 = din('fs1', (128, NCH))
    fg1 = din('fg1', (128, NCH))
    fbt1 = din('fbt1', (128, NCH))
    kb1 = din('kb1', (128, 1))
    vb1 = din('vb1', (128, 1))
    qs1 = din('qs1', (1, 1))
    y_out = nc.dram_tensor('y_out', [NT, 128, NCH, 128], F32, kind='ExternalOutput').ap()
    tern_sc = nc.dram_tensor('tern_sc', [NCH, 128, W_], BF16, kind='Internal').ap()

    es = ExitStack()
    with es:
        SB = lambda n, shp, dt=F32: es.enter_context(nc.sbuf_tensor(n, list(shp), dt))

        xt = SB('xt', (128, NCH * TT))
        a1 = SB('a1', (128, NCH * TT))
        a2 = SB('a2', (128, NCH * TT))
        wba = SB('wba', (128, W_))
        wbb = SB('wbb', (128, W_))
        k0r = SB('k0r', (128, S))
        v0t = SB('v0t', (128, S))
        k1b = SB('k1b', (128, S))
        v1t = SB('v1t', (128, S))
        costl = SB('costl', (128, TT))
        sintl = SB('sintl', (128, TT))
        attns = SB('attns', (128, 1024))
        crep = SB('crep', (128, 512))
        tmpb = SB('tmpb', (128, 512))
        btb = SB('btb', (128, 256))
        rows = SB('rows', (1, 2048))
        ternb = SB('ternb', (128, W_), BF16)
        lnpair = SB('lnpair', (128, 2 * W_), BF16)
        biases = SB('biases', (128, 8 * NCH))
        smallc = SB('smallc', (128, 8))
        diag = SB('diag', (128, 128))
        ident = SB('ident', (128, 128))
        rotm = SB('rotm', (128, 128))
        rep16 = SB('rep16', (8, 128))
        onec = SB('onec', (128, 1))
        oner = SB('oner', (1, 128))
        ltt = SB('ltt', (128, 16))
        rsc = SB('rsc', (128, 64))

        wrtb = btb
        gbuf = crep[:, 0:TT]
        wrep = crep[:, TT:2 * TT]
        wts = tmpb[0:8, 256:512]

        pb = [es.enter_context(nc.psum_tensor('pb%d' % i, [128, 512], F32)) for i in range(8)]

        b = Bld(nc, es)

        def row(off, w):
            return rows[0:1, off:off + w]

        R_AL, R_AL1, R_M, R_V, R_TMP = 0, 256, 512, 1024, 1536
        R_R1, R_R2, R_C1, R_C2 = 512, 768, 1024, 1280

        BIAS = {'bb0': 0, 'qb1': NCH, 'ob1': 2 * NCH, 'fb1': 3 * NCH,
                'fs1': 4 * NCH, 'fg1': 5 * NCH, 'fbt1': 6 * NCH}

        def bcol(name, o):
            return biases[:, BIAS[name] + o: BIAS[name] + o + 1]

        KCOL = smallc[:, 0:1]
        VCOL = smallc[:, 1:2]
        QSCOL = smallc[:, 2:3]
        MLCOL = smallc[:, 3:4]

        for dst, src in ((diag[:, :], diagi), (ident[:, :], identi), (rotm[:, :], rotmi),
                         (rep16[:, :], rep16i),
                         (biases[:, 0:NCH], bb0), (biases[:, NCH:2 * NCH], qb1),
                         (biases[:, 2 * NCH:3 * NCH], ob1), (biases[:, 3 * NCH:4 * NCH], fb1),
                         (biases[:, 4 * NCH:5 * NCH], fs1), (biases[:, 5 * NCH:6 * NCH], fg1),
                         (biases[:, 6 * NCH:7 * NCH], fbt1),
                         (smallc[:, 0:1], kb1), (smallc[:, 1:2], vb1), (smallc[:, 3:4], mlasti),
                         (rows[0:1, 1792:1792 + TT], rampi), (rows[0:1, R_M:R_M + 1], qs1)):
            b.dma('gpsimd', dst, src, 'sC')
        c_load = b.cnt['sC']

        b.wait('vector', 'sC', c_load)

        def _init_v(e):
            e.memset(onec[:, :], 1.0)
            e.memset(oner[:, :], 1.0)
            e.drain()
            e.tensor_scalar(row(R_AL, TT), row(1792, TT), 1.0 / 255.0, None, ALU.mult)
            e.tensor_scalar(row(R_AL1, TT), row(1792, TT), -1.0 / 255.0, 1.0, ALU.mult, ALU.add)
            e.drain()
            e.reciprocal(row(R_M + 1, 1), row(R_M, 1))
            e.drain()
        b.emit('vector', _init_v)
        cv = b.mark('vector', 'sV')
        b.wait('tensor', 'sV', cv)
        cm = b.emit_inc('tensor', lambda e: e.matmul(pb[0][0:128, 0:1], oner[:, :],
                                                     row(R_M + 1, 1), start=True, stop=True), 'sMM')
        b.wait('scalar', 'sMM', cm)
        b.emit('scalar', lambda e: e.copy(smallc[:, 2:3], pb[0][0:128, 0:1]))
        cs = b.mark('scalar', 'sSC')
        b.wait('tensor', 'sSC', cs)

        def rows_from_stats(ntok, fold=1):
            b.wait('vector', 'sMM', b.cnt['sMM'])

            def _rows(e):
                if fold == 1:
                    e.tensor_scalar(row(R_M, ntok), pb[6][0:1, 0:ntok], 1.0 / 4096.0, None, ALU.mult)
                    e.tensor_scalar(row(R_V, ntok), pb[7][0:1, 0:ntok], 1.0 / 4096.0, None, ALU.mult)
                    e.drain()
                else:
                    for bank_i, dst in ((6, R_M), (7, R_V)):
                        e.tensor_copy(row(dst, ntok), pb[bank_i][0:1, 0:ntok])
                        e.drain()
                        for f in range(1, fold):
                            e.tensor_tensor(row(dst, ntok), row(dst, ntok),
                                            pb[bank_i][0:1, f * ntok:(f + 1) * ntok], ALU.add)
                            e.drain()
                        e.tensor_scalar(row(dst, ntok), row(dst, ntok), 1.0 / 4096.0, None, ALU.mult)
                        e.drain()
                e.tensor_tensor(row(R_TMP, ntok), row(R_M, ntok), row(R_M, ntok), ALU.mult)
                e.drain()
                e.tensor_tensor(row(R_V, ntok), row(R_V, ntok), row(R_TMP, ntok), ALU.subtract)
                e.drain()
                e.tensor_scalar(row(R_V, ntok), row(R_V, ntok), EPS, None, ALU.add)
                e.drain()
            b.emit('vector', _rows)
            cv = b.mark('vector', 'sV')
            b.wait('scalar', 'sV', cv)
            b.emit('scalar', lambda e: e.activation(row(R_TMP, ntok), row(R_V, ntok),
                                                    AF.Sqrt, bias=0.0, scale=1.0))
            cs = b.mark('scalar', 'sSC')
            b.wait('vector', 'sSC', cs)

            def _rr(e):
                e.reciprocal(row(R_V, ntok), row(R_TMP, ntok))
                e.drain()
                e.tensor_tensor(row(R_TMP, ntok), row(R_M, ntok), row(R_V, ntok), ALU.mult)
                e.drain()
            b.emit('vector', _rr)
            return b.mark('vector', 'sV')

        def bcast_rows(ntok, dep_v, banks=(2, 3)):
            b.wait('tensor', 'sV', dep_v)

            def _bc(e, banks=banks):
                e.matmul(pb[banks[0]][0:128, 0:ntok], oner[:, :], row(R_V, ntok),
                         start=True, stop=True)
                return e.matmul(pb[banks[1]][0:128, 0:ntok], oner[:, :], row(R_TMP, ntok),
                                start=True, stop=True)
            return b.emit_inc('tensor', _bc, 'sMM')

        def in_stats(src_sb, ntok, chunk_w, sq_buf, dep_chunks=None):
            if dep_chunks is None:
                b.wait('scalar', 'sV', b.cnt['sV'])

            def _sq(e, marks=dep_chunks):
                for c in range(NCH):
                    if marks is not None:
                        e.wait_ge(b.h['sV'], marks[c])
                    e.activation(sq_buf[:, c * ntok:(c + 1) * ntok],
                                 src_sb[:, c * chunk_w:c * chunk_w + ntok],
                                 AF.Square, bias=0.0, scale=1.0)
            b.emit('scalar', _sq)
            csq = b.mark('scalar', 'sSC')
            b.wait('tensor', 'sSC', csq)
            if ntok == chunk_w:
                npair = NCH // 2

                def _sums(e):
                    for c2 in range(npair):
                        e.matmul(pb[6][0:1, 0:2 * ntok], onec[:, :],
                                 src_sb[:, c2 * 2 * ntok:(c2 + 1) * 2 * ntok],
                                 start=(c2 == 0), stop=(c2 == npair - 1))
                        last = e.matmul(pb[7][0:1, 0:2 * ntok], onec[:, :],
                                        sq_buf[:, c2 * 2 * ntok:(c2 + 1) * 2 * ntok],
                                        start=(c2 == 0), stop=(c2 == npair - 1))
                    return last
                b.emit_inc('tensor', _sums, 'sMM')
                return 2

            def _sums(e):
                for c in range(NCH):
                    e.matmul(pb[6][0:1, 0:ntok], onec[:, :],
                             src_sb[:, c * chunk_w:c * chunk_w + ntok],
                             start=(c == 0), stop=(c == NCH - 1))
                    last = e.matmul(pb[7][0:1, 0:ntok], onec[:, :],
                                    sq_buf[:, c * ntok:(c + 1) * ntok],
                                    start=(c == 0), stop=(c == NCH - 1))
                return last
            b.emit_inc('tensor', _sums, 'sMM')
            return 1

        def ln_into(dst, src_sb, ntok, chunk_w, cmm_bcast):
            b.wait('vector', 'sMM', cmm_bcast)
            marks = []
            hv = b.h['sV']
            for c in range(NCH):
                b.cnt['sV'] += 1
                marks.append(b.cnt['sV'])

                def _ln(e, c=c, v=b.cnt['sV']):
                    s = src_sb[:, c * chunk_w:c * chunk_w + ntok]
                    d = dst[:, c * ntok:(c + 1) * ntok]
                    e.tensor_tensor(d, s, pb[2][0:128, 0:ntok], ALU.mult)
                    e.drain()
                    e.tensor_tensor(d, d, pb[3][0:128, 0:ntok], ALU.subtract)
                    e.drain().then_inc(hv, 1)
                b.emit('vector', _ln)
            return marks[-1], marks

        def residual_update(y_sb, ntok, chunk_w, cmm_bcast):
            b.wait('vector', 'sMM', cmm_bcast)
            marks = []
            hv = b.h['sV']
            for c in range(NCH):
                b.cnt['sV'] += 1
                marks.append(b.cnt['sV'])

                def _res(e, c=c):
                    yc = y_sb[:, c * ntok:(c + 1) * ntok]
                    t = tmpb[:, 0:ntok]
                    e.tensor_tensor(t, yc, pb[2][0:128, 0:ntok], ALU.mult)
                    e.drain()
                    e.tensor_tensor(t, t, pb[3][0:128, 0:ntok], ALU.subtract)
                    e.drain()
                    dc = xt[:, c * chunk_w:c * chunk_w + ntok]
                    e.tensor_tensor(dc, dc, t, ALU.add)
                    e.drain().then_inc(hv, 1)
                b.emit('vector', _res)
            return marks[-1], marks

        def proj(w_dram, rhs_sb, ntok, chunk_w, out_cb, dep_v, bias_name=None,
                 lora_dep=None, stats=False, dep_chunks=None):
            pend = []
            csc_hist = {}

            def emit_stat(e_o, e_dst):
                def _stat(e, dst=e_dst, o=e_o):
                    e.matmul(pb[6][0:1, 0:ntok], onec[:, :], dst,
                             start=(o == 0), stop=(o == NCH - 1))
                    return e.matmul(pb[7][0:1, 0:ntok], onec[:, :],
                                    tmpb[:, (e_o % 2) * ntok:(e_o % 2) * ntok + ntok],
                                    start=(o == 0), stop=(o == NCH - 1))
                b.emit_inc('tensor', _stat, 'sMM')

            for o in range(NCH):
                par = o % 2
                buf = wba if par == 0 else wbb
                bank = pb[par]
                cw = b.dma('sync', buf[:, :], w_dram[o], 'sW%d' % par, guard='wb%d' % par)
                b.wait('tensor', 'sW%d' % par, cw)
                if lora_dep is not None:
                    cbt = b.dma('sync', btb[:, par * 128:par * 128 + 128], btw[o], 'sB%d' % par,
                                guard='bt%d' % par)
                    b.wait('tensor', 'sB%d' % par, cbt)
                if o == 0 and dep_v and dep_chunks is None:
                    b.wait('tensor', 'sV', dep_v)
                if o == 0 and lora_dep:
                    b.wait('tensor', 'sV', lora_dep)
                if o >= 2:
                    b.wait('tensor', 'sSC', csc_hist[o - 2])

                def _grp(e, buf=buf, bank=bank, par=par, first=(o == 0)):
                    for c in range(NCH):
                        if first and dep_chunks is not None:
                            e.wait_ge(b.h['sV'], dep_chunks[c])
                        last = e.matmul(bank[0:128, 0:ntok], buf[:, c * 128:(c + 1) * 128],
                                        rhs_sb[:, c * chunk_w:c * chunk_w + ntok],
                                        start=(c == 0), stop=(c == NCH - 1 and lora_dep is None))
                    if lora_dep is not None:
                        last = e.matmul(bank[0:128, 0:ntok], btb[:, par * 128:par * 128 + 128],
                                        gbuf[:, 0:ntok], start=False, stop=True)
                    return last
                cmm = b.emit_inc('tensor', _grp, 'sMM')
                b.slot_mm['wb%d' % par] = ('sMM', cmm)
                b.slot_mm['bt%d' % par] = ('sMM', cmm)
                if stats and pend:
                    po, pdst, pcsc = pend.pop(0)
                    b.wait('tensor', 'sSC', pcsc)
                    emit_stat(po, pdst)
                b.wait('scalar', 'sMM', cmm)
                dst = out_cb(o)
                if bias_name is None:
                    b.emit('scalar', lambda e, dst=dst, bank=bank: e.copy(dst, bank[0:128, 0:ntok]))
                else:
                    bc = bcol(bias_name, o)
                    b.emit('scalar', lambda e, dst=dst, bank=bank, bc=bc:
                           e.activation(dst, bank[0:128, 0:ntok], AF.Identity, bias=bc, scale=1.0))
                if stats:
                    b.emit('scalar', lambda e, bank=bank, o=o:
                           e.activation(tmpb[:, (o % 2) * ntok:(o % 2) * ntok + ntok],
                                        bank[0:128, 0:ntok], AF.Square, bias=0.0, scale=1.0))
                csc = b.mark('scalar', 'sSC')
                csc_hist[o] = csc
                if stats:
                    pend.append((o, dst, csc))
            if stats:
                po, pdst, pcsc = pend.pop(0)
                b.wait('tensor', 'sSC', pcsc)
                emit_stat(po, pdst)

        cv_rope_prev = 0
        cwk0 = b.dma('sync', wba[:, :], wk0, 'sW0', guard='wb0')
        cwv0 = b.dma('sync', wbb[:, :], wv0, 'sW1', guard='wb1')
        b.wait('tensor', 'sW0', cwk0)
        b.wait('tensor', 'sW1', cwv0)
        for g in range(4):
            g0 = g * 512
            for c in range(NCH):
                slot = c % 2
                xc = a1[:, slot * 512: slot * 512 + 512]
                sq = a1[:, 1024 + slot * 512: 1024 + slot * 512 + 512]
                cx = b.dma('sync', xc, x0p[c, :, g0:g0 + 512], 'sP%d' % slot, guard='p0x%d' % slot)
                b.wait('scalar', 'sP%d' % slot, cx)
                b.emit('scalar', lambda e, xc=xc, sq=sq: e.activation(sq, xc, AF.Square,
                                                                     bias=0.0, scale=1.0))
                csq = b.mark('scalar', 'sSC')
                b.wait('tensor', 'sSC', csq)

                def _st(e, xc=xc, sq=sq, c=c):
                    e.matmul(pb[6][0:1, 0:512], onec[:, :], xc, start=(c == 0), stop=(c == NCH - 1))
                    return e.matmul(pb[7][0:1, 0:512], onec[:, :], sq,
                                    start=(c == 0), stop=(c == NCH - 1))
                b.slot_mm['p0x%d' % slot] = ('sMM', b.emit_inc('tensor', _st, 'sMM'))
            cvr = rows_from_stats(512)
            cbc = bcast_rows(512, cvr, banks=(0, 1))
            for c in range(NCH):
                slot = c % 2
                xc = a1[:, slot * 512: slot * 512 + 512]
                lnc = a1[:, 1024 + slot * 512: 1024 + slot * 512 + 512]
                cx = b.dma('sync', xc, x0p[c, :, g0:g0 + 512], 'sP%d' % slot, guard='p0x%d' % slot)
                b.wait('vector', 'sP%d' % slot, cx)
                if c == 0:
                    b.wait('vector', 'sMM', cbc)

                def _ln(e, xc=xc, lnc=lnc):
                    e.tensor_tensor(lnc, xc, pb[0][0:128, 0:512], ALU.mult)
                    e.drain()
                    e.tensor_tensor(lnc, lnc, pb[1][0:128, 0:512], ALU.subtract)
                    e.drain()
                b.emit('vector', _ln)
                cvl = b.mark('vector', 'sV')
                b.wait('tensor', 'sV', cvl)

                def _kv(e, lnc=lnc, c=c):
                    e.matmul(pb[2][0:128, 0:512], wba[:, c * 128:(c + 1) * 128], lnc,
                             start=(c == 0), stop=(c == NCH - 1))
                    return e.matmul(pb[3][0:128, 0:512], wbb[:, c * 128:(c + 1) * 128], lnc,
                                    start=(c == 0), stop=(c == NCH - 1))
                b.slot_mm['p0x%d' % slot] = ('sMM', b.emit_inc('tensor', _kv, 'sMM'))
            cm_kv = b.cnt['sMM']
            b.wait('gpsimd', 'sV', cv_rope_prev)
            ccos = b.dma('gpsimd', a1[:, 2048:2560], cosp[:, g0:g0 + 512], 'sCS')
            csin = b.dma('gpsimd', a1[:, 2560:3072], sinp[:, g0:g0 + 512], 'sCS')
            b.wait('scalar', 'sMM', cm_kv)
            b.emit('scalar', lambda e: e.copy(a2[:, 0:512], pb[2][0:128, 0:512]))
            b.emit('scalar', lambda e: e.copy(a2[:, 512:1024], pb[3][0:128, 0:512]))
            cs_kv = b.mark('scalar', 'sSC')
            b.wait('tensor', 'sSC', cs_kv)
            cm_rot = b.emit_inc('tensor', lambda e: e.matmul(pb[2][0:128, 0:512], rotm[:, :],
                                                             a2[:, 0:512], start=True, stop=True), 'sMM')
            b.wait('vector', 'sMM', cm_rot)
            b.wait('vector', 'sCS', max(ccos, csin))

            def _rope(e, g0=g0):
                e.tensor_tensor(a1[:, 3072:3584], a2[:, 0:512], a1[:, 2048:2560], ALU.mult)
                e.drain()
                e.tensor_tensor(a1[:, 3584:4096], pb[2][0:128, 0:512], a1[:, 2560:3072], ALU.mult)
                e.drain()
                e.tensor_tensor(k0r[:, g0:g0 + 512], a1[:, 3072:3584], a1[:, 3584:4096], ALU.add)
                e.drain()
            b.emit('vector', _rope)
            cv_rope = b.mark('vector', 'sV')
            cv_rope_prev = cv_rope
            b.wait('tensor', 'sV', cv_rope)
            for bb_ in range(4):
                cmt = b.emit_inc('tensor', lambda e, bb_=bb_: e.transpose(
                    pb[4][0:128, 0:128], a2[:, 512 + bb_ * 128:512 + bb_ * 128 + 128],
                    ident[:, :]), 'sMM')
                b.wait('scalar', 'sMM', cmt)
                dst = v0t[:, (g * 4 + bb_) * 128:(g * 4 + bb_) * 128 + 128]
                b.emit('scalar', lambda e, dst=dst: e.copy(dst, pb[4][0:128, 0:128]))
                cst = b.mark('scalar', 'sSC')
                b.wait('tensor', 'sSC', cst)
        b.slot_mm['wb0'] = ('sMM', b.cnt['sMM'])
        b.slot_mm['wb1'] = ('sMM', b.cnt['sMM'])

        cv_stage_prev = cv_rope_prev
        cx_out_prev = 0
        for t in range(n_t):
            t0 = t * TT

            b.wait('gpsimd', 'sV', cv_stage_prev)
            cxt = b.dma('gpsimd', xt[:].rearrange('p (c w) -> p c w', c=NCH),
                        x0p[:, :, t0:t0 + TT].rearrange('c p w -> p c w'), 'sXT')
            ccos = b.dma('gpsimd', costl[:, :], cosp[:, t0:t0 + TT], 'sCS')
            csin = b.dma('gpsimd', sintl[:, :], sinp[:, t0:t0 + TT], 'sCS')

            b.wait('scalar', 'sXT', cxt)
            fold = in_stats(xt, TT, TT, a1)
            cvr = rows_from_stats(TT, fold)
            cbc = bcast_rows(TT, cvr)
            cln, cln_marks = ln_into(a1, xt, TT, TT, cbc)

            for o in range(H):
                par = o % 2
                buf = wba if par == 0 else wbb
                bank = pb[par]
                cw = b.dma('sync', buf[:, :], wq0[o], 'sW%d' % par, guard='wb%d' % par)
                b.wait('tensor', 'sW%d' % par, cw)

                def _grp(e, buf=buf, bank=bank, first=(o == 0), marks=cln_marks):
                    for c in range(NCH):
                        if first:
                            e.wait_ge(b.h['sV'], marks[c])
                        last = e.matmul(bank[0:128, 0:TT], buf[:, c * 128:(c + 1) * 128],
                                        a1[:, c * TT:(c + 1) * TT],
                                        start=(c == 0), stop=(c == NCH - 1))
                    return last
                cmm = b.emit_inc('tensor', _grp, 'sMM')
                b.slot_mm['wb%d' % par] = ('sMM', cmm)
                b.wait('scalar', 'sMM', cmm)
                b.emit('scalar', lambda e, bank=bank: e.copy(tmpb[:, 0:TT], bank[0:128, 0:TT]))
                cs = b.mark('scalar', 'sSC')
                b.wait('tensor', 'sSC', cs)
                cmr = b.emit_inc('tensor', lambda e: e.matmul(pb[4][0:128, 0:TT], rotm[:, :],
                                                              tmpb[:, 0:TT], start=True, stop=True), 'sMM')
                b.wait('vector', 'sMM', cmr)
                if o == 0:
                    b.wait('vector', 'sCS', max(ccos, csin))

                def _qrope(e, o=o):
                    d = a2[:, o * TT:(o + 1) * TT]
                    e.tensor_tensor(d, tmpb[:, 0:TT], costl[:, :], ALU.mult)
                    e.drain()
                    e.tensor_tensor(tmpb[:, 256:512], pb[4][0:128, 0:TT], sintl[:, :], ALU.mult)
                    e.drain()
                    e.tensor_tensor(d, d, tmpb[:, 256:512], ALU.add)
                    e.drain()
                b.emit('vector', _qrope)
                cvq = b.mark('vector', 'sV')
                b.wait('tensor', 'sV', cvq)
            cvq_last = cvq
            if True:
                pass

            n_w = 1 if t == 0 else 2
            av_free = cvq_last
            den_free = 0
            sig_last = {0: 0, 1: 0}
            combine_q = []

            def flush_combine():
                while combine_q:
                    combine_q.pop(0)()
            for h in range(H):
                for wi in range(n_w):
                    wbase = (t - (n_w - 1 - wi)) * 256
                    nk = min(512, S - wbase) // 128
                    npair = nk // 2
                    avb = pb[4] if (n_w == 2 and wi == 0) else pb[5]
                    dnb = pb[6] if (n_w == 2 and wi == 0) else pb[7]
                    css = {}
                    pend_av = []
                    pend_den = []

                    def emit_av(kc, avb=avb, nk=nk, wbase=wbase):
                        if kc == 0:
                            b.wait('tensor', 'sV', av_free)
                        b.wait('tensor', 'sSC', css[kc])
                        asl = attns[:, (kc % 4) * TT:(kc % 4) * TT + TT]
                        kb = wbase + kc * 128

                        def _av(e, asl=asl, kb=kb, kc=kc, nk=nk, avb=avb):
                            return e.matmul(avb[0:128, 0:TT], v0t[:, kb:kb + 128], asl,
                                            start=(kc == 0), stop=(kc == nk - 1))
                        b.emit_inc('tensor', _av, 'sMM')

                    def emit_den(p, dnb=dnb, npair=npair):
                        if p == 0:
                            b.wait('tensor', 'sSC', den_free)
                        b.wait('tensor', 'sSC', css[2 * p + 1])
                        sl = (2 * p) % 4

                        def _den(e, p=p, sl=sl, npair=npair, dnb=dnb):
                            return e.matmul(dnb[0:1, 0:512], onec[:, :],
                                            attns[:, sl * TT:sl * TT + 2 * TT],
                                            start=(p == 0), stop=(p == npair - 1))
                        b.emit_inc('tensor', _den, 'sMM')

                    for kc in range(nk):
                        kb = wbase + kc * 128
                        sb = pb[2 + (kc % 2)]
                        b.wait('tensor', 'sSC', sig_last[kc % 2])
                        cms = b.emit_inc('tensor', lambda e, sb=sb, kb=kb, h=h:
                                         e.matmul(sb[0:128, 0:TT], k0r[:, kb:kb + 128],
                                                  a2[:, h * TT:(h + 1) * TT],
                                                  start=True, stop=True), 'sMM')
                        if wi == 0 and kc == min(2, nk - 1):
                            flush_combine()
                        b.wait('scalar', 'sMM', cms)
                        asl = attns[:, (kc % 4) * TT:(kc % 4) * TT + TT]
                        b.emit('scalar', lambda e, sb=sb, asl=asl:
                               e.activation(asl, sb[0:128, 0:TT], AF.Sigmoid, bias=0.0, scale=SCALE))
                        css[kc] = b.mark('scalar', 'sSC')
                        sig_last[kc % 2] = css[kc]
                        pend_av.append(kc)
                        if len(pend_av) > 2:
                            emit_av(pend_av.pop(0))
                        if kc % 2 == 1:
                            pend_den.append(kc // 2)
                            if len(pend_den) > 1:
                                emit_den(pend_den.pop(0))
                    while pend_av:
                        emit_av(pend_av.pop(0))
                    while pend_den:
                        emit_den(pend_den.pop(0))
                cm_att = b.cnt['sMM']

                def _combine(h=h, cm_att=cm_att):
                    nonlocal av_free, den_free
                    b.wait('vector', 'sMM', cm_att)

                    def _cc(e, n_w=n_w):
                        if n_w == 2:
                            e.tensor_copy(row(R_R1, TT), pb[6][0:1, 0:TT])
                            e.tensor_copy(row(R_R2, TT), pb[7][0:1, 0:TT])
                            e.drain()
                            e.tensor_tensor(row(R_R1, TT), row(R_R1, TT), pb[6][0:1, TT:2 * TT], ALU.add)
                            e.tensor_tensor(row(R_R2, TT), row(R_R2, TT), pb[7][0:1, TT:2 * TT], ALU.add)
                            e.drain()
                            e.reciprocal(row(R_R1, TT), row(R_R1, TT))
                            e.reciprocal(row(R_R2, TT), row(R_R2, TT))
                            e.drain()
                            e.tensor_tensor(row(R_C1, TT), row(R_R1, TT), row(R_AL1, TT), ALU.mult)
                            e.tensor_tensor(row(R_C2, TT), row(R_R2, TT), row(R_AL, TT), ALU.mult)
                            e.drain()
                        else:
                            e.tensor_copy(row(R_R2, TT), pb[7][0:1, 0:TT])
                            e.drain()
                            e.tensor_tensor(row(R_R2, TT), row(R_R2, TT), pb[7][0:1, TT:2 * TT], ALU.add)
                            e.drain()
                            e.reciprocal(row(R_C2, TT), row(R_R2, TT))
                            e.drain()
                    b.emit('vector', _cc)
                    cv_cc = b.mark('vector', 'sV')
                    b.wait('tensor', 'sV', cv_cc)

                    def _bc(e, n_w=n_w):
                        if n_w == 2:
                            e.matmul(pb[6][0:128, 0:TT], oner[:, :], row(R_C1, TT),
                                     start=True, stop=True)
                        return e.matmul(pb[7][0:128, 0:TT], oner[:, :], row(R_C2, TT),
                                        start=True, stop=True)
                    cm_bc = b.emit_inc('tensor', _bc, 'sMM')
                    b.wait('scalar', 'sMM', cm_bc)

                    def _crep(e, n_w=n_w):
                        if n_w == 2:
                            e.copy(crep[:, 0:TT], pb[6][0:128, 0:TT])
                        e.copy(crep[:, TT:2 * TT], pb[7][0:128, 0:TT])
                    b.emit('scalar', _crep)
                    cs_cr = b.mark('scalar', 'sSC')
                    den_free = cs_cr
                    b.wait('vector', 'sSC', cs_cr)

                    def _blend(e, h=h, n_w=n_w):
                        d = a1[:, h * TT:(h + 1) * TT]
                        if n_w == 2:
                            e.tensor_tensor(tmpb[:, 0:TT], pb[4][0:128, 0:TT], crep[:, 0:TT],
                                            ALU.mult)
                            e.drain()
                            e.tensor_tensor(d, pb[5][0:128, 0:TT], crep[:, TT:2 * TT], ALU.mult)
                            e.drain()
                            e.tensor_tensor(d, d, tmpb[:, 0:TT], ALU.add)
                            e.drain()
                        else:
                            e.tensor_tensor(d, pb[5][0:128, 0:TT], crep[:, TT:2 * TT], ALU.mult)
                            e.drain()
                    b.emit('vector', _blend)
                    cv_bl = b.mark('vector', 'sV')
                    av_free = cv_bl
                combine_q.append(_combine)
            flush_combine()

            proj(wo0, a1, TT, TT, lambda o: a2[:, o * TT:(o + 1) * TT],
                 dep_v=b.cnt['sV'], stats=True)
            cvr = rows_from_stats(TT)
            cbc2 = bcast_rows(TT, cvr)
            res_marks = residual_update(a2, TT, TT, cbc2)[1]

            fold = in_stats(xt, TT, TT, a1, dep_chunks=res_marks)
            cvr = rows_from_stats(TT, fold)
            cbc = bcast_rows(TT, cvr)
            cln, cln_marks = ln_into(a1, xt, TT, TT, cbc)

            cat = b.dma('sync', wba[:, :], atw, 'sW0', guard='wb0')
            crt = b.dma('sync', wrtb[:, :], wrt, 'sRT', guard=['bt0', 'bt1'])
            b.wait('tensor', 'sW0', cat)
            b.wait('tensor', 'sRT', crt)

            def _hrt(e, marks=cln_marks):
                for c in range(NCH):
                    e.wait_ge(b.h['sV'], marks[c])
                    e.matmul(pb[4][0:128, 0:TT], wba[:, c * 128:(c + 1) * 128],
                             a1[:, c * TT:(c + 1) * TT], start=(c == 0), stop=(c == NCH - 1))
                    last = e.matmul(pb[5][0:8, 0:TT], wrtb[:, c * 8:(c + 1) * 8],
                                    a1[:, c * TT:(c + 1) * TT], start=(c == 0), stop=(c == NCH - 1))
                return last
            cm_h = b.emit_inc('tensor', _hrt, 'sMM')
            b.slot_mm['wb0'] = ('sMM', cm_h)
            b.slot_mm['bt0'] = ('sMM', cm_h)
            b.slot_mm['bt1'] = ('sMM', cm_h)
            b.wait('scalar', 'sMM', cm_h)
            b.emit('scalar', lambda e: e.copy(tmpb[0:8, 0:TT], pb[5][0:8, 0:TT]))
            cs_l = b.mark('scalar', 'sSC')
            b.wait('tensor', 'sSC', cs_l)
            for half in range(2):
                cm_t = b.emit_inc('tensor', lambda e, half=half: e.transpose(
                    pb[3][0:128, 0:8], tmpb[0:8, half * 128:half * 128 + 128],
                    ident[0:8, 0:8]), 'sMM')
                b.wait('scalar', 'sMM', cm_t)
                b.emit('scalar', lambda e, half=half: e.copy(ltt[:, half * 8:half * 8 + 8],
                                                             pb[3][0:128, 0:8]))
                cs_t = b.mark('scalar', 'sSC')
                b.wait('tensor', 'sSC', cs_t)
            b.wait('vector', 'sSC', b.cnt['sSC'])

            def _gate_v1(e):
                for half in range(2):
                    lt = ltt[:, half * 8:half * 8 + 8]
                    m1 = rsc[:, half * 2 + 0:half * 2 + 1]
                    e.reduce_max(m1, lt, AX.X)
                    e.drain()
                    msk1 = rsc[:, 16 + half * 8:16 + half * 8 + 8]
                    e.tensor_scalar(msk1, lt, m1, None, ALU.is_ge)
                    e.drain()
                    l2 = rsc[:, 32 + half * 8:32 + half * 8 + 8]
                    e.scalar_tensor_tensor(l2, msk1, NEG_BIG, lt, ALU.mult, ALU.add)
                    e.drain()
                    m2 = rsc[:, half * 2 + 1:half * 2 + 2]
                    e.reduce_max(m2, l2, AX.X)
                    e.drain()
                    msk2 = rsc[:, 48 + half * 8:48 + half * 8 + 8]
                    e.tensor_scalar(msk2, l2, m2, None, ALU.is_ge)
                    e.drain()
                    dcol = rsc[:, 4 + half * 2:4 + half * 2 + 1]
                    e.tensor_tensor(dcol, m1, m2, ALU.subtract)
                    e.drain()
            b.emit('vector', _gate_v1)
            cv_g1 = b.mark('vector', 'sV')
            b.wait('scalar', 'sV', cv_g1)

            def _gate_s(e):
                for half in range(2):
                    dcol = rsc[:, 4 + half * 2:4 + half * 2 + 1]
                    e.activation(rsc[:, 8 + half * 2:8 + half * 2 + 1], dcol, AF.Sigmoid,
                                 bias=0.0, scale=1.0)
                    e.activation(rsc[:, 9 + half * 2:9 + half * 2 + 1], dcol, AF.Sigmoid,
                                 bias=0.0, scale=-1.0)
            b.emit('scalar', _gate_s)
            cs_g = b.mark('scalar', 'sSC')
            b.wait('vector', 'sSC', cs_g)

            def _gate_v2(e):
                for half in range(2):
                    w1 = rsc[:, 8 + half * 2:8 + half * 2 + 1]
                    w2 = rsc[:, 9 + half * 2:9 + half * 2 + 1]
                    msk1 = rsc[:, 16 + half * 8:16 + half * 8 + 8]
                    msk2 = rsc[:, 48 + half * 8:48 + half * 8 + 8]
                    t1 = rsc[:, 32 + half * 8:32 + half * 8 + 8]
                    e.tensor_scalar(t1, msk1, w1, None, ALU.mult)
                    e.tensor_scalar(msk2, msk2, w2, None, ALU.mult)
                    e.drain()
                    e.tensor_tensor(msk1, t1, msk2, ALU.add)
                    e.drain()
            b.emit('vector', _gate_v2)
            cv_g2 = b.mark('vector', 'sV')
            b.wait('tensor', 'sV', cv_g2)
            for half in range(2):
                cm_t = b.emit_inc('tensor', lambda e, half=half: e.transpose(
                    pb[3][0:8, 0:128], rsc[:, 16 + half * 8:16 + half * 8 + 8],
                    ident[:, :]), 'sMM')
                b.wait('scalar', 'sMM', cm_t)
                b.emit('scalar', lambda e, half=half: e.copy(wts[:, half * 128:half * 128 + 128],
                                                             pb[3][0:8, 0:128]))
                cs_t = b.mark('scalar', 'sSC')
                b.wait('tensor', 'sSC', cs_t)
            b.wait('tensor', 'sSC', b.cnt['sSC'])
            cm_rep = b.emit_inc('tensor', lambda e: e.matmul(pb[3][0:128, 0:TT], rep16[:, :],
                                                             wts[:, :], start=True, stop=True), 'sMM')
            b.wait('scalar', 'sMM', cm_rep)
            b.emit('scalar', lambda e: e.copy(wrep[:, :], pb[3][0:128, 0:TT]))
            cs_rep = b.mark('scalar', 'sSC')
            b.wait('vector', 'sSC', cs_rep)

            def _g(e):
                e.tensor_tensor(gbuf[:, :], pb[4][0:128, 0:TT], wrep[:, :], ALU.mult)
                e.drain()
            b.emit('vector', _g)
            cv_gg = b.mark('vector', 'sV')

            proj(wb0, a1, TT, TT, lambda o: a2[:, o * TT:(o + 1) * TT],
                 dep_v=cv_gg, bias_name='bb0', lora_dep=cv_gg, stats=True,
                 dep_chunks=cln_marks)
            cvr = rows_from_stats(TT)
            cbc2 = bcast_rows(TT, cvr)
            res_marks = residual_update(a2, TT, TT, cbc2)[1]

            fold = in_stats(xt, TT, TT, a1, dep_chunks=res_marks)
            cvr = rows_from_stats(TT, fold)
            cbc = bcast_rows(TT, cvr)
            cln, cln_marks = ln_into(a1, xt, TT, TT, cbc)

            cwk1 = b.dma('sync', wba[:, :], wk1, 'sW0', guard='wb0')
            cwv1 = b.dma('sync', wbb[:, :], wv1, 'sW1', guard='wb1')
            b.wait('tensor', 'sW0', cwk1)
            b.wait('tensor', 'sW1', cwv1)

            def _kv1(e, marks=cln_marks):
                for c in range(NCH):
                    e.wait_ge(b.h['sV'], marks[c])
                    e.matmul(pb[4][0:128, 0:TT], wba[:, c * 128:(c + 1) * 128],
                             a1[:, c * TT:(c + 1) * TT], start=(c == 0), stop=(c == NCH - 1))
                    last = e.matmul(pb[5][0:128, 0:TT], wbb[:, c * 128:(c + 1) * 128],
                                    a1[:, c * TT:(c + 1) * TT], start=(c == 0), stop=(c == NCH - 1))
                return last
            cm_kv1 = b.emit_inc('tensor', _kv1, 'sMM')
            b.slot_mm['wb0'] = ('sMM', cm_kv1)
            b.slot_mm['wb1'] = ('sMM', cm_kv1)
            b.wait('scalar', 'sMM', cm_kv1)
            b.emit('scalar', lambda e, t0=t0: e.activation(k1b[:, t0:t0 + TT], pb[4][0:128, 0:TT],
                                                           AF.Identity, bias=KCOL, scale=1.0))
            b.emit('scalar', lambda e: e.activation(tmpb[:, 0:TT], pb[5][0:128, 0:TT],
                                                    AF.Identity, bias=VCOL, scale=1.0))
            cs_kv1 = b.mark('scalar', 'sSC')
            b.wait('tensor', 'sSC', cs_kv1)
            for bb_ in range(2):
                cm_tr = b.emit_inc('tensor', lambda e, bb_=bb_: e.transpose(
                    pb[2][0:128, 0:128], tmpb[:, bb_ * 128:bb_ * 128 + 128],
                    ident[:, :]), 'sMM')
                b.wait('scalar', 'sMM', cm_tr)
                dst = v1t[:, (2 * t + bb_) * 128:(2 * t + bb_) * 128 + 128]
                b.emit('scalar', lambda e, dst=dst: e.copy(dst, pb[2][0:128, 0:128]))
                cs_tr = b.mark('scalar', 'sSC')
                b.wait('tensor', 'sSC', cs_tr)

            for o in range(H):
                par = o % 2
                buf = wba if par == 0 else wbb
                bank = pb[par]
                cw = b.dma('sync', buf[:, :], wq1[o], 'sW%d' % par, guard='wb%d' % par)
                b.wait('tensor', 'sW%d' % par, cw)

                def _grp(e, buf=buf, bank=bank):
                    for c in range(NCH):
                        last = e.matmul(bank[0:128, 0:128], buf[:, c * 128:(c + 1) * 128],
                                        a1[:, c * TT:c * TT + 128],
                                        start=(c == 0), stop=(c == NCH - 1))
                    return last
                cmm = b.emit_inc('tensor', _grp, 'sMM')
                b.slot_mm['wb%d' % par] = ('sMM', cmm)
                b.wait('scalar', 'sMM', cmm)
                dst = a2[:, o * 128:(o + 1) * 128]
                bc = bcol('qb1', o)
                b.emit('scalar', lambda e, dst=dst, bank=bank, bc=bc:
                       e.activation(dst, bank[0:128, 0:128], AF.Identity, bias=bc, scale=1.0))
                cs = b.mark('scalar', 'sSC')
                b.wait('tensor', 'sSC', cs)

            av_free = 0
            den_free = 0
            rep_free = 0
            sig_last = {0: 0, 1: 0}
            combine_q = []

            def flush_combine():
                while combine_q:
                    combine_q.pop(0)()
            for h in range(H):
                nkc = 2 * t + 2
                order = [nkc - 2, nkc - 1] + list(range(nkc - 2))
                gates = {}
                slot_of = {}
                pend_av = []
                pend_den = []

                def emit_av(pi, order=order, nkc=nkc):
                    kc = order[pi]
                    if pi == 0:
                        b.wait('tensor', 'sSC', av_free)
                    b.wait('tensor', gates[kc][0], gates[kc][1])
                    sl = slot_of[kc]
                    esl = attns[:, sl * 128:sl * 128 + 128]

                    def _av(e, esl=esl, kc=kc, pi=pi, nkc=nkc):
                        return e.matmul(pb[4][0:128, 0:128], v1t[:, kc * 128:kc * 128 + 128], esl,
                                        start=(pi == 0), stop=(pi == nkc - 1))
                    b.emit_inc('tensor', _av, 'sMM')

                def emit_den(p, order=order, nkc=nkc):
                    if p == 0:
                        b.wait('tensor', 'sV', den_free)
                    for pp in (2 * p, 2 * p + 1):
                        kk = order[pp]
                        b.wait('tensor', gates[kk][0], gates[kk][1])
                    sl = (2 * p) % 4

                    def _den(e, p=p, sl=sl, nkc=nkc):
                        return e.matmul(pb[6][0:1, 0:256], onec[:, :],
                                        attns[:, sl * 128:sl * 128 + 256],
                                        start=(p == 0), stop=(p == nkc // 2 - 1))
                    b.emit_inc('tensor', _den, 'sMM')

                for pi, kc in enumerate(order):
                    sb = pb[2 + (pi % 2)]
                    b.wait('tensor', 'sSC', sig_last[pi % 2])
                    cms = b.emit_inc('tensor', lambda e, sb=sb, kc=kc, h=h:
                                     e.matmul(sb[0:128, 0:128], k1b[:, kc * 128:kc * 128 + 128],
                                              a2[:, h * 128:(h + 1) * 128],
                                              start=True, stop=True), 'sMM')
                    if pi == min(2, nkc - 1):
                        flush_combine()
                    b.wait('scalar', 'sMM', cms)
                    sl = pi % 4
                    slot_of[kc] = sl
                    esl = attns[:, sl * 128:sl * 128 + 128]
                    b.emit('scalar', lambda e, sb=sb, esl=esl:
                           e.activation(esl, sb[0:128, 0:128], AF.Exp, bias=0.0, scale=SCALE))
                    css = b.mark('scalar', 'sSC')
                    sig_last[pi % 2] = css
                    if kc >= nkc - 2:
                        b.wait('vector', 'sSC', css)
                        if kc == nkc - 2:
                            def _msk(e, esl=esl):
                                e.tensor_tensor(esl, esl, diag[:, :], ALU.mult)
                                e.drain()
                        else:
                            def _msk(e, esl=esl):
                                e.tensor_scalar(esl, esl, MLCOL, None, ALU.mult)
                                e.drain()
                        b.emit('vector', _msk)
                        cvm = b.mark('vector', 'sV')
                        gates[kc] = ('sV', cvm)
                    else:
                        gates[kc] = ('sSC', css)
                    pend_av.append(pi)
                    if len(pend_av) > 2:
                        emit_av(pend_av.pop(0))
                    if pi % 2 == 1:
                        pend_den.append(pi // 2)
                        if len(pend_den) > 1:
                            emit_den(pend_den.pop(0))
                while pend_av:
                    emit_av(pend_av.pop(0))
                while pend_den:
                    emit_den(pend_den.pop(0))
                cm_att = b.cnt['sMM']

                def _combine(h=h, cm_att=cm_att):
                    nonlocal av_free, den_free, rep_free
                    b.wait('vector', 'sMM', cm_att)

                    def _r(e):
                        e.tensor_copy(row(R_R1, 128), pb[6][0:1, 0:128])
                        e.drain()
                        e.tensor_tensor(row(R_R1, 128), row(R_R1, 128), pb[6][0:1, 128:256], ALU.add)
                        e.drain()
                        e.reciprocal(row(R_R1, 128), row(R_R1, 128))
                        e.drain()
                    b.emit('vector', _r)
                    cv_r = b.mark('vector', 'sV')
                    den_free = cv_r
                    b.wait('tensor', 'sV', cv_r)
                    b.wait('tensor', 'sV', rep_free)
                    cm_rb = b.emit_inc('tensor', lambda e: e.matmul(pb[7][0:128, 0:128],
                                                                    oner[:, :], row(R_R1, 128),
                                                                    start=True, stop=True), 'sMM')
                    b.wait('scalar', 'sMM', cm_rb)
                    b.emit('scalar', lambda e: e.copy(tmpb[:, 0:128], pb[4][0:128, 0:128]))
                    cs_av = b.mark('scalar', 'sSC')
                    av_free = cs_av
                    b.wait('vector', 'sSC', cs_av)

                    def _nrm(e, h=h):
                        e.tensor_tensor(a1[:, h * 128:(h + 1) * 128], tmpb[:, 0:128],
                                        pb[7][0:128, 0:128], ALU.mult)
                        e.drain()
                    b.emit('vector', _nrm)
                    cv_n = b.mark('vector', 'sV')
                    rep_free = cv_n
                combine_q.append(_combine)
            flush_combine()

            proj(wo1, a1, 128, 128, lambda o: a2[:, o * 128:(o + 1) * 128],
                 dep_v=b.cnt['sV'], bias_name='ob1', stats=True)
            cvr = rows_from_stats(128)
            cbc2 = bcast_rows(128, cvr)
            res_marks = residual_update(a2, 128, TT, cbc2)[1]

            in_stats(xt, 128, TT, a1, dep_chunks=res_marks)
            cvr = rows_from_stats(128)
            cbc = bcast_rows(128, cvr)
            cln, _ = ln_into(a1, xt, 128, TT, cbc)
            b.wait('vector', 'sV', cln)

            def _split(e):
                e.tensor_copy(lnpair[:, 0:W_], a1[:, 0:W_])
                e.drain()
                e.tensor_tensor(lnpair[:, W_:2 * W_], a1[:, 0:W_], lnpair[:, 0:W_], ALU.subtract)
                e.drain()
            b.emit('vector', _split)
            cv_sp = b.mark('vector', 'sV')
            b.wait('tensor', 'sV', cv_sp)

            tern_sc_done = {0: 0, 1: 0}
            tern_cvv = {}
            tern_pend = []
            b.wait('scalar', 'sV', cv_sp)
            for o in range(NCH):
                bank = pb[o % 2]
                if t == 0:
                    par = o % 2
                    buf = wba if par == 0 else wbb
                    b.wait('sync', 'sSC', tern_sc_done[par])
                    cw = b.dma('sync', buf[:, :], wf1[o], 'sW%d' % par, guard='wb%d' % par)
                    b.wait('scalar', 'sW%d' % par, cw)

                    def _tern_s(e, buf=buf):
                        e.activation(a1[:, 0:W_], buf[:, :], AF.Abs, bias=0.0, scale=1.0)
                        e.activation(a1[:, W_:2 * W_], buf[:, :], AF.Sign, bias=0.0, scale=1.0)
                        e.drain()
                    b.emit('scalar', _tern_s)
                    cs_t = b.mark('scalar', 'sSC')
                    tern_sc_done[par] = cs_t
                    b.slot_mm['wb%d' % par] = ('sSC', cs_t)
                    b.wait('vector', 'sSC', cs_t)
                    b.wait('vector', 'sTS', b.cnt['sTS'])

                    def _tern_v(e):
                        e.tensor_single_scalar(a1[:, 0:W_], a1[:, 0:W_], 0.1, ALU.is_ge)
                        e.drain()
                        e.tensor_tensor(ternb[:, :], a1[:, W_:2 * W_], a1[:, 0:W_], ALU.mult)
                        e.drain()
                    b.emit('vector', _tern_v)
                    cv_t = b.mark('vector', 'sV')
                    b.wait('gpsimd', 'sV', cv_t)
                    b.dma('gpsimd', tern_sc[o], ternb[:, :], 'sTS')
                    b.wait('tensor', 'sV', cv_t)
                else:
                    if o == 0:
                        b.wait('sync', 'sTS', 32 * 16)
                    cw = b.dma('sync', ternb[:, :], tern_sc[o], 'sW0',
                               guard=(['tern', 'wb0'] if o == 0 else 'tern'))
                    b.wait('tensor', 'sW0', cw)

                if o >= 2:
                    b.wait('tensor', 'sV', tern_cvv[o - 2])

                def _grp(e, bank=bank):
                    for c in range(NCH):
                        e.matmul(bank[0:128, 0:128], ternb[:, c * 128:(c + 1) * 128],
                                 lnpair[:, c * 128:c * 128 + 128], start=(c == 0), stop=False)
                    for c in range(NCH):
                        last = e.matmul(bank[0:128, 0:128], ternb[:, c * 128:(c + 1) * 128],
                                        lnpair[:, W_ + c * 128:W_ + c * 128 + 128],
                                        start=False, stop=(c == NCH - 1))
                    return last
                cmm = b.emit_inc('tensor', _grp, 'sMM')
                b.slot_mm['tern'] = ('sMM', cmm)
                if tern_pend:
                    po, pdst, pcs2 = tern_pend.pop(0)
                    b.wait('tensor', 'sSC', pcs2)

                    def _stat(e, dst=pdst, o=po):
                        e.matmul(pb[6][0:1, 0:128], onec[:, :], dst,
                                 start=(o == 0), stop=(o == NCH - 1))
                        return e.matmul(pb[7][0:1, 0:128], onec[:, :],
                                        tmpb[:, 256 + (o % 2) * 128:384 + (o % 2) * 128],
                                        start=(o == 0), stop=(o == NCH - 1))
                    b.emit_inc('tensor', _stat, 'sMM')
                b.wait('vector', 'sMM', cmm)
                dst = a2[:, o * 128:(o + 1) * 128]

                def _ev(e, dst=dst, bank=bank, o=o):
                    e.tensor_scalar(dst, bank[0:128, 0:128], bcol('fb1', o), bcol('fs1', o),
                                    ALU.add, ALU.mult)
                    e.drain()
                b.emit('vector', _ev)
                cvv = b.mark('vector', 'sV')
                tern_cvv[o] = cvv
                b.wait('scalar', 'sV', cvv)
                b.emit('scalar', lambda e, dst=dst, o=o:
                       e.activation(tmpb[:, 256 + (o % 2) * 128:384 + (o % 2) * 128], dst,
                                    AF.Square, bias=0.0, scale=1.0))
                cs2 = b.mark('scalar', 'sSC')
                tern_pend.append((o, dst, cs2))

            if tern_pend:
                po, pdst, pcs2 = tern_pend.pop(0)
                b.wait('tensor', 'sSC', pcs2)

                def _stat(e, dst=pdst, o=po):
                    e.matmul(pb[6][0:1, 0:128], onec[:, :], dst,
                             start=(o == 0), stop=(o == NCH - 1))
                    return e.matmul(pb[7][0:1, 0:128], onec[:, :],
                                    tmpb[:, 256 + (o % 2) * 128:384 + (o % 2) * 128],
                                    start=(o == 0), stop=(o == NCH - 1))
                b.emit_inc('tensor', _stat, 'sMM')
            b.slot_mm['wb0'] = b.slot_mm['tern']
            cvr = rows_from_stats(128)
            cbc3 = bcast_rows(128, cvr)
            b.wait('vector', 'sMM', cbc3)

            def _lnaff(e):
                for c in range(NCH):
                    yc = a2[:, c * 128:(c + 1) * 128]
                    e.tensor_tensor(yc, yc, pb[2][0:128, 0:128], ALU.mult)
                    e.drain()
                    e.tensor_tensor(yc, yc, pb[3][0:128, 0:128], ALU.subtract)
                    e.drain()
                    e.tensor_scalar(yc, yc, bcol('fg1', c), bcol('fbt1', c), ALU.mult, ALU.add)
                    e.drain()
            b.emit('vector', _lnaff)
            cv_la = b.mark('vector', 'sV')
            b.wait('scalar', 'sV', cv_la)
            b.emit('scalar', lambda e: e.activation(a1[:, 0:W_], a2[:, 0:W_], AF.Tanh,
                                                    bias=0.0, scale=QSCOL))
            cs_th = b.mark('scalar', 'sSC')
            b.wait('vector', 'sSC', cs_th)

            def _round2(e):
                e.tensor_scalar(a2[:, 0:W_], a1[:, 0:W_], 127.0, 128.5, ALU.mult, ALU.add)
                e.drain()
                e.tensor_copy(a1[:, 0:W_], a2[:, 0:W_])
                e.drain()
                for k in (128.0, 64.0, 32.0, 16.0, 8.0, 4.0, 2.0, 1.0):
                    e.tensor_scalar(a1[:, W_:2 * W_], a1[:, 0:W_], k, k, ALU.is_ge, ALU.mult)
                    e.drain()
                    e.tensor_tensor(a1[:, 0:W_], a1[:, 0:W_], a1[:, W_:2 * W_], ALU.subtract)
                    e.drain()
                e.tensor_tensor(a2[:, 0:W_], a2[:, 0:W_], a1[:, 0:W_], ALU.subtract)
                e.drain()
                e.tensor_scalar(a2[:, 0:W_], a2[:, 0:W_], -128.0, 1.0 / 127.0, ALU.add, ALU.mult)
                e.drain()
            b.emit('vector', _round2)
            cv_rd = b.mark('vector', 'sV')

            b.wait('scalar', 'sV', cv_rd)
            b.emit('scalar', lambda e: e.activation(a1[:, 0:W_], a2[:, 0:W_], AF.Square,
                                                    bias=0.0, scale=1.0))
            cs3 = b.mark('scalar', 'sSC')
            b.wait('tensor', 'sSC', cs3)

            def _st3(e):
                for c4 in range(NCH // 4):
                    e.matmul(pb[6][0:1, 0:512], onec[:, :], a2[:, c4 * 512:(c4 + 1) * 512],
                             start=(c4 == 0), stop=(c4 == NCH // 4 - 1))
                    last = e.matmul(pb[7][0:1, 0:512], onec[:, :], a1[:, c4 * 512:(c4 + 1) * 512],
                                    start=(c4 == 0), stop=(c4 == NCH // 4 - 1))
                return last
            b.emit_inc('tensor', _st3, 'sMM')
            cvr = rows_from_stats(128, fold=4)
            cbc4 = bcast_rows(128, cvr)
            res_marks = residual_update(a2, 128, TT, cbc4)[1]

            b.wait('vector', 'sV', b.cnt['sV'])

            def _stage(e):
                e.tensor_copy(a2[:, 0:W_].rearrange('p (c w) -> p c w', c=NCH),
                              xt[:].rearrange('p (c w) -> p c w', c=NCH)[:, :, 0:128])
                e.drain()
            b.emit('vector', _stage)
            cv_stage_prev = b.mark('vector', 'sV')
            b.wait('gpsimd', 'sV', cv_stage_prev)
            cx_out_prev = b.dma('gpsimd', y_out[t],
                                a2[:, 0:W_].rearrange('p (c w) -> p c w', c=NCH), 'sOUT')
            b.wait('vector', 'sOUT', cx_out_prev)

        globals()['LAST_CNT'] = dict(b.cnt)
        b.wait('gpsimd', 'sOUT', b.cnt['sOUT'])
        for s in ('sTS', 'sCS', 'sXT', 'sREP'):
            if b.cnt[s]:
                b.wait('gpsimd', s, b.cnt[s])

        with nc.Block() as block:
            @block.sync
            def _(e):
                for f in b.q['sync']:
                    f(e)

            @block.tensor
            def _(e):
                for f in b.q['tensor']:
                    f(e)

            @block.vector
            def _(e):
                for f in b.q['vector']:
                    f(e)

            @block.scalar
            def _(e):
                for f in b.q['scalar']:
                    f(e)

            @block.gpsimd
            def _(e):
                for f in b.q['gpsimd']:
                    f(e)

    return nc



def _rope_cache():
    base = 10000.0
    inv_freq = 1.0 / base ** (np.arange(0, DH, 2, dtype=np.float32) / DH)
    t = np.arange(S, dtype=np.float32)
    f = np.outer(t, inv_freq)
    cos = np.zeros((S, DH), dtype=np.float32)
    sin = np.zeros((S, DH), dtype=np.float32)
    cos[:, 0::2] = np.cos(f)
    cos[:, 1::2] = np.cos(f)
    sin[:, 0::2] = np.sin(f)
    sin[:, 1::2] = np.sin(f)
    return cos, sin


def _pack_w(wt):
    return np.ascontiguousarray(
        wt.reshape(NCH, 128, NCH, 128).transpose(2, 1, 0, 3).reshape(NCH, 128, W_))


def _pack_w_small(wt, outw):
    return np.ascontiguousarray(
        wt.reshape(NCH, 128, outw).transpose(1, 0, 2).reshape(128, NCH * outw))


def kernel(**inputs):
    ins = {k: np.asarray(v) for k, v in inputs.items()}
    x0 = ins['L0_attn_x']

    cos, sin = _rope_cache()

    sh = {}
    sh['wq0'] = _pack_w(np.ascontiguousarray(ins['L0_attn_Wq'].T))
    sh['wo0'] = _pack_w(np.ascontiguousarray(ins['L0_attn_Wo'].T))
    sh['wb0'] = _pack_w(np.ascontiguousarray(ins['L0_ffn_W_base'].T))
    sh['wq1'] = _pack_w(np.ascontiguousarray(ins['L1_attn_wq_w'].T))
    sh['wo1'] = _pack_w(np.ascontiguousarray(ins['L1_attn_wo_w'].T))
    sh['wf1'] = _pack_w(np.ascontiguousarray(ins['L1_ffn_weight'].T))
    sh['wk0'] = _pack_w_small(np.ascontiguousarray(ins['L0_attn_Wk'].T), 128)
    sh['wv0'] = _pack_w_small(np.ascontiguousarray(ins['L0_attn_Wv'].T), 128)
    sh['wk1'] = _pack_w_small(np.ascontiguousarray(ins['L1_attn_wk_w'].T), 128)
    sh['wv1'] = _pack_w_small(np.ascontiguousarray(ins['L1_attn_wv_w'].T), 128)
    sh['wrt'] = _pack_w_small(np.ascontiguousarray(ins['L0_ffn_W_router'].T), 8)
    sh['atw'] = _pack_w_small(np.ascontiguousarray(ins['L0_ffn_A'].reshape(128, 4096).T), 128)
    bt = np.ascontiguousarray(ins['L0_ffn_B'].transpose(0, 2, 1).reshape(128, 4096))
    sh['btw'] = np.ascontiguousarray(bt.reshape(128, NCH, 128).transpose(1, 0, 2))
    for nm, key in (('bb0', 'L0_ffn_b_base'), ('qb1', 'L1_attn_wq_b'), ('ob1', 'L1_attn_wo_b'),
                    ('fb1', 'L1_ffn_bias'), ('fs1', 'L1_ffn_scale'),
                    ('fg1', 'L1_ffn_ln_gamma'), ('fbt1', 'L1_ffn_ln_beta')):
        sh[nm] = np.ascontiguousarray(ins[key].reshape(NCH, 128).T)
    sh['kb1'] = np.ascontiguousarray(ins['L1_attn_wk_b'].reshape(128, 1))
    sh['vb1'] = np.ascontiguousarray(ins['L1_attn_wv_b'].reshape(128, 1))
    sh['qs1'] = np.ascontiguousarray(ins['L1_ffn_quant_scale'].reshape(1, 1))
    kq = np.arange(128)
    sh['diagi'] = (kq[:, None] <= kq[None, :]).astype(np.float32)
    sh['identi'] = np.eye(128, dtype=np.float32)
    rotm = np.zeros((128, 128), np.float32)
    for d in range(64):
        rotm[d, 64 + d] = -1.0
        rotm[64 + d, d] = 1.0
    sh['rotmi'] = np.ascontiguousarray(rotm.T)
    rep16 = np.zeros((8, 128), np.float32)
    for e_ in range(8):
        rep16[e_, e_ * 16:(e_ + 1) * 16] = 2.0
    sh['rep16i'] = rep16

    in_maps = []
    for core in range(8):
        seq, role = core // 2, core % 2
        perm = np.arange(S)
        if role == 1:
            perm = np.ascontiguousarray(perm.reshape(-1, 2, 128)[:, ::-1, :]).reshape(-1)
        xl = x0[seq][perm]
        m = dict(sh)
        m['x0p'] = np.ascontiguousarray(xl.T.reshape(NCH, 128, S))
        m['cosp'] = np.ascontiguousarray(cos[perm].T)
        m['sinp'] = np.ascontiguousarray(sin[perm].T)
        if role == 0:
            m['rampi'] = np.arange(TT, dtype=np.float32).reshape(1, TT)
        else:
            m['rampi'] = np.concatenate([np.arange(128, 256), np.arange(0, 128)]
                                        ).astype(np.float32).reshape(1, TT)
        m['mlasti'] = np.full((128, 1), float(role), np.float32)
        in_maps.append(m)

    nc = build_nc()
    res = run_bass_kernel_spmd(nc, in_maps, core_ids=list(range(8)))

    out = np.empty((4, S, D), np.float32)
    for core in range(8):
        seq, role = core // 2, core % 2
        yo = res.results[core]['y_out']
        for t in range(NT):
            g = 2 * t + role
            out[seq, g * 128:(g + 1) * 128, :] = yo[t].transpose(2, 1, 0).reshape(128, D)
    return out
